# version 02
# baseline (speedup 1.0000x reference)
import math
import numpy as np
import ml_dtypes

import concourse.bass as bass
import concourse.mybir as mybir
from concourse.bass_utils import run_bass_kernel_spmd

F32 = mybir.dt.float32
BF16 = mybir.dt.bfloat16
I32 = mybir.dt.int32
AF = mybir.ActivationFunctionType
ALU = mybir.AluOpType
AX = mybir.AxisListType.X

S = 2048
D = 1024
NT = S // 128
ND = D // 128
F = 2816
NFC = F // 128
NE = 8
DQK = 128
NB = 32

MU = math.sqrt(0.5)
SIGMA = math.sqrt(0.25 * math.pi)
REL_SCALE = DQK ** 0.5
MASK_NEG = -100000.0
RND_C = float(3 * 2.0 ** 22)

STRIP_W = 2432
STRIP_PAD = 512
BVEC_N = 2560


def _rel_bucket_ext():
    out = np.zeros(BVEC_N, dtype=np.int64)
    for i in range(BVEC_N):
        m = i - STRIP_PAD
        if m < 0:
            out[i] = NB
        elif m < 16:
            out[i] = m
        else:
            t = np.float32(np.log(np.float32(m) / np.float32(16.0)))
            t = np.float32(t / np.float32(math.log(128 / 16)))
            t = np.float32(t * np.float32(16))
            out[i] = min(16 + int(np.int32(t)), NB - 1)
    return out


def build_program(debug_taps=False, moe_ne=NE, moe_silu=True, l0_tiles=NT, l0_stage=9, HPAR=2, p6_ibs=8):
    nc = bass.Bass("TRN2", target_bir_lowering=False, detect_race_conditions=False)

    def din(name, shape, dtype=F32):
        return nc.dram_tensor(name, shape, dtype, kind="ExternalInput").ap()

    x_in = din("L0_attn_x", [S, D])
    wqkv = din("L0_attn_Wqkv", [D, 3 * D])
    wout0 = din("L0_attn_Wout", [D, D])
    bout0 = din("L0_attn_bout", [D])
    router_w = din("L0_ffn_router_w", [D, NE])
    gate_p = din("L0_ffn_gate_proj", [NE, D, F])
    up_p = din("L0_ffn_up_proj", [NE, D, F])
    down_p = din("L0_ffn_down_proj", [NE, F, D])
    wqk1 = din("L1_attn_wqk", [D, DQK])
    bqk1 = din("L1_attn_bqk", [DQK])
    gamma1 = din("L1_attn_gamma", [2, DQK])
    beta1 = din("L1_attn_beta", [2, DQK])
    wv1 = din("L1_attn_wv", [D, D])
    bv1 = din("L1_attn_bv", [D])
    relb = din("L1_attn_rel_bias", [NB])
    w_int8 = din("L1_ffn_w_int8", [D, D], I32)
    wscale = din("L1_ffn_weight_scale", [1, 1])
    bias1 = din("L1_ffn_bias", [D])
    ident = din("aux_ident", [128, 128])
    oht = din("aux_oht", [NB + 1, BVEC_N])
    ones_row = din("aux_ones", [1, 128])
    sel8 = din("aux_sel8", [NE, NE * 128], BF16)

    out = nc.dram_tensor("out", [S, D], F32, kind="ExternalOutput").ap()

    x1_dram = nc.dram_tensor("x1_scratch", [S, D], F32, kind="Internal").ap()
    bvec_dram = nc.dram_tensor("bvec_scratch", [BVEC_N], F32, kind="Internal").ap()

    taps = {}
    if debug_taps:
        taps["xnT"] = nc.dram_tensor("tap_xnT", [ND, 128, S], F32, kind="ExternalOutput").ap()
        taps["ewT"] = nc.dram_tensor("tap_ewT", [NE, S], F32, kind="ExternalOutput").ap()
        taps["g"] = nc.dram_tensor("tap_g", [S, D], F32, kind="ExternalOutput").ap()
        taps["x1"] = nc.dram_tensor("tap_x1", [S, D], F32, kind="ExternalOutput").ap()
        taps["g1"] = nc.dram_tensor("tap_g1", [S, D], F32, kind="ExternalOutput").ap()
        taps["strip"] = nc.dram_tensor("tap_strip", [128, STRIP_W], F32, kind="ExternalOutput").ap()
        taps["wrep"] = nc.dram_tensor("tap_wrep", [128, S], F32, kind="ExternalOutput").ap()
        taps["acts"] = nc.dram_tensor("tap_acts", [11, 128, S], F32, kind="ExternalOutput").ap()

    sb = nc.alloc_sbuf_tensor
    xnT = [sb(f"xnT{c}", [128, S], BF16) for c in range(ND)]
    accg = sb("accg", [128, NT * D], BF16)
    A1 = sb("A1", [128, 11 * S], BF16)
    WARENA = sb("WARENA", [128, 23552], BF16)
    w_rep = sb("w_rep", [128, S], BF16)
    ewT = sb("ewT", [NE, S], BF16)
    xt = sb("xt", [128, 2 * D], F32)
    xn_t = sb("xn_t", [128, 2 * D], F32)
    sg_t = sb("sg_t", [128, 2 * 1024], BF16)
    up_t = sb("up_t", [128, 2 * 1088], BF16)
    stat = sb("stat", [128, 32], F32)
    scores = sb("scores", [128, NT * 8], F32)
    strip = sb("strip", [128, STRIP_W], F32)
    ident_sb = sb("ident_sb", [128, 128], F32)
    ones_sb = sb("ones_sb", [1, 128], F32)
    tblp = sb("tblp", [NB + 1, 1], F32)
    bvec_sb = sb("bvec_sb", [128, 20], F32)
    smallA = sb("smallA", [128, 64], F32)
    rw_hl = sb("rw_hl", [128, 8 * 16], BF16)
    bias_rep = sb("bias_rep", [128, 3 * D], BF16)
    rw_stage = sb("rw_stage", [128, 64], F32)
    cns = sb("cns", [128, 4], F32)
    ones_bf = sb("ones_bf", [1, 128], BF16)
    sel8_sb = sb("sel8_sb", [NE, NE * 128], BF16)
    kvq = sb("kvq", [128, 544], BF16)
    inner_t = xn_t
    innerT_t = sg_t
    cols1 = sb("cols1", [128, 8], F32)
    g1b = None

    pp = nc.alloc_psum_tensor
    pA = pp("pA", [128, 1024], F32)
    pB = pp("pB", [128, 1024], F32)
    pC = pp("pC", [128, 1024], F32)
    pD = pp("pD", [128, 1024], F32)

    ENG = ["sync", "gpsimd", "vector", "scalar", "tensor"]
    prog = {e: [] for e in ENG}
    sem_handles = {}
    counts = {}

    class Ctx:
        pass

    ctx = Ctx()

    def emit(eng, fn):
        prog[eng].append(fn)

    class SemTracker:
        def __init__(self, name):
            self.name = name
            self.n = 0

        def tick(self, amt):
            self.n += amt
            return self.n

    sems = {k: SemTracker(k) for k in ["dw", "dx", "t", "v", "s", "g"]}

    def inc(eng, ins_fn, sem, amt):
        c = sems[sem].tick(amt)

        def f(e, ins_fn=ins_fn, sem=sem, amt=amt):
            ins_fn(e).then_inc(sem_handles[sem], amt)
        emit(eng, f)
        return c

    def raw(eng, ins_fn):
        emit(eng, ins_fn)

    def wait(eng, sem, val):
        if val <= 0:
            return

        def f(e, sem=sem, val=val):
            e.wait_ge(sem_handles[sem], val)
        emit(eng, f)

    def phase0():
        cg = inc("gpsimd", lambda e: e.memset(cns[:, 0:1], 1e-5), "g", 1)
        cg = inc("gpsimd", lambda e: e.memset(cns[:, 1:2], 1e-6), "g", 1)
        cg = inc("gpsimd", lambda e: e.memset(ones_bf[0:1, :], 1.0), "g", 1)
        cg = inc("gpsimd", lambda e: e.memset(kvq[:, 543:544], 1.0), "g", 1)
        ctx.cns_ready = cg
        c = inc("sync", lambda e: e.dma_start(ident_sb[:, :], ident), "dx", 16)
        ctx.sel8_ready = inc("sync", lambda e: e.dma_start(sel8_sb[:, :], sel8), "dx", 16)
        inc("sync", lambda e: e.dma_start(ones_sb[:, :], ones_row), "dx", 16)
        rw_r = router_w.rearrange("(dc p) e -> p dc e", p=128)
        c2 = inc("sync", lambda e: e.dma_start(rw_stage[:, 0:64].rearrange("p (dc e) -> p dc e", dc=8), rw_r), "dx", 16)
        wait("scalar", "dx", c2)
        for dc in range(ND):
            raw("scalar", lambda e, dc=dc: e.copy(rw_hl[:, dc * 16: dc * 16 + 8], rw_stage[:, dc * 8: dc * 8 + 8]))
        cs = inc("scalar", lambda e: e.copy(smallA[:, 63:64], ident_sb[:, 0:1]), "s", 1)
        wait("vector", "s", cs)
        for dc in range(ND):
            fn = lambda e, dc=dc: e.tensor_tensor(
                rw_hl[:, dc * 16 + 8: dc * 16 + 16], rw_stage[:, dc * 8: dc * 8 + 8],
                rw_hl[:, dc * 16: dc * 16 + 8], ALU.subtract)
            if dc < ND - 1:
                raw("vector", fn)
            else:
                ctx.rw_ready = inc("vector", fn, "v", 1)

        row_buf = smallA
        c3 = inc("sync", lambda e: e.dma_start(xt[0:1, 0:D], bout0[None, :]), "dx", 16)
        c3 = inc("sync", lambda e: e.dma_start(xt[0:1, D:2 * D], bv1[None, :]), "dx", 16)
        c3 = inc("sync", lambda e: e.dma_start(xn_t[0:1, 0:D], bias1[None, :]), "dx", 16)
        wait("tensor", "dx", c3)
        tmm = None
        for k in range(3):
            pb = (pA, pB, pC)[k]
            src_row = (xt, xt, xn_t)[k]
            off = (0, D, 0)[k]
            for half in range(2):
                tmm = inc("tensor", lambda e, half=half, pb=pb, src_row=src_row, off=off: e.matmul(
                    pb[:, half * 512:(half + 1) * 512],
                    ones_sb[0:1, :], src_row[0:1, off + half * 512: off + (half + 1) * 512],
                    start=True, stop=True), "t", 1)
        ctx.bias_mm_done = tmm
        wait("vector", "t", tmm)
        cv = None
        for k in range(3):
            cv = inc("vector", lambda e, k=k: e.tensor_copy(
                bias_rep[:, k * D:(k + 1) * D], (pA if k == 0 else (pB if k == 1 else pC))[:, 0:1024]), "v", 1)
        ctx.bias_rep_ready = cv
        ctx.p0_psum_free = cv

        c4 = inc("sync", lambda e: e.dma_start(cols1[:, 0:1], bqk1[:, None]), "dx", 16)
        c4 = inc("sync", lambda e: e.dma_start(cols1[:, 1:2], gamma1[0, :][:, None]), "dx", 16)
        c4 = inc("sync", lambda e: e.dma_start(cols1[:, 2:3], gamma1[1, :][:, None]), "dx", 16)
        c4 = inc("sync", lambda e: e.dma_start(cols1[:, 3:4], beta1[0, :][:, None]), "dx", 16)
        c4 = inc("sync", lambda e: e.dma_start(cols1[:, 4:5], beta1[1, :][:, None]), "dx", 16)
        c4 = inc("sync", lambda e: e.dma_start(cols1[0:1, 5:6], wscale), "dx", 16)
        c4 = inc("sync", lambda e: e.dma_start(tblp[0:NB, 0:1], relb[:, None]), "dx", 16)
        ctx.cols_ready = c4

    def phase1():
        def vtick(fn):
            return inc("vector", fn, "v", 1)

        def vwait(c):
            wait("vector", "v", c)

        lo_arena = accg
        wait("scalar", "g", ctx.cns_ready)
        ctx.p1_evac = {}
        ctx.p1_tile_done = {}
        wait("sync", "t", ctx.bias_mm_done)
        mm_last = 0
        for t in range(NT):
            buf = t % 2
            if t >= 2:
                wait("sync", "v", ctx.p1_tile_done[t - 2][0])
                wait("sync", "s", ctx.p1_tile_done[t - 2][1])
            cx = inc("sync", lambda e, t=t, buf=buf: e.dma_start(
                xt[:, buf * D:(buf + 1) * D], x_in[t * 128:(t + 1) * 128, :]), "dx", 16)
            wait("vector", "dx", cx)
            c0 = vtick(lambda e, buf=buf: e.reduce_sum(stat[:, 0:1], xt[:, buf * D:(buf + 1) * D], axis=AX))
            wait("scalar", "dx", cx)
            csq = inc("scalar", lambda e, buf=buf: e.activation(
                xn_t[:, buf * D:(buf + 1) * D], xt[:, buf * D:(buf + 1) * D], AF.Square), "s", 1)
            wait("vector", "s", csq)
            c1 = vtick(lambda e, buf=buf: e.reduce_sum(stat[:, 1:2], xn_t[:, buf * D:(buf + 1) * D], axis=AX))
            vwait(c0)
            c2 = vtick(lambda e: e.tensor_scalar(stat[:, 2:3], stat[:, 0:1], 1.0 / D, None, ALU.mult))
            vwait(c2)
            c3 = vtick(lambda e: e.tensor_tensor(stat[:, 3:4], stat[:, 2:3], stat[:, 2:3], ALU.mult))
            vwait(c3)
            vwait(c1)
            c4 = vtick(lambda e: e.scalar_tensor_tensor(
                stat[:, 4:5], stat[:, 1:2], 1.0 / D, stat[:, 3:4], ALU.mult, ALU.subtract))
            wait("scalar", "v", c4)
            cs1 = inc("scalar", lambda e: e.activation(stat[:, 5:6], stat[:, 4:5], AF.Sqrt, bias=cns[:, 0:1]), "s", 1)
            wait("vector", "s", cs1)
            c5 = vtick(lambda e: e.reciprocal(stat[:, 6:7], stat[:, 5:6]))
            vwait(c5)
            c6 = vtick(lambda e: e.scalar_tensor_tensor(
                stat[:, 7:8], stat[:, 2:3], -1.0, stat[:, 6:7], ALU.mult, ALU.mult))
            vwait(c6)
            cs2 = vtick(lambda e, buf=buf: e.tensor_scalar(
                xn_t[:, buf * D:(buf + 1) * D], xt[:, buf * D:(buf + 1) * D],
                stat[:, 6:7], stat[:, 7:8], ALU.mult, ALU.add))
            wait("tensor", "v", cs2)
            if t == 0:
                wait("tensor", "v", ctx.bias_rep_ready)
            if t >= 2:
                wait("tensor", "v", ctx.p1_evac[t - 2])
            for dc in range(ND):
                pbank = (pA if (dc % 2 == 0) else pB) if t % 2 == 0 else (pC if (dc % 2 == 0) else pD)
                off = (dc // 2) * 128
                mm_last = inc("tensor", lambda e, dc=dc, buf=buf, pbank=pbank, off=off: e.transpose(
                    pbank[:, off:off + 128], xn_t[:, buf * D + dc * 128: buf * D + (dc + 1) * 128], ident_sb[:, :]),
                    "t", 1)
            wait("scalar", "t", mm_last)
            sev = None
            for dc in range(ND):
                pbank = (pA if (dc % 2 == 0) else pB) if t % 2 == 0 else (pC if (dc % 2 == 0) else pD)
                off = (dc // 2) * 128
                sev = inc("scalar", lambda e, dc=dc, t=t, pbank=pbank, off=off: e.copy(
                    xnT[dc][:, t * 128:(t + 1) * 128], pbank[:, off:off + 128]), "s", 1)
            wait("vector", "s", sev)
            vev = None
            for dc in range(ND):
                pbank = (pA if (dc % 2 == 0) else pB) if t % 2 == 0 else (pC if (dc % 2 == 0) else pD)
                off = (dc // 2) * 128
                vev = vtick(lambda e, dc=dc, t=t, pbank=pbank, off=off: e.tensor_tensor(
                    lo_arena[:, dc * S + t * 128: dc * S + (t + 1) * 128],
                    pbank[:, off:off + 128], xnT[dc][:, t * 128:(t + 1) * 128], ALU.subtract))
            ctx.p1_evac[t] = vev
            ctx.p1_tile_done[t] = (cs2, csq)

        ctx.xnT_done_v = sems["v"].n
        ctx.xnT_done_s = sems["s"].n

        wait("tensor", "v", ctx.p1_evac[NT - 1])
        wait("tensor", "v", ctx.rw_ready)
        ctx.scores_evac = {}
        for t in range(NT):
            pbank = pA if t % 2 == 0 else pB
            if t >= 2:
                wait("tensor", "v", ctx.scores_evac[t - 2])
            smm = None
            for dc in range(ND):
                smm = inc("tensor", lambda e, t=t, dc=dc, pbank=pbank: e.matmul(
                    pbank[:, 0:8], xnT[dc][:, t * 128:(t + 1) * 128], rw_hl[:, dc * 16: dc * 16 + 8],
                    start=(dc == 0), stop=False), "t", 1)
            for dc in range(ND):
                smm = inc("tensor", lambda e, t=t, dc=dc, pbank=pbank: e.matmul(
                    pbank[:, 0:8], xnT[dc][:, t * 128:(t + 1) * 128], rw_hl[:, dc * 16 + 8: dc * 16 + 16],
                    start=False, stop=False), "t", 1)
            for dc in range(ND):
                smm = inc("tensor", lambda e, t=t, dc=dc, pbank=pbank: e.matmul(
                    pbank[:, 0:8], lo_arena[:, dc * S + t * 128: dc * S + (t + 1) * 128],
                    rw_hl[:, dc * 16: dc * 16 + 8],
                    start=False, stop=(dc == ND - 1)), "t", 1)
            wait("vector", "t", smm)
            ctx.scores_evac[t] = vtick(lambda e, t=t, pbank=pbank: e.tensor_copy(
                scores[:, t * 8:(t + 1) * 8], pbank[:, 0:8]))

        ctx.ew_evac = {}
        for t in range(NT):
            sl = scores[:, t * 8:(t + 1) * 8]
            vwait(ctx.scores_evac[t])
            c0 = vtick(lambda e, sl=sl: e.reduce_max(stat[:, 8:9], sl, axis=AX))
            vwait(c0)
            c1 = vtick(lambda e, sl=sl: e.tensor_scalar(smallA[:, 0:8], sl, stat[:, 8:9], None, ALU.is_ge))
            vwait(c1)
            c2 = vtick(lambda e, sl=sl: e.scalar_tensor_tensor(
                smallA[:, 8:16], smallA[:, 0:8], -1e30, sl, ALU.mult, ALU.add))
            vwait(c2)
            c3 = vtick(lambda e: e.reduce_max(stat[:, 9:10], smallA[:, 8:16], axis=AX))
            vwait(c3)
            c4 = vtick(lambda e: e.tensor_scalar(smallA[:, 8:16], smallA[:, 8:16], stat[:, 9:10], None, ALU.is_ge))
            c5 = vtick(lambda e: e.tensor_tensor(stat[:, 10:11], stat[:, 8:9], stat[:, 9:10], ALU.subtract))
            wait("scalar", "v", c5)
            cs = inc("scalar", lambda e: e.activation(stat[:, 11:12], stat[:, 10:11], AF.Sigmoid), "s", 1)
            wait("vector", "s", cs)
            c6 = vtick(lambda e: e.tensor_scalar(stat[:, 12:13], stat[:, 11:12], -1.0, 1.0, ALU.mult, ALU.add))
            c7 = vtick(lambda e: e.tensor_scalar(smallA[:, 16:24], smallA[:, 0:8], stat[:, 11:12], None, ALU.mult))
            vwait(c6)
            vwait(c7)
            vwait(c4)
            c8 = vtick(lambda e: e.scalar_tensor_tensor(
                smallA[:, 16:24], smallA[:, 8:16], stat[:, 12:13], smallA[:, 16:24], ALU.mult, ALU.add))
            wait("tensor", "v", c8)
            if t >= 2:
                wait("tensor", "v", ctx.ew_evac[t - 2])
            pbank = pC if t % 2 == 0 else pD
            temm = inc("tensor", lambda e, pbank=pbank: e.transpose(
                pbank[0:8, 0:128], smallA[:, 16:24], ident_sb[:, :]), "t", 1)
            wait("vector", "t", temm)
            ctx.ew_evac[t] = vtick(lambda e, t=t, pbank=pbank: e.tensor_copy(
                ewT[0:8, t * 128:(t + 1) * 128], pbank[0:8, 0:128]))
        ctx.routing_done = sems["v"].n

        if debug_taps:
            wait("gpsimd", "v", ctx.routing_done)
            cdx = None
            for dc in range(ND):
                cdx = inc("gpsimd", lambda e, dc=dc: e.dma_start(taps["xnT"][dc], xnT[dc][:, :]), "dw", 16)
            cdx = inc("gpsimd", lambda e: e.dma_start(taps["ewT"], ewT[:, :]), "dw", 16)
            ctx.tap1_done = cdx
            for eng in ENG:
                wait(eng, "dw", cdx)

    def phase2():
        NSP = 2
        FCS = 11
        NPAIR = 11
        gp_r = gate_p.rearrange("e (dc p) f -> e p dc f", p=128)
        up_r = up_p.rearrange("e (dc p) f -> e p dc f", p=128)
        dn_r = down_p.rearrange("e (fc p) d -> e p fc d", p=128)
        WGv = WARENA.ap()[:, 0:6144].rearrange("p (sl dc f) -> p sl dc f", sl=3, dc=8)
        WUv = WARENA.ap()[:, 6144:12288].rearrange("p (sl dc f) -> p sl dc f", sl=3, dc=8)
        WDa = WARENA.ap()[:, 12288:23552]

        queue = [(e, pair) for e in range(NE) for pair in range(NPAIR)]
        slot_loaded = {}
        slot_consumed = {}
        slot_of = {}
        wd_loaded = {}
        wd_consumed = {}
        pf_ptr = [0]

        def issue_next_slot():
            if pf_ptr[0] >= len(queue):
                return
            idx = pf_ptr[0]
            e, pair = queue[idx]
            if idx >= 3:
                wait("gpsimd", "t", slot_consumed[queue[idx - 3]])
            sl = idx % 3
            slot_of[(e, pair)] = sl
            f0 = pair * 256
            inc("gpsimd", lambda ee, e=e, sl=sl, f0=f0: ee.dma_start(
                WGv[:, sl], gp_r[e][:, :, f0:f0 + 256]), "dw", 16)
            slot_loaded[(e, pair)] = inc("gpsimd", lambda ee, e=e, sl=sl, f0=f0: ee.dma_start(
                WUv[:, sl], up_r[e][:, :, f0:f0 + 256]), "dw", 16)
            pf_ptr[0] = idx + 1

        def issue_wd(e, sp):
            key = (e, sp - 1) if sp >= 1 else (e - 1, 1)
            if key in wd_consumed:
                wait("gpsimd", "t", wd_consumed[key])
            wd_loaded[(e, sp)] = inc("gpsimd", lambda ee, e=e, sp=sp: ee.dma_start(
                WDa.rearrange("p (fc d) -> p fc d", fc=FCS), dn_r[e][:, sp * FCS:(sp + 1) * FCS, :]), "dw", 16)

        wait("gpsimd", "v", ctx.routing_done)
        for _ in range(3):
            issue_next_slot()
        issue_wd(0, 0)

        psum_units = [(pA, pB), (pC, pD)]
        unit_free = {0: [], 1: []}
        ctx.sg_guard = {}
        delay_q = []
        acts_ready = {}
        mac_last = None

        for e in range(moe_ne):
            wait("tensor", "v", ctx.routing_done)
            for g in unit_free[0]:
                wait("tensor", g[0], g[1])
            if e == 0:
                wait("tensor", "dx", ctx.sel8_ready)
            wr_mm = None
            for q in range(4):
                pb = (pA, pB)[q // 2]
                wr_mm = inc("tensor", lambda ee, e=e, q=q, pb=pb: ee.matmul(
                    pb[:, (q % 2) * 512:(q % 2 + 1) * 512], sel8_sb[0:NE, e * 128:(e + 1) * 128],
                    ewT[0:NE, q * 512:(q + 1) * 512], start=True, stop=True), "t", 1)
            wait("vector", "t", wr_mm)
            cwr = None
            for q in range(2):
                pb = (pA, pB)[q]
                cwr = inc("vector", lambda ee, q=q, pb=pb: ee.tensor_copy(
                    w_rep[:, q * 1024:(q + 1) * 1024], pb[:, :]), "v", 1)
            unit_free = {0: [("v", cwr)], 1: unit_free[1]}

            for sp in range(NSP):
                for fcl in range(FCS):
                    fc = sp * FCS + fcl
                    pair, j = fc // 2, fc % 2
                    for th in range(2):
                        u = (fc * 2 + th) % 2
                        gate_ps, up_ps = psum_units[u]
                        wait("tensor", "dw", slot_loaded[(e, pair)])
                        for gg in unit_free[u]:
                            wait("tensor", gg[0], gg[1])
                        gmm = umm = None
                        sl = slot_of[(e, pair)]
                        for dc in range(ND):
                            for tw in range(2):
                                gmm = inc("tensor", lambda ee, dc=dc, tw=tw, th=th, sl=sl, j=j, gate_ps=gate_ps: ee.matmul(
                                    gate_ps[:, tw * 512:(tw + 1) * 512],
                                    WGv[:, sl, dc, j * 128:(j + 1) * 128],
                                    xnT[dc][:, th * 1024 + tw * 512: th * 1024 + (tw + 1) * 512],
                                    start=(dc == 0), stop=(dc == ND - 1)), "t", 1)
                        for dc in range(ND):
                            for tw in range(2):
                                umm = inc("tensor", lambda ee, dc=dc, tw=tw, th=th, sl=sl, j=j, up_ps=up_ps: ee.matmul(
                                    up_ps[:, tw * 512:(tw + 1) * 512],
                                    WUv[:, sl, dc, j * 128:(j + 1) * 128],
                                    xnT[dc][:, th * 1024 + tw * 512: th * 1024 + (tw + 1) * 512],
                                    start=(dc == 0), stop=(dc == ND - 1)), "t", 1)
                        slot_consumed[(e, pair)] = sems["t"].n
                        b2 = (fc * 2 + th) % 2
                        wait("scalar", "t", gmm)
                        prev = ctx.sg_guard.get(b2)
                        if prev is not None:
                            wait("scalar", "v", prev)
                        csg = inc("scalar", lambda ee, b2=b2, gate_ps=gate_ps: ee.activation(
                            sg_t[:, b2 * 1024:(b2 + 1) * 1024], gate_ps[:, :], AF.Silu), "s", 1)
                        wait("vector", "t", umm)
                        wait("vector", "s", csg)
                        ct1 = inc("vector", lambda ee, b2=b2, up_ps=up_ps: ee.tensor_tensor(
                            up_t[:, b2 * 1088: b2 * 1088 + 1024], up_ps[:, :],
                            sg_t[:, b2 * 1024:(b2 + 1) * 1024], ALU.mult), "v", 1)
                        ctx.sg_guard[b2] = ct1
                        unit_free[u] = [("s", csg), ("v", ct1)]
                        def acts_op(ee, b2=b2, fcl=fcl, th=th):
                            return ee.tensor_tensor(
                                A1[:, fcl * S + th * 1024: fcl * S + (th + 1) * 1024],
                                up_t[:, b2 * 1088: b2 * 1088 + 1024],
                                w_rep[:, th * 1024:(th + 1) * 1024], ALU.mult)
                        delay_q.append((ct1, acts_op, fcl))
                        if len(delay_q) >= 2:
                            cdep, op, fcl_d = delay_q.pop(0)
                            wait("vector", "v", cdep)
                            acts_ready[fcl_d] = inc("vector", op, "v", 1)
                    if j == 1:
                        issue_next_slot()
                while delay_q:
                    cdep, op, fcl_d = delay_q.pop(0)
                    wait("vector", "v", cdep)
                    acts_ready[fcl_d] = inc("vector", op, "v", 1)

                wait("tensor", "dw", wd_loaded[(e, sp)])
                for fcl in range(FCS):
                    wait("tensor", "v", acts_ready[fcl])
                down_free = {}
                for t in range(NT):
                    dp = (pA, pB)[t % 2]
                    if t < 2:
                        for uu in range(2):
                            for gg in unit_free[uu]:
                                wait("tensor", gg[0], gg[1])
                    else:
                        wait("tensor", "v", down_free[t % 2])
                    dmm = None
                    for fcl in range(FCS):
                        for h in range(2):
                            dmm = inc("tensor", lambda ee, fcl=fcl, h=h, t=t, dp=dp: ee.matmul(
                                dp[:, h * 512:(h + 1) * 512],
                                A1[:, fcl * S + t * 128: fcl * S + (t + 1) * 128],
                                WDa[:, fcl * 1024 + h * 512: fcl * 1024 + (h + 1) * 512],
                                start=(fcl == 0), stop=(fcl == FCS - 1)), "t", 1)
                    wait("vector", "t", dmm)
                    if e == 0 and sp == 0:
                        cmac = inc("vector", lambda ee, t=t, dp=dp: ee.tensor_copy(
                            accg[:, t * 1024:(t + 1) * 1024], dp[:, :]), "v", 1)
                    else:
                        cmac = inc("vector", lambda ee, t=t, dp=dp: ee.tensor_tensor(
                            accg[:, t * 1024:(t + 1) * 1024], dp[:, :],
                            accg[:, t * 1024:(t + 1) * 1024], ALU.add), "v", 1)
                    down_free[t % 2] = cmac
                    mac_last = cmac
                wd_consumed[(e, sp)] = sems["t"].n
                unit_free = {0: [("v", mac_last)], 1: []}
                if sp == 0:
                    issue_wd(e, 1)
                elif e + 1 < NE:
                    issue_wd(e + 1, 0)
        ctx.moe_mac_done = mac_last

        cg = None
        wait("scalar", "v", ctx.moe_mac_done)
        for t in range(NT):
            cg = inc("scalar", lambda ee, t=t: ee.activation(
                accg[:, t * 1024:(t + 1) * 1024], accg[:, t * 1024:(t + 1) * 1024],
                AF.Silu if moe_silu else AF.Copy), "s", 1)
        ctx.g_ready = cg

        if debug_taps:
            wait("gpsimd", "s", cg)
            cdx = None
            for t in range(NT):
                cdx = inc("gpsimd", lambda ee, t=t: ee.dma_start(
                    taps["g"][t * 128:(t + 1) * 128, :], accg[:, t * 1024:(t + 1) * 1024]), "dw", 16)
            cdx = inc("gpsimd", lambda ee: ee.dma_start(taps["wrep"], w_rep[:, :]), "dw", 16)
            for fcl in range(FCS):
                cdx = inc("gpsimd", lambda ee, fcl=fcl: ee.dma_start(
                    taps["acts"][fcl], A1[:, fcl * S:(fcl + 1) * S]), "dw", 16)
            for eng in ENG:
                wait(eng, "dw", cdx)

    def phase3():
        H = 16
        DH = 64
        wk_ap = WARENA.ap()[:, 0:8192].rearrange("p (dc f) -> p dc f", dc=8)
        wv_ap = WARENA.ap()[:, 8192:16384].rearrange("p (dc f) -> p dc f", dc=8)
        wq_ap = WARENA.ap()[:, 0:8192].rearrange("p (dc f) -> p dc f", dc=8)
        wout_ap = WARENA.ap()[:, 8192:16384].rearrange("p (dc f) -> p dc f", dc=8)
        wqkv_r = wqkv.rearrange("(dc p) f -> p dc f", p=128)
        wout_r = wout0.rearrange("(dc p) f -> p dc f", p=128)

        moe_t_done = sems["t"].n
        moe_v_done = sems["v"].n
        STG = 16384
        KV2 = 17408
        KS2 = KV2 + 1024
        ONES1 = KS2 + 16
        wait("gpsimd", "t", moe_t_done)
        cmz = inc("gpsimd", lambda e: e.memset(A1[:, KV2:ONES1], 0.0), "g", 1)
        cmz = inc("gpsimd", lambda e: e.memset(A1[:, ONES1:ONES1 + 1], 1.0), "g", 1)
        ctx.kv2_zeroed = cmz
        ck = inc("gpsimd", lambda e: e.dma_start(wk_ap, wqkv_r[:, :, D:2 * D]), "dw", 16)
        cv_ = inc("gpsimd", lambda e: e.dma_start(wv_ap, wqkv_r[:, :, 2 * D:3 * D]), "dw", 16)
        ctx.wk_ready, ctx.wv_ready = ck, cv_

        wait("tensor", "dw", cv_)
        wait("tensor", "g", ctx.kv2_zeroed)
        wait("tensor", "v", moe_v_done)
        wait("tensor", "s", ctx.g_ready)
        kel = {}
        vcp = {}
        ksmac = {}
        for t in range(NT):
            kp = pA if t % 2 == 0 else pB
            if t >= 2:
                wait("tensor", "v", kel[t - 2][1])
            kmm = None
            for dc in range(ND):
                for tw in range(2):
                    kmm = inc("tensor", lambda e, dc=dc, tw=tw, t=t, kp=kp: e.matmul(
                        kp[:, tw * 512:(tw + 1) * 512], xnT[dc][:, t * 128:(t + 1) * 128],
                        wk_ap[:, dc, tw * 512:(tw + 1) * 512],
                        start=(dc == 0), stop=(dc == ND - 1)), "t", 1)
            vmm = {}
            for vg in range(2):
                prev = vcp.get((t * 2 + vg) - 2)
                if prev is not None:
                    wait("tensor", "v", prev)
                if vg == 1 and t >= 1:
                    wait("tensor", "v", ksmac[t - 1])
                for dc in range(ND):
                    vmm[vg] = inc("tensor", lambda e, dc=dc, vg=vg, t=t: e.matmul(
                        pC[:, vg * 512:(vg + 1) * 512], xnT[dc][:, t * 128:(t + 1) * 128],
                        wv_ap[:, dc, vg * 512:(vg + 1) * 512],
                        start=(dc == 0), stop=(dc == ND - 1)), "t", 1)
            b = t % 2
            scr = xn_t if b == 0 else xt
            wait("vector", "t", kmm)
            c1 = inc("vector", lambda e, kp=kp, scr=scr: e.tensor_scalar(
                scr[:, 0:1024], kp[:, :], 0.0, None, ALU.min), "v", 1)
            wait("scalar", "v", c1)
            c2 = inc("scalar", lambda e, scr=scr: e.activation(
                scr[:, 1024:2048], scr[:, 0:1024], AF.Exp), "s", 1)
            wait("vector", "s", c2)
            c3 = inc("vector", lambda e, kp=kp, scr=scr, b=b: e.scalar_tensor_tensor(
                sg_t[:, b * 1024:(b + 1) * 1024], kp[:, :], 0.0, scr[:, 1024:2048],
                ALU.max, ALU.add), "v", 1)
            kel[t] = (c2, c3)
            for vg in range(2):
                wait("vector", "t", vmm[vg])
                vcp[t * 2 + vg] = inc("vector", lambda e, vg=vg, b=b: e.tensor_copy(
                    up_t[:, b * 1088 + vg * 512: b * 1088 + (vg + 1) * 512],
                    pC[:, vg * 512:(vg + 1) * 512]), "v", 1)
            wait("tensor", "v", c3)
            wait("tensor", "v", vcp[t * 2 + 1])
            for h in range(H):
                inc("tensor", lambda e, h=h, t=t, b=b: e.matmul(
                    pD[0:64, h * 64:(h + 1) * 64],
                    sg_t[:, b * 1024 + h * 64: b * 1024 + (h + 1) * 64],
                    up_t[:, b * 1088 + h * 64: b * 1088 + h * 64 + 64],
                    start=(t == 0 and h % 8 == 0), stop=(t == NT - 1 and h % 8 == 7)), "t", 1)
            ksm = None
            for c in range(ND):
                ksm = inc("tensor", lambda e, c=c, t=t, b=b: e.matmul(
                    pC[:, 512 + c:513 + c],
                    sg_t[:, b * 1024 + c * 128: b * 1024 + (c + 1) * 128],
                    A1[:, ONES1:ONES1 + 1],
                    start=(c == 0), stop=(c == ND - 1)), "t", 1)
            wait("vector", "t", ksm)
            if t == 0:
                ksmac[t] = inc("vector", lambda e: e.tensor_copy(
                    smallA[:, 48:56], pC[:, 512:520]), "v", 1)
            else:
                ksmac[t] = inc("vector", lambda e: e.tensor_tensor(
                    smallA[:, 48:56], pC[:, 512:520], smallA[:, 48:56], ALU.add), "v", 1)
        wait("vector", "t", sems["t"].n)
        ckv = inc("vector", lambda e: e.tensor_copy(A1[0:64, STG:STG + 512], pD[0:64, 0:512]), "v", 1)
        ckv = inc("vector", lambda e: e.tensor_copy(A1[0:64, STG + 512:STG + 1024], pD[0:64, 512:1024]), "v", 1)
        for h in range(H):
            ckv = inc("vector", lambda e, h=h: e.tensor_copy(
                A1[64 * (h % 2):64 * (h % 2) + 64, KS2 + h:KS2 + h + 1],
                smallA[64 * (h % 2):64 * (h % 2) + 64, 48 + h // 2:49 + h // 2]), "v", 1)
        wait("gpsimd", "v", ckv)
        cdma = None
        for h in range(H):
            cdma = inc("gpsimd", lambda e, h=h: e.dma_start(
                A1[64 * (h % 2):64 * (h % 2) + 64,
                   KV2 + (h // 2) * 128 + 64 * (h % 2): KV2 + (h // 2) * 128 + 64 * (h % 2) + 64],
                A1[0:64, STG + h * 64: STG + (h + 1) * 64]), "dw", 16)
        ctx.kv_ready_dw = cdma
        ctx.kv_ready = ckv

        cq = inc("gpsimd", lambda e: e.dma_start(wq_ap, wqkv_r[:, :, 0:D]), "dw", 16)
        co = inc("gpsimd", lambda e: e.dma_start(wout_ap, wout_r), "dw", 16)
        ctx.wq_ready, ctx.wout_ready = cq, co
        wait("tensor", "dw", cq)
        wait("tensor", "v", ctx.kv_ready)
        qel = {}
        for qc in range(ND):
            for th2 in range(2):
                step = qc * 2 + th2
                qp = pA if step % 2 == 0 else pB
                if step >= 2:
                    wait("tensor", "v", qel[step - 2])
                qmm = None
                for dc in range(ND):
                    for tw in range(2):
                        qmm = inc("tensor", lambda e, dc=dc, tw=tw, th2=th2, qc=qc, qp=qp: e.matmul(
                            qp[:, tw * 512:(tw + 1) * 512],
                            wq_ap[:, dc, qc * 128:(qc + 1) * 128],
                            xnT[dc][:, th2 * 1024 + tw * 512: th2 * 1024 + (tw + 1) * 512],
                            start=(dc == 0), stop=(dc == ND - 1)), "t", 1)
                b = step % 2
                scr = xn_t if b == 0 else xt
                wait("vector", "t", qmm)
                c1 = inc("vector", lambda e, qp=qp, scr=scr: e.tensor_scalar(
                    scr[:, 0:1024], qp[:, :], 0.0, None, ALU.min), "v", 1)
                wait("scalar", "v", c1)
                c2 = inc("scalar", lambda e, scr=scr: e.activation(
                    scr[:, 1024:2048], scr[:, 0:1024], AF.Exp), "s", 1)
                wait("vector", "s", c2)
                c3 = inc("vector", lambda e, qp=qp, scr=scr, qc=qc, th2=th2: e.scalar_tensor_tensor(
                    A1[:, qc * S + th2 * 1024: qc * S + (th2 + 1) * 1024], qp[:, :], 0.0,
                    scr[:, 1024:2048], ALU.max, ALU.add), "v", 1)
                qel[step] = c3
        ctx.q_ready = qel[15]

        wait("tensor", "dw", co)
        wait("tensor", "dw", ctx.kv_ready_dw)
        wait("tensor", "v", ctx.q_ready)
        den_ev = {}
        in_ev = {}
        comb = {}
        for t in range(l0_tiles):
            npp = pA if t % 2 == 0 else pB
            dnp = pB if t % 2 == 0 else pA
            if t >= 2:
                wait("tensor", "s", in_ev[t - 2])
            if t >= 1:
                wait("tensor", "v", den_ev[t - 1])
            for dc in range(ND):
                inc("tensor", lambda e, dc=dc, t=t, npp=npp: e.matmul(
                    npp[:, dc * 128:(dc + 1) * 128],
                    A1[:, dc * S + t * 128: dc * S + (t + 1) * 128],
                    A1[:, KV2 + dc * 128: KV2 + (dc + 1) * 128],
                    start=True, stop=True), "t", 1)
            if l0_stage < 2:
                continue
            nmm = None
            if t >= 1:
                wait("tensor", "s", in_ev[t - 1])
            for dc in range(ND):
                nmm = inc("tensor", lambda e, dc=dc, t=t, dnp=dnp: e.matmul(
                    dnp[:, 2 * dc:2 * dc + 2],
                    A1[:, dc * S + t * 128: dc * S + (t + 1) * 128],
                    A1[:, KS2 + 2 * dc: KS2 + 2 * dc + 2],
                    start=True, stop=True), "t", 1)
            if l0_stage < 3:
                continue
            wait("vector", "t", nmm)
            if t >= 1:
                wait("vector", "s", in_ev[t - 1])
            c1 = inc("vector", lambda e, dnp=dnp: e.tensor_scalar(
                stat[:, 16:32], dnp[:, 0:16], 1e-6, None, ALU.add), "v", 1)
            wait("vector", "v", c1)
            c2 = inc("vector", lambda e: e.reciprocal(smallA[:, 32:48], stat[:, 16:32]), "v", 1)
            den_ev[t] = c2
            wait("scalar", "v", c2)
            if l0_stage < 4:
                continue
            if t >= 1:
                wait("scalar", "t", comb[t - 1][1])
            sev = None
            for h in range(H):
                sev = inc("scalar", lambda e, h=h, npp=npp: e.activation(
                    inner_t[:, 0 * D + h * 64: 0 * D + (h + 1) * 64], npp[:, h * 64:(h + 1) * 64],
                    AF.Copy, scale=smallA[:, 32 + h:33 + h]), "s", 1)
            in_ev[t] = sev
            if l0_stage < 5:
                continue
            wait("tensor", "s", sev)
            if t >= 1:
                wait("tensor", "s", comb[t - 1][2])
            trl = None
            for dc in range(ND):
                trl = inc("tensor", lambda e, dc=dc: e.transpose(
                    pC[:, dc * 128 - (dc // 4) * 512 + (dc // 4) * 512: dc * 128 + 128] if False else pC[:, dc * 128: dc * 128 + 128] if dc < 4 else pC[:, 512 + (dc - 4) * 128: 512 + (dc - 4) * 128 + 128],
                    inner_t[:, dc * 128:(dc + 1) * 128], ident_sb[:, :]), "t", 1)
            tr_done = trl
            wait("scalar", "t", trl)
            iev = None
            for dc in range(ND):
                iev = inc("scalar", lambda e, dc=dc: e.copy(
                    sg_t[:, dc * 128:(dc + 1) * 128],
                    pC[:, dc * 128: dc * 128 + 128] if dc < 4 else pC[:, 512 + (dc - 4) * 128: 512 + (dc - 4) * 128 + 128]), "s", 1)
            if l0_stage < 6:
                continue
            wait("tensor", "s", iev)
            if t >= 1:
                wait("tensor", "v", comb[t - 1][0])
            pmm = None
            for dcol in range(2):
                for ic in range(ND):
                    pmm = inc("tensor", lambda e, dcol=dcol, ic=ic: e.matmul(
                        pD[:, dcol * 512:(dcol + 1) * 512], sg_t[:, ic * 128:(ic + 1) * 128],
                        wout_ap[:, ic, dcol * 512:(dcol + 1) * 512],
                        start=(ic == 0), stop=(ic == ND - 1)), "t", 1)
            if l0_stage < 7:
                continue
            if t == 0:
                wait("sync", "v", ctx.q_ready)
            if t >= 2:
                wait("sync", "v", comb[t - 2][3])
            cx = inc("sync", lambda e, t=t: e.dma_start(
                xt[:, (t % 2) * D:(t % 2 + 1) * D], x_in[t * 128:(t + 1) * 128, :]), "dx", 16)
            wait("vector", "t", pmm)
            if t == 0:
                wait("vector", "s", ctx.g_ready)
            cc1 = inc("vector", lambda e: e.tensor_tensor(
                xn_t[:, 1024:2048], pD[:, :], bias_rep[:, 0:D], ALU.add), "v", 1)
            wait("vector", "v", cc1)
            cc2 = inc("vector", lambda e, t=t: e.tensor_tensor(
                xn_t[:, 1024:2048], xn_t[:, 1024:2048], accg[:, t * 1024:(t + 1) * 1024], ALU.mult), "v", 1)
            wait("vector", "dx", cx)
            wait("vector", "v", cc2)
            cc3 = inc("vector", lambda e, t=t: e.tensor_tensor(
                xn_t[:, 1024:2048], xn_t[:, 1024:2048], xt[:, (t % 2) * D:(t % 2 + 1) * D], ALU.add), "v", 1)
            wait("sync", "v", cc3)
            cst = inc("sync", lambda e, t=t: e.dma_start(
                x1_dram[t * 128:(t + 1) * 128, :], xn_t[:, 1024:2048]), "dx", 16)
            wait("vector", "dx", cst)
            comb[t] = (cc1, tr_done, iev, cc3)
        ctx.x1_done = sems["dx"].n

        if debug_taps:
            wait("gpsimd", "dx", ctx.x1_done)
            cdx = inc("gpsimd", lambda e: e.dma_start(taps["x1"], x1_dram), "dw", 16)
            for eng in ENG:
                wait(eng, "dw", cdx)

    def phase4():
        p3_t = sems["t"].n
        p3_v = sems["v"].n
        p3_s = sems["s"].n

        def vtick(fn):
            return inc("vector", fn, "v", 1)

        def vwait(c):
            wait("vector", "v", c)

        wait("tensor", "v", p3_v)
        wait("tensor", "s", p3_s)
        ctx.p4_evac = {}
        tile_done = {}
        for t in range(NT):
            buf = t % 2
            if t == 0:
                wait("sync", "v", p3_v)
                wait("sync", "s", p3_s)
            if t >= 2:
                wait("sync", "v", tile_done[t - 2][0])
                wait("sync", "s", tile_done[t - 2][1])
                wait("sync", "t", tile_done[t - 2][2])
            cx = inc("sync", lambda e, t=t, buf=buf: e.dma_start(
                xt[:, buf * D:(buf + 1) * D], x1_dram[t * 128:(t + 1) * 128, :]), "dx", 16)
            wait("vector", "dx", cx)
            c0 = vtick(lambda e, buf=buf: e.reduce_sum(stat[:, 0:1], xt[:, buf * D:(buf + 1) * D], axis=AX))
            wait("scalar", "dx", cx)
            csq = inc("scalar", lambda e, buf=buf: e.activation(
                xn_t[:, buf * D:(buf + 1) * D], xt[:, buf * D:(buf + 1) * D], AF.Square), "s", 1)
            wait("vector", "s", csq)
            c1 = vtick(lambda e, buf=buf: e.reduce_sum(stat[:, 1:2], xn_t[:, buf * D:(buf + 1) * D], axis=AX))
            vwait(c0)
            c2 = vtick(lambda e: e.tensor_scalar(stat[:, 2:3], stat[:, 0:1], 1.0 / D, None, ALU.mult))
            vwait(c2)
            c3 = vtick(lambda e: e.tensor_tensor(stat[:, 3:4], stat[:, 2:3], stat[:, 2:3], ALU.mult))
            vwait(c3)
            vwait(c1)
            c4 = vtick(lambda e: e.scalar_tensor_tensor(
                stat[:, 4:5], stat[:, 1:2], 1.0 / D, stat[:, 3:4], ALU.mult, ALU.subtract))
            wait("scalar", "v", c4)
            cs1 = inc("scalar", lambda e: e.activation(stat[:, 5:6], stat[:, 4:5], AF.Sqrt, bias=cns[:, 0:1]), "s", 1)
            wait("vector", "s", cs1)
            c5 = vtick(lambda e: e.reciprocal(stat[:, 6:7], stat[:, 5:6]))
            vwait(c5)
            c6 = vtick(lambda e: e.scalar_tensor_tensor(
                stat[:, 7:8], stat[:, 2:3], -1.0, stat[:, 6:7], ALU.mult, ALU.mult))
            vwait(c6)
            cs2 = vtick(lambda e, buf=buf: e.tensor_scalar(
                xn_t[:, buf * D:(buf + 1) * D], xt[:, buf * D:(buf + 1) * D],
                stat[:, 6:7], stat[:, 7:8], ALU.mult, ALU.add))
            c7 = vtick(lambda e: e.tensor_tensor(stat[:, 8:9], stat[:, 4:5], cns[:, 0:1], ALU.add))
            vwait(c7)
            c8 = vtick(lambda e: e.reciprocal(stat[:, 9:10], stat[:, 8:9]))
            vwait(c8)
            c9 = vtick(lambda e: e.tensor_tensor(stat[:, 10:11], stat[:, 4:5], stat[:, 9:10], ALU.mult))
            vwait(c9)
            c10 = vtick(lambda e: e.tensor_tensor(stat[:, 11:12], stat[:, 10:11], cns[:, 1:2], ALU.add))
            wait("scalar", "v", c10)
            cs3 = inc("scalar", lambda e: e.activation(stat[:, 12:13], stat[:, 11:12], AF.Sqrt), "s", 1)
            wait("vector", "s", cs3)
            c11 = vtick(lambda e: e.reciprocal(stat[:, 13:14], stat[:, 12:13]))
            vwait(cs2)
            c12 = vtick(lambda e, buf=buf: e.reduce_max(
                stat[:, 14:15], xn_t[:, buf * D:(buf + 1) * D], axis=AX, apply_absolute_value=True))
            vwait(c11)
            vwait(c12)
            c13 = vtick(lambda e: e.tensor_tensor(stat[:, 15:16], stat[:, 13:14], stat[:, 14:15], ALU.mult))
            vwait(c13)
            c14 = vtick(lambda e: e.tensor_scalar(stat[:, 16:17], stat[:, 15:16], 1e-5, None, ALU.max))
            vwait(c14)
            c15 = vtick(lambda e: e.reciprocal(stat[:, 17:18], stat[:, 16:17]))
            vwait(c15)
            c16 = vtick(lambda e: e.tensor_scalar(stat[:, 18:19], stat[:, 17:18], 127.0, None, ALU.mult))
            c17 = vtick(lambda e: e.tensor_scalar(stat[:, 19:20], stat[:, 16:17], 1.0 / 127.0, None, ALU.mult))
            vwait(c16)
            c18 = vtick(lambda e: e.tensor_tensor(stat[:, 20:21], stat[:, 13:14], stat[:, 18:19], ALU.mult))
            vwait(c18)
            q1 = vtick(lambda e, buf=buf: e.tensor_scalar(
                xt[:, buf * D:(buf + 1) * D], xn_t[:, buf * D:(buf + 1) * D],
                stat[:, 20:21], None, ALU.mult))
            vwait(q1)
            q2 = vtick(lambda e, buf=buf: e.tensor_scalar(
                xt[:, buf * D:(buf + 1) * D], xt[:, buf * D:(buf + 1) * D],
                RND_C, RND_C, ALU.add, ALU.subtract))
            vwait(q2)
            q3 = vtick(lambda e, buf=buf: e.tensor_scalar(
                xt[:, buf * D:(buf + 1) * D], xt[:, buf * D:(buf + 1) * D],
                127.0, -128.0, ALU.min, ALU.max))
            vwait(q3)
            vwait(c17)
            q4 = vtick(lambda e, buf=buf: e.tensor_scalar(
                xt[:, buf * D:(buf + 1) * D], xt[:, buf * D:(buf + 1) * D],
                stat[:, 19:20], None, ALU.mult))
            wait("tensor", "v", cs2)
            if t >= 1:
                wait("tensor", "s", ctx.p4_evac[t - 1])
            mm1 = None
            for dc in range(ND):
                pbank = pA if dc % 2 == 0 else pB
                off = (dc // 2) * 128
                mm1 = inc("tensor", lambda e, dc=dc, buf=buf, pbank=pbank, off=off: e.transpose(
                    pbank[:, off:off + 128], xn_t[:, buf * D + dc * 128: buf * D + (dc + 1) * 128],
                    ident_sb[:, :]), "t", 1)
            wait("tensor", "v", q4)
            mm2 = None
            for dc in range(ND):
                pbank = pC if dc % 2 == 0 else pD
                off = (dc // 2) * 128
                mm2 = inc("tensor", lambda e, dc=dc, buf=buf, pbank=pbank, off=off: e.transpose(
                    pbank[:, off:off + 128], xt[:, buf * D + dc * 128: buf * D + (dc + 1) * 128],
                    ident_sb[:, :]), "t", 1)
            wait("scalar", "t", mm2)
            sev = None
            for dc in range(ND):
                pbank = pA if dc % 2 == 0 else pB
                off = (dc // 2) * 128
                sev = inc("scalar", lambda e, dc=dc, t=t, pbank=pbank, off=off: e.copy(
                    xnT[dc][:, t * 128:(t + 1) * 128], pbank[:, off:off + 128]), "s", 1)
            for dc in range(ND):
                pbank = pC if dc % 2 == 0 else pD
                off = (dc // 2) * 128
                sev = inc("scalar", lambda e, dc=dc, t=t, pbank=pbank, off=off: e.copy(
                    A1[:, dc * S + t * 128: dc * S + (t + 1) * 128], pbank[:, off:off + 128]), "s", 1)
            ctx.p4_evac[t] = sev
            tile_done[t] = (q4, csq, mm2)
        ctx.p4_done_s = sems["s"].n
        ctx.p4_done_v = sems["v"].n
        ctx.p4_done_t = sems["t"].n

    def phase5():
        wi_r = w_int8.rearrange("(wc p) i -> wc p i", p=128)
        wait("tensor", "s", ctx.p4_done_s)
        wait("tensor", "dx", ctx.cols_ready)
        cwm = inc("tensor", lambda e: e.matmul(
            pA[:, 0:1], ones_sb[0:1, :], cols1[0:1, 5:6], start=True, stop=True), "t", 1)
        wait("vector", "t", cwm)
        cws = inc("vector", lambda e: e.tensor_copy(smallA[:, 60:61], pA[:, 0:1]), "v", 1)
        wait("gpsimd", "v", ctx.p4_done_v)
        wait("gpsimd", "t", ctx.p4_done_t)
        wt_ev = {}
        for wc in range(ND):
            buf = wc % 2
            scr = xt if buf == 0 else xn_t
            if wc >= 2:
                wait("gpsimd", "t", wt_ev[wc - 2][0])
            cwl = inc("gpsimd", lambda e, wc=wc, scr=scr: e.dma_start(
                scr[:, 0:1024], wi_r[wc]), "dw", 16)
            wait("tensor", "dw", cwl)
            if wc == 0:
                wait("tensor", "v", cws)
            if wc >= 2:
                wait("tensor", "s", wt_ev[wc - 2][1])
            mmw = None
            for ic in range(ND):
                pbank = (pA if ic % 2 == 0 else pB) if wc % 2 == 0 else (pC if ic % 2 == 0 else pD)
                off = (ic // 2) * 128
                mmw = inc("tensor", lambda e, ic=ic, scr=scr, pbank=pbank, off=off: e.transpose(
                    pbank[:, off:off + 128], scr[:, ic * 128:(ic + 1) * 128], ident_sb[:, :]), "t", 1)
            wait("scalar", "t", mmw)
            sev = None
            for ic in range(ND):
                pbank = (pA if ic % 2 == 0 else pB) if wc % 2 == 0 else (pC if ic % 2 == 0 else pD)
                off = (ic // 2) * 128
                sev = inc("scalar", lambda e, ic=ic, wc=wc, pbank=pbank, off=off: e.copy(
                    WARENA.ap()[:, 0:8192].rearrange("p (ic o) -> p ic o", ic=8)[:, ic, wc * 128:(wc + 1) * 128],
                    pbank[:, off:off + 128]), "s", 1)
            wt_ev[wc] = (mmw, sev)
        wt_v = WARENA.ap()[:, 0:8192].rearrange("p (ic o) -> p ic o", ic=8)
        wait("tensor", "s", wt_ev[ND - 1][1])
        wait("tensor", "s", wt_ev[ND - 2][1])
        g1ev = {}
        for t in range(NT):
            gp = (pA, pB, pC, pD)[t % 4]
            if t >= 4:
                wait("tensor", "s", g1ev[t - 4])
            gmm = None
            for dc in range(ND):
                for oc in range(2):
                    gmm = inc("tensor", lambda e, dc=dc, oc=oc, t=t, gp=gp: e.matmul(
                        gp[:, oc * 512:(oc + 1) * 512],
                        A1[:, dc * S + t * 128: dc * S + (t + 1) * 128],
                        wt_v[:, dc, oc * 512:(oc + 1) * 512],
                        start=(dc == 0), stop=(dc == ND - 1)), "t", 1)
            wait("vector", "t", gmm)
            if t == 0:
                wait("vector", "v", cws)
            if t >= 2:
                wait("vector", "s", g1ev[t - 2])
            buf = t % 2
            scr = xt if buf == 0 else xn_t
            cv1 = inc("vector", lambda e, gp=gp, scr=scr: e.scalar_tensor_tensor(
                scr[:, 1024:2048], gp[:, :], smallA[:, 60:61], bias_rep[:, 2 * D:3 * D],
                ALU.mult, ALU.add), "v", 1)
            wait("scalar", "v", cv1)
            g1ev[t] = inc("scalar", lambda e, t=t, scr=scr: e.activation(
                accg[:, t * 1024:(t + 1) * 1024], scr[:, 1024:2048], AF.Silu), "s", 1)
        ctx.g1_ready = g1ev[NT - 1]

        if debug_taps:
            wait("gpsimd", "s", ctx.g1_ready)
            cdx = None
            for t in range(NT):
                cdx = inc("gpsimd", lambda ee, t=t: ee.dma_start(
                    taps["g1"][t * 128:(t + 1) * 128, :], accg[:, t * 1024:(t + 1) * 1024]), "dw", 16)
            for eng in ENG:
                wait(eng, "dw", cdx)

    def phase6():
        H2 = 2048
        NIB = 8
        v_tm = WARENA.ap()[:, 0:16384].rearrange("p (t d) -> p t d", t=NT)
        wqk_v = WARENA.ap()[:, 16384:17408].rearrange("p (dc f) -> p dc f", dc=8)
        at_sb = WARENA.ap()[:, 17408:17920]
        wv_v = A1.ap()[:, 0:8192].rearrange("p (dc f) -> p dc f", dc=8)
        QF, KF = 16384, 18432
        qf_ap = A1.ap()[:, QF:QF + 2048]
        kf_ap = A1.ap()[:, KF:KF + 2048]

        p5_t = sems["t"].n
        wait("vector", "dx", ctx.cols_ready)
        ct1_ = inc("vector", lambda e: e.tensor_scalar(
            tblp[0:NB, 0:1], tblp[0:NB, 0:1], REL_SCALE / SIGMA, -MU / SIGMA, ALU.mult, ALU.add), "v", 1)
        cms = inc("gpsimd", lambda e: e.memset(tblp[NB:NB + 1, 0:1], MASK_NEG), "g", 1)
        wait("tensor", "s", ctx.p4_done_s)
        bv_ev = {}
        for hf in range(2):
            wait("sync", "s", sems["s"].n)
            wait("sync", "v", sems["v"].n)
            chl = inc("sync", lambda e, hf=hf: e.dma_start(
                xt[0:NB + 1, 0:1280], oht[:, hf * 1280:(hf + 1) * 1280]), "dx", 16)
            wait("tensor", "dx", chl)
            wait("tensor", "v", ct1_)
            wait("tensor", "g", cms)
            wait("tensor", "s", sems["s"].n)
            wait("tensor", "v", sems["v"].n)
            for mc in range(10):
                c = hf * 10 + mc
                bank = c % 2
                prev = bv_ev.get(bank)
                if prev is not None:
                    wait("tensor", "v", prev)
                bmm = inc("tensor", lambda e, mc=mc, bank=bank: e.matmul(
                    pC[0:1, bank * 512: bank * 512 + 128],
                    tblp[0:NB + 1, 0:1], xt[0:NB + 1, mc * 128:(mc + 1) * 128],
                    start=True, stop=True), "t", 1)
                wait("vector", "t", bmm)
                if c < 16:
                    bv_ev[bank] = inc("vector", lambda e, c=c, bank=bank: e.tensor_copy(
                        xn_t[0:1, c * 128:(c + 1) * 128],
                        pC[0:1, bank * 512: bank * 512 + 128]), "v", 1)
                else:
                    bv_ev[bank] = inc("vector", lambda e, c=c, bank=bank: e.tensor_copy(
                        strip[0:1, (c - 16) * 128:(c - 15) * 128],
                        pC[0:1, bank * 512: bank * 512 + 128]), "v", 1)
        wait("sync", "v", sems["v"].n)
        cbd = inc("sync", lambda e: e.dma_start(bvec_dram[0:2048], xn_t[0:1, 0:2048]), "dx", 16)
        cbd = inc("sync", lambda e: e.dma_start(bvec_dram[2048:2560], strip[0:1, 0:512]), "dx", 16)
        wait("sync", "dx", cbd)
        cst = None
        for p in range(128):
            cst = inc("sync", lambda e, p=p: e.dma_start(
                strip[p:p + 1, :], bvec_dram[128 - p:128 - p + STRIP_W]), "dx", 16)
        ctx.strip_ready = cst
        if debug_taps:
            wait("gpsimd", "dx", cst)
            inc("gpsimd", lambda e: e.dma_start(taps["strip"], strip[:, :]), "dw", 16)

        wqk_r = wqk1.rearrange("(dc p) f -> p dc f", p=128)
        wv_r = wv1.rearrange("(dc p) f -> p dc f", p=128)
        wait("gpsimd", "t", p5_t)
        cqk = inc("gpsimd", lambda e: e.dma_start(wqk_v, wqk_r), "dw", 16)
        cwv = inc("gpsimd", lambda e: e.dma_start(wv_v, wv_r), "dw", 16)

        wait("tensor", "dw", cqk)
        wait("tensor", "v", sems["v"].n)
        wait("tensor", "s", sems["s"].n)
        qkev = {}
        for tb in range(4):
            qp = (pA, pB, pC, pD)[tb]
            qmm = None
            for dc in range(ND):
                qmm = inc("tensor", lambda e, dc=dc, tb=tb, qp=qp: e.matmul(
                    qp[:, 0:512], wqk_v[:, dc, :], xnT[dc][:, tb * 512:(tb + 1) * 512],
                    start=(dc == 0), stop=(dc == ND - 1)), "t", 1)
            wait("scalar", "t", qmm)
            wait("scalar", "dx", ctx.cols_ready)
            if tb >= 1:
                wait("scalar", "v", qkev[tb - 1][1])
            csl = inc("scalar", lambda e, tb=tb, qp=qp: e.activation(
                sg_t[:, 1024:1536], qp[:, 0:512], AF.Silu, bias=cols1[:, 0:1]), "s", 1)
            wait("vector", "s", csl)
            cq1 = inc("vector", lambda e, tb=tb: e.tensor_scalar(
                qf_ap[:, tb * 512:(tb + 1) * 512], sg_t[:, 1024:1536],
                cols1[:, 1:2], cols1[:, 3:4], ALU.mult, ALU.add), "v", 1)
            cq2 = inc("vector", lambda e, tb=tb: e.tensor_scalar(
                kf_ap[:, tb * 512:(tb + 1) * 512], sg_t[:, 1024:1536],
                cols1[:, 2:3], cols1[:, 4:5], ALU.mult, ALU.add), "v", 1)
            qkev[tb] = (csl, cq2)
        ctx.qk_ready = qkev[3][1]

        wait("tensor", "dw", cwv)
        wait("tensor", "v", ctx.qk_ready)
        vev = {}
        for t in range(NT):
            vp = pA if t % 2 == 0 else pB
            if t >= 2:
                wait("tensor", "s", vev[t - 2])
            vmm = None
            for dc in range(ND):
                for oc in range(2):
                    vmm = inc("tensor", lambda e, dc=dc, oc=oc, t=t, vp=vp: e.matmul(
                        vp[:, oc * 512:(oc + 1) * 512],
                        A1[:, dc * S + t * 128: dc * S + (t + 1) * 128] if False else
                        xnT[dc][:, t * 128:(t + 1) * 128],
                        wv_v[:, dc, oc * 512:(oc + 1) * 512],
                        start=(dc == 0), stop=(dc == ND - 1)), "t", 1)
            buf = t % 2
            scr = xt if buf == 0 else xn_t
            wait("vector", "t", vmm)
            if t >= 2:
                wait("vector", "s", vev[t - 2])
            cv1 = inc("vector", lambda e, vp=vp, scr=scr: e.tensor_tensor(
                scr[:, 0:1024], vp[:, :], bias_rep[:, D:2 * D], ALU.add), "v", 1)
            wait("scalar", "v", cv1)
            vev[t] = inc("scalar", lambda e, t=t, scr=scr: e.activation(
                v_tm[:, t, :], scr[:, 0:1024], AF.Silu), "s", 1)
        ctx.v_ready = vev[NT - 1]

        C_MUL = 1.0 / (S * SIGMA)
        wait("tensor", "s", ctx.v_ready)
        wait("tensor", "dx", ctx.strip_ready)
        wait("vector", "dx", ctx.strip_ready)
        at_free = {}
        comb2 = {}
        ctx.p6_prev_gelu = None
        ctx.p6_prev_store = None
        ctx.p6_yv = {}
        for ib in range(p6_ibs):
            i0 = ib * 256
            njc = 2 * ib + 2
            pending = None
            for jc in range(njc):
                abuf = jc % 2
                ap_ = pA if abuf == 0 else pB
                prevy = ctx.p6_yv.get(abuf)
                if prevy is not None:
                    wait("tensor", "v", prevy)
                amm = inc("tensor", lambda e, jc=jc, i0=i0, ap_=ap_: e.matmul(
                    ap_[:, 0:256], kf_ap[:, jc * 128:(jc + 1) * 128], qf_ap[:, i0:i0 + 256],
                    start=True, stop=True), "t", 1)
                Dg = i0 - jc * 128
                so = Dg + 384
                wait("vector", "t", amm)
                prevg = ctx.p6_prev_gelu
                if prevg is not None:
                    wait("vector", "s", prevg)
                y1 = inc("vector", lambda e, ap_=ap_, so=so: e.scalar_tensor_tensor(
                    xt[:, 0:256], ap_[:, 0:256], C_MUL, strip[:, so:so + 256],
                    ALU.mult, ALU.add), "v", 1)
                ctx.p6_yv[abuf] = y1
                wait("scalar", "v", y1)
                g1_ = inc("scalar", lambda e: e.activation(
                    xn_t[:, 0:256], xt[:, 0:256], AF.Gelu), "s", 1)
                ctx.p6_prev_gelu = g1_
                m1_ = inc("vector", lambda e: e.tensor_scalar(
                    xt[:, 256:512], xt[:, 0:256], 0.0, None, ALU.is_equal), "v", 1)
                wait("vector", "s", g1_)
                wait("vector", "v", m1_)
                n1 = inc("vector", lambda e: e.scalar_tensor_tensor(
                    xn_t[:, 256:512], xt[:, 256:512], 0.5, xn_t[:, 0:256], ALU.mult, ALU.add), "v", 1)
                d1 = inc("vector", lambda e: e.tensor_tensor(
                    xt[:, 512:768], xt[:, 0:256], xt[:, 256:512], ALU.add), "v", 1)
                wait("vector", "v", d1)
                r1 = inc("vector", lambda e: e.reciprocal(xt[:, 768:1024], xt[:, 512:768]), "v", 1)
                wait("vector", "v", r1)
                wait("vector", "v", n1)
                if at_free.get(abuf) is not None:
                    wait("vector", "t", at_free[abuf])
                ca = inc("vector", lambda e, abuf=abuf: e.tensor_tensor(
                    at_sb[:, abuf * 256:(abuf + 1) * 256], xn_t[:, 256:512], xt[:, 768:1024],
                    ALU.mult), "v", 1)
                if pending is not None:
                    pjc, pca, pabuf = pending
                    wait("tensor", "v", pca)
                    if pjc < 2 and ib > 0:
                        wait("tensor", "v", comb2[ib - 1])
                    last_amm = None
                    for isub in range(2):
                        op_ = pC if isub == 0 else pD
                        for dh in range(2):
                            last_amm = inc("tensor", lambda e, isub=isub, dh=dh, pjc=pjc, pabuf=pabuf, op_=op_, njc=njc: e.matmul(
                                op_[:, dh * 512:(dh + 1) * 512],
                                at_sb[:, pabuf * 256 + isub * 128: pabuf * 256 + isub * 128 + 128],
                                v_tm[:, pjc, dh * 512:(dh + 1) * 512],
                                start=(pjc == 0), stop=(pjc == njc - 1)), "t", 1)
                    at_free[pabuf] = last_amm
                pending = (jc, ca, abuf)
            pjc, pca, pabuf = pending
            wait("tensor", "v", pca)
            if pjc < 2 and ib > 0:
                wait("tensor", "v", comb2[ib - 1])
            last_amm = None
            for isub in range(2):
                op_ = pC if isub == 0 else pD
                for dh in range(2):
                    last_amm = inc("tensor", lambda e, isub=isub, dh=dh, pjc=pjc, pabuf=pabuf, op_=op_, njc=njc: e.matmul(
                        op_[:, dh * 512:(dh + 1) * 512],
                        at_sb[:, pabuf * 256 + isub * 128: pabuf * 256 + isub * 128 + 128],
                        v_tm[:, pjc, dh * 512:(dh + 1) * 512],
                        start=(pjc == 0), stop=(pjc == njc - 1)), "t", 1)
            at_free[pabuf] = last_amm
            cc_l = None
            for isub in range(2):
                t = 2 * ib + isub
                op_ = pC if isub == 0 else pD
                if ctx.p6_prev_store is not None:
                    wait("sync", "dx", ctx.p6_prev_store)
                cx = inc("sync", lambda e, t=t: e.dma_start(
                    xt[:, 1024:2048], x1_dram[t * 128:(t + 1) * 128, :]), "dx", 16)
                wait("vector", "t", at_free[0])
                wait("vector", "t", at_free[1])
                if ib == 0 and isub == 0:
                    wait("vector", "s", ctx.g1_ready)
                c1_ = inc("vector", lambda e, t=t, op_=op_: e.tensor_tensor(
                    sg_t[:, 0:1024], op_[:, :], accg[:, t * 1024:(t + 1) * 1024], ALU.mult), "v", 1)
                wait("vector", "dx", cx)
                wait("vector", "v", c1_)
                c2_ = inc("vector", lambda e: e.tensor_tensor(
                    xt[:, 1024:2048], xt[:, 1024:2048], sg_t[:, 0:1024], ALU.add), "v", 1)
                wait("sync", "v", c2_)
                cso = inc("sync", lambda e, t=t: e.dma_start(
                    out[t * 128:(t + 1) * 128, :], xt[:, 1024:2048]), "dx", 16)
                ctx.p6_prev_store = cso
                cc_l = c2_
            comb2[ib] = cc_l
        ctx.out_done = sems["dx"].n
        wait("sync", "dx", ctx.out_done)

    phase_list = [phase0, phase1, phase2, phase3, phase4, phase5, phase6]

    with (
        nc.semaphore("sem_dw") as h_dw,
        nc.semaphore("sem_dx") as h_dx,
        nc.semaphore("sem_t") as h_t,
        nc.semaphore("sem_v") as h_v,
        nc.semaphore("sem_s") as h_s,
        nc.semaphore("sem_g") as h_g,
    ):
        sem_handles.update({"dw": h_dw, "dx": h_dx, "t": h_t, "v": h_v, "s": h_s, "g": h_g})
        for ph in phase_list:
            ph()



        with nc.Block() as block:
            @block.sync
            def _(e):
                for f in prog["sync"]:
                    f(e)

            @block.gpsimd
            def _(e):
                for f in prog["gpsimd"]:
                    f(e)

            @block.vector
            def _(e):
                for f in prog["vector"]:
                    f(e)

            @block.scalar
            def _(e):
                for f in prog["scalar"]:
                    f(e)

            @block.tensor
            def _(e):
                for f in prog["tensor"]:
                    f(e)

    return nc


def _aux_inputs():
    ident = np.eye(128, dtype=np.float32)
    buckets = _rel_bucket_ext()
    oht = np.zeros((NB + 1, BVEC_N), dtype=np.float32)
    oht[buckets, np.arange(BVEC_N)] = 1.0
    ones = np.ones((1, 128), dtype=np.float32)
    sel = np.zeros((NE, NE * 128), dtype=ml_dtypes.bfloat16)
    for e in range(NE):
        sel[e, e * 128:(e + 1) * 128] = 1.0
    return {"aux_ident": ident, "aux_oht": oht, "aux_ones": ones, "aux_sel8": sel}


def kernel(**inputs):
    nc = build_program(debug_taps=False)
    aux = _aux_inputs()
    x_full = np.ascontiguousarray(inputs["L0_attn_x"], dtype=np.float32)
    base = {k: np.ascontiguousarray(v) for k, v in inputs.items() if k != "L0_attn_x"}
    base["L1_ffn_weight_scale"] = np.asarray(inputs["L1_ffn_weight_scale"], dtype=np.float32).reshape(1, 1)
    in_maps = []
    for c in range(8):
        m = dict(base)
        m.update(aux)
        m["L0_attn_x"] = x_full[c]
        in_maps.append(m)
    res = run_bass_kernel_spmd(nc, in_maps, core_ids=list(range(8)))
    return np.stack([res.results[c]["out"] for c in range(8)], axis=0)


# version 03
# speedup vs baseline: 1.1185x; 1.1185x over previous
import math
import numpy as np
import ml_dtypes

import concourse.bass as bass
import concourse.mybir as mybir
from concourse.bass_utils import run_bass_kernel_spmd

F32 = mybir.dt.float32
BF16 = mybir.dt.bfloat16
I32 = mybir.dt.int32
AF = mybir.ActivationFunctionType
ALU = mybir.AluOpType
AX = mybir.AxisListType.X

S = 2048
D = 1024
NT = S // 128
ND = D // 128
F = 2816
NFC = F // 128
NE = 8
DQK = 128
NB = 32

MU = math.sqrt(0.5)
SIGMA = math.sqrt(0.25 * math.pi)
REL_SCALE = DQK ** 0.5
MASK_NEG = -100000.0
RND_C = float(3 * 2.0 ** 22)

STRIP_W = 2432
STRIP_PAD = 512
BVEC_N = 2560


def _rel_bucket_ext():
    out = np.zeros(BVEC_N, dtype=np.int64)
    for i in range(BVEC_N):
        m = i - STRIP_PAD
        if m < 0:
            out[i] = NB
        elif m < 16:
            out[i] = m
        else:
            t = np.float32(np.log(np.float32(m) / np.float32(16.0)))
            t = np.float32(t / np.float32(math.log(128 / 16)))
            t = np.float32(t * np.float32(16))
            out[i] = min(16 + int(np.int32(t)), NB - 1)
    return out


def build_program(debug_taps=False, moe_ne=NE, moe_silu=True, l0_tiles=NT, l0_stage=9, HPAR=2, p6_ibs=8):
    nc = bass.Bass("TRN2", target_bir_lowering=False, detect_race_conditions=False)

    def din(name, shape, dtype=F32):
        return nc.dram_tensor(name, shape, dtype, kind="ExternalInput").ap()

    x_in = din("L0_attn_x", [S, D])
    wqkv = din("L0_attn_Wqkv", [D, 3 * D])
    wout0 = din("L0_attn_Wout", [D, D])
    bout0 = din("L0_attn_bout", [D])
    router_w = din("L0_ffn_router_w", [D, NE])
    gate_p = din("L0_ffn_gate_proj", [NE, D, F])
    up_p = din("L0_ffn_up_proj", [NE, D, F])
    down_p = din("L0_ffn_down_proj", [NE, F, D])
    wqk1 = din("L1_attn_wqk", [D, DQK])
    bqk1 = din("L1_attn_bqk", [DQK])
    gamma1 = din("L1_attn_gamma", [2, DQK])
    beta1 = din("L1_attn_beta", [2, DQK])
    wv1 = din("L1_attn_wv", [D, D])
    bv1 = din("L1_attn_bv", [D])
    relb = din("L1_attn_rel_bias", [NB])
    w_int8 = din("L1_ffn_w_int8", [D, D], I32)
    wscale = din("L1_ffn_weight_scale", [1, 1])
    bias1 = din("L1_ffn_bias", [D])
    ident = din("aux_ident", [128, 128])
    oht = din("aux_oht", [NB + 1, BVEC_N])
    ones_row = din("aux_ones", [1, 128])
    sel8 = din("aux_sel8", [NE, NE * 128], BF16)

    out = nc.dram_tensor("out", [S, D], F32, kind="ExternalOutput").ap()

    x1_dram = nc.dram_tensor("x1_scratch", [S, D], F32, kind="Internal").ap()
    bvec_dram = nc.dram_tensor("bvec_scratch", [BVEC_N], F32, kind="Internal").ap()

    taps = {}
    if debug_taps:
        taps["xnT"] = nc.dram_tensor("tap_xnT", [ND, 128, S], F32, kind="ExternalOutput").ap()
        taps["ewT"] = nc.dram_tensor("tap_ewT", [NE, S], F32, kind="ExternalOutput").ap()
        taps["g"] = nc.dram_tensor("tap_g", [S, D], F32, kind="ExternalOutput").ap()
        taps["x1"] = nc.dram_tensor("tap_x1", [S, D], F32, kind="ExternalOutput").ap()
        taps["g1"] = nc.dram_tensor("tap_g1", [S, D], F32, kind="ExternalOutput").ap()
        taps["strip"] = nc.dram_tensor("tap_strip", [128, STRIP_W], F32, kind="ExternalOutput").ap()
        taps["wrep"] = nc.dram_tensor("tap_wrep", [128, S], F32, kind="ExternalOutput").ap()
        taps["acts"] = nc.dram_tensor("tap_acts", [11, 128, S], F32, kind="ExternalOutput").ap()

    sb = nc.alloc_sbuf_tensor
    xnT = [sb(f"xnT{c}", [128, S], BF16) for c in range(ND)]
    accg = sb("accg", [128, NT * D], BF16)
    A1 = sb("A1", [128, 11 * S], BF16)
    WARENA = sb("WARENA", [128, 23552], BF16)
    w_rep = sb("w_rep", [128, S], BF16)
    ewT = sb("ewT", [NE, S], BF16)
    xt = sb("xt", [128, 2 * D], F32)
    xn_t = sb("xn_t", [128, 2 * D], F32)
    sg_t = sb("sg_t", [128, 2 * 1024], BF16)
    up_t = sb("up_t", [128, 2 * 1088], BF16)
    stat = sb("stat", [128, 32], F32)
    scores = sb("scores", [128, NT * 8], F32)
    strip = sb("strip", [128, STRIP_W], F32)
    ident_sb = sb("ident_sb", [128, 128], F32)
    ones_sb = sb("ones_sb", [1, 128], F32)
    tblp = sb("tblp", [NB + 1, 1], F32)
    bvec_sb = sb("bvec_sb", [128, 20], F32)
    smallA = sb("smallA", [128, 64], F32)
    rw_hl = sb("rw_hl", [128, 8 * 16], BF16)
    bias_rep = sb("bias_rep", [128, 3 * D], BF16)
    rw_stage = sb("rw_stage", [128, 64], F32)
    cns = sb("cns", [128, 4], F32)
    ones_bf = sb("ones_bf", [1, 128], BF16)
    sel8_sb = sb("sel8_sb", [NE, NE * 128], BF16)
    kvq = sb("kvq", [128, 544], BF16)
    inner_t = xn_t
    innerT_t = sg_t
    cols1 = sb("cols1", [128, 8], F32)
    g1b = None

    pp = nc.alloc_psum_tensor
    pA = pp("pA", [128, 1024], F32)
    pB = pp("pB", [128, 1024], F32)
    pC = pp("pC", [128, 1024], F32)
    pD = pp("pD", [128, 1024], F32)

    ENG = ["sync", "gpsimd", "vector", "scalar", "tensor"]
    prog = {e: [] for e in ENG}
    sem_handles = {}
    counts = {}

    class Ctx:
        pass

    ctx = Ctx()

    def emit(eng, fn):
        prog[eng].append(fn)

    class SemTracker:
        def __init__(self, name):
            self.name = name
            self.n = 0

        def tick(self, amt):
            self.n += amt
            return self.n

    sems = {k: SemTracker(k) for k in ["dw", "dx", "t", "v", "s", "g"]}

    def inc(eng, ins_fn, sem, amt):
        c = sems[sem].tick(amt)

        def f(e, ins_fn=ins_fn, sem=sem, amt=amt):
            ins_fn(e).then_inc(sem_handles[sem], amt)
        emit(eng, f)
        return c

    def raw(eng, ins_fn):
        emit(eng, ins_fn)

    def wait(eng, sem, val):
        if val <= 0:
            return

        def f(e, sem=sem, val=val):
            e.wait_ge(sem_handles[sem], val)
        emit(eng, f)

    def phase0():
        cg = inc("gpsimd", lambda e: e.memset(cns[:, 0:1], 1e-5), "g", 1)
        cg = inc("gpsimd", lambda e: e.memset(cns[:, 1:2], 1e-6), "g", 1)
        cg = inc("gpsimd", lambda e: e.memset(ones_bf[0:1, :], 1.0), "g", 1)
        cg = inc("gpsimd", lambda e: e.memset(kvq[:, 543:544], 1.0), "g", 1)
        ctx.cns_ready = cg
        c = inc("sync", lambda e: e.dma_start(ident_sb[:, :], ident), "dx", 16)
        ctx.sel8_ready = inc("sync", lambda e: e.dma_start(sel8_sb[:, :], sel8), "dx", 16)
        inc("sync", lambda e: e.dma_start(ones_sb[:, :], ones_row), "dx", 16)
        rw_r = router_w.rearrange("(dc p) e -> p dc e", p=128)
        c2 = inc("sync", lambda e: e.dma_start(rw_stage[:, 0:64].rearrange("p (dc e) -> p dc e", dc=8), rw_r), "dx", 16)
        wait("scalar", "dx", c2)
        for dc in range(ND):
            raw("scalar", lambda e, dc=dc: e.copy(rw_hl[:, dc * 16: dc * 16 + 8], rw_stage[:, dc * 8: dc * 8 + 8]))
        cs = inc("scalar", lambda e: e.copy(smallA[:, 63:64], ident_sb[:, 0:1]), "s", 1)
        wait("vector", "s", cs)
        for dc in range(ND):
            fn = lambda e, dc=dc: e.tensor_tensor(
                rw_hl[:, dc * 16 + 8: dc * 16 + 16], rw_stage[:, dc * 8: dc * 8 + 8],
                rw_hl[:, dc * 16: dc * 16 + 8], ALU.subtract)
            if dc < ND - 1:
                raw("vector", fn)
            else:
                ctx.rw_ready = inc("vector", fn, "v", 1)

        row_buf = smallA
        c3 = inc("sync", lambda e: e.dma_start(xt[0:1, 0:D], bout0[None, :]), "dx", 16)
        c3 = inc("sync", lambda e: e.dma_start(xt[0:1, D:2 * D], bv1[None, :]), "dx", 16)
        c3 = inc("sync", lambda e: e.dma_start(xn_t[0:1, 0:D], bias1[None, :]), "dx", 16)
        wait("tensor", "dx", c3)
        tmm = None
        for k in range(3):
            pb = (pA, pB, pC)[k]
            src_row = (xt, xt, xn_t)[k]
            off = (0, D, 0)[k]
            for half in range(2):
                tmm = inc("tensor", lambda e, half=half, pb=pb, src_row=src_row, off=off: e.matmul(
                    pb[:, half * 512:(half + 1) * 512],
                    ones_sb[0:1, :], src_row[0:1, off + half * 512: off + (half + 1) * 512],
                    start=True, stop=True), "t", 1)
        ctx.bias_mm_done = tmm
        wait("vector", "t", tmm)
        cv = None
        for k in range(3):
            cv = inc("vector", lambda e, k=k: e.tensor_copy(
                bias_rep[:, k * D:(k + 1) * D], (pA if k == 0 else (pB if k == 1 else pC))[:, 0:1024]), "v", 1)
        ctx.bias_rep_ready = cv
        ctx.p0_psum_free = cv

        c4 = inc("sync", lambda e: e.dma_start(cols1[:, 0:1], bqk1[:, None]), "dx", 16)
        c4 = inc("sync", lambda e: e.dma_start(cols1[:, 1:2], gamma1[0, :][:, None]), "dx", 16)
        c4 = inc("sync", lambda e: e.dma_start(cols1[:, 2:3], gamma1[1, :][:, None]), "dx", 16)
        c4 = inc("sync", lambda e: e.dma_start(cols1[:, 3:4], beta1[0, :][:, None]), "dx", 16)
        c4 = inc("sync", lambda e: e.dma_start(cols1[:, 4:5], beta1[1, :][:, None]), "dx", 16)
        c4 = inc("sync", lambda e: e.dma_start(cols1[0:1, 5:6], wscale), "dx", 16)
        c4 = inc("sync", lambda e: e.dma_start(tblp[0:NB, 0:1], relb[:, None]), "dx", 16)
        ctx.cols_ready = c4

    def phase1():
        def vtick(fn):
            return inc("vector", fn, "v", 1)

        def vwait(c):
            wait("vector", "v", c)

        lo_arena = accg
        wait("scalar", "g", ctx.cns_ready)
        ctx.p1_evac = {}
        ctx.p1_tile_done = {}
        wait("sync", "t", ctx.bias_mm_done)
        mm_last = 0
        for t in range(NT):
            buf = t % 2
            if t >= 2:
                wait("sync", "v", ctx.p1_tile_done[t - 2][0])
                wait("sync", "s", ctx.p1_tile_done[t - 2][1])
            cx = inc("sync", lambda e, t=t, buf=buf: e.dma_start(
                xt[:, buf * D:(buf + 1) * D], x_in[t * 128:(t + 1) * 128, :]), "dx", 16)
            wait("vector", "dx", cx)
            c0 = vtick(lambda e, buf=buf: e.reduce_sum(stat[:, 0:1], xt[:, buf * D:(buf + 1) * D], axis=AX))
            wait("scalar", "dx", cx)
            csq = inc("scalar", lambda e, buf=buf: e.activation(
                xn_t[:, buf * D:(buf + 1) * D], xt[:, buf * D:(buf + 1) * D], AF.Square), "s", 1)
            wait("vector", "s", csq)
            c1 = vtick(lambda e, buf=buf: e.reduce_sum(stat[:, 1:2], xn_t[:, buf * D:(buf + 1) * D], axis=AX))
            vwait(c0)
            c2 = vtick(lambda e: e.tensor_scalar(stat[:, 2:3], stat[:, 0:1], 1.0 / D, None, ALU.mult))
            vwait(c2)
            c3 = vtick(lambda e: e.tensor_tensor(stat[:, 3:4], stat[:, 2:3], stat[:, 2:3], ALU.mult))
            vwait(c3)
            vwait(c1)
            c4 = vtick(lambda e: e.scalar_tensor_tensor(
                stat[:, 4:5], stat[:, 1:2], 1.0 / D, stat[:, 3:4], ALU.mult, ALU.subtract))
            wait("scalar", "v", c4)
            cs1 = inc("scalar", lambda e: e.activation(stat[:, 5:6], stat[:, 4:5], AF.Sqrt, bias=cns[:, 0:1]), "s", 1)
            wait("vector", "s", cs1)
            c5 = vtick(lambda e: e.reciprocal(stat[:, 6:7], stat[:, 5:6]))
            vwait(c5)
            c6 = vtick(lambda e: e.scalar_tensor_tensor(
                stat[:, 7:8], stat[:, 2:3], -1.0, stat[:, 6:7], ALU.mult, ALU.mult))
            vwait(c6)
            cs2 = vtick(lambda e, buf=buf: e.tensor_scalar(
                xn_t[:, buf * D:(buf + 1) * D], xt[:, buf * D:(buf + 1) * D],
                stat[:, 6:7], stat[:, 7:8], ALU.mult, ALU.add))
            wait("tensor", "v", cs2)
            if t == 0:
                wait("tensor", "v", ctx.bias_rep_ready)
            if t >= 2:
                wait("tensor", "v", ctx.p1_evac[t - 2])
            for dc in range(ND):
                pbank = (pA if (dc % 2 == 0) else pB) if t % 2 == 0 else (pC if (dc % 2 == 0) else pD)
                off = (dc // 2) * 128
                mm_last = inc("tensor", lambda e, dc=dc, buf=buf, pbank=pbank, off=off: e.transpose(
                    pbank[:, off:off + 128], xn_t[:, buf * D + dc * 128: buf * D + (dc + 1) * 128], ident_sb[:, :]),
                    "t", 1)
            wait("scalar", "t", mm_last)
            sev = None
            for dc in range(ND):
                pbank = (pA if (dc % 2 == 0) else pB) if t % 2 == 0 else (pC if (dc % 2 == 0) else pD)
                off = (dc // 2) * 128
                sev = inc("scalar", lambda e, dc=dc, t=t, pbank=pbank, off=off: e.copy(
                    xnT[dc][:, t * 128:(t + 1) * 128], pbank[:, off:off + 128]), "s", 1)
            wait("vector", "s", sev)
            vev = None
            for dc in range(ND):
                pbank = (pA if (dc % 2 == 0) else pB) if t % 2 == 0 else (pC if (dc % 2 == 0) else pD)
                off = (dc // 2) * 128
                vev = vtick(lambda e, dc=dc, t=t, pbank=pbank, off=off: e.tensor_tensor(
                    lo_arena[:, dc * S + t * 128: dc * S + (t + 1) * 128],
                    pbank[:, off:off + 128], xnT[dc][:, t * 128:(t + 1) * 128], ALU.subtract))
            ctx.p1_evac[t] = vev
            ctx.p1_tile_done[t] = (cs2, csq)

        ctx.xnT_done_v = sems["v"].n
        ctx.xnT_done_s = sems["s"].n

        wait("tensor", "v", ctx.p1_evac[NT - 1])
        wait("tensor", "v", ctx.rw_ready)
        ctx.scores_evac = {}
        for t in range(NT):
            pbank = pA if t % 2 == 0 else pB
            if t >= 2:
                wait("tensor", "v", ctx.scores_evac[t - 2])
            smm = None
            for dc in range(ND):
                smm = inc("tensor", lambda e, t=t, dc=dc, pbank=pbank: e.matmul(
                    pbank[:, 0:8], xnT[dc][:, t * 128:(t + 1) * 128], rw_hl[:, dc * 16: dc * 16 + 8],
                    start=(dc == 0), stop=False), "t", 1)
            for dc in range(ND):
                smm = inc("tensor", lambda e, t=t, dc=dc, pbank=pbank: e.matmul(
                    pbank[:, 0:8], xnT[dc][:, t * 128:(t + 1) * 128], rw_hl[:, dc * 16 + 8: dc * 16 + 16],
                    start=False, stop=False), "t", 1)
            for dc in range(ND):
                smm = inc("tensor", lambda e, t=t, dc=dc, pbank=pbank: e.matmul(
                    pbank[:, 0:8], lo_arena[:, dc * S + t * 128: dc * S + (t + 1) * 128],
                    rw_hl[:, dc * 16: dc * 16 + 8],
                    start=False, stop=(dc == ND - 1)), "t", 1)
            wait("vector", "t", smm)
            ctx.scores_evac[t] = vtick(lambda e, t=t, pbank=pbank: e.tensor_copy(
                scores[:, t * 8:(t + 1) * 8], pbank[:, 0:8]))

        ctx.ew_evac = {}
        for t in range(NT):
            sl = scores[:, t * 8:(t + 1) * 8]
            vwait(ctx.scores_evac[t])
            c0 = vtick(lambda e, sl=sl: e.reduce_max(stat[:, 8:9], sl, axis=AX))
            vwait(c0)
            c1 = vtick(lambda e, sl=sl: e.tensor_scalar(smallA[:, 0:8], sl, stat[:, 8:9], None, ALU.is_ge))
            vwait(c1)
            c2 = vtick(lambda e, sl=sl: e.scalar_tensor_tensor(
                smallA[:, 8:16], smallA[:, 0:8], -1e30, sl, ALU.mult, ALU.add))
            vwait(c2)
            c3 = vtick(lambda e: e.reduce_max(stat[:, 9:10], smallA[:, 8:16], axis=AX))
            vwait(c3)
            c4 = vtick(lambda e: e.tensor_scalar(smallA[:, 8:16], smallA[:, 8:16], stat[:, 9:10], None, ALU.is_ge))
            c5 = vtick(lambda e: e.tensor_tensor(stat[:, 10:11], stat[:, 8:9], stat[:, 9:10], ALU.subtract))
            wait("scalar", "v", c5)
            cs = inc("scalar", lambda e: e.activation(stat[:, 11:12], stat[:, 10:11], AF.Sigmoid), "s", 1)
            wait("vector", "s", cs)
            c6 = vtick(lambda e: e.tensor_scalar(stat[:, 12:13], stat[:, 11:12], -1.0, 1.0, ALU.mult, ALU.add))
            c7 = vtick(lambda e: e.tensor_scalar(smallA[:, 16:24], smallA[:, 0:8], stat[:, 11:12], None, ALU.mult))
            vwait(c6)
            vwait(c7)
            vwait(c4)
            c8 = vtick(lambda e: e.scalar_tensor_tensor(
                smallA[:, 16:24], smallA[:, 8:16], stat[:, 12:13], smallA[:, 16:24], ALU.mult, ALU.add))
            wait("tensor", "v", c8)
            if t >= 2:
                wait("tensor", "v", ctx.ew_evac[t - 2])
            pbank = pC if t % 2 == 0 else pD
            temm = inc("tensor", lambda e, pbank=pbank: e.transpose(
                pbank[0:8, 0:128], smallA[:, 16:24], ident_sb[:, :]), "t", 1)
            wait("vector", "t", temm)
            ctx.ew_evac[t] = vtick(lambda e, t=t, pbank=pbank: e.tensor_copy(
                ewT[0:8, t * 128:(t + 1) * 128], pbank[0:8, 0:128]))
        ctx.routing_done = sems["v"].n

        if debug_taps:
            wait("gpsimd", "v", ctx.routing_done)
            cdx = None
            for dc in range(ND):
                cdx = inc("gpsimd", lambda e, dc=dc: e.dma_start(taps["xnT"][dc], xnT[dc][:, :]), "dw", 16)
            cdx = inc("gpsimd", lambda e: e.dma_start(taps["ewT"], ewT[:, :]), "dw", 16)
            ctx.tap1_done = cdx
            for eng in ENG:
                wait(eng, "dw", cdx)

    def phase2():
        NSP = 2
        FCS = 11
        NPAIR = 11
        gp_r = gate_p.rearrange("e (dc p) f -> e p dc f", p=128)
        up_r = up_p.rearrange("e (dc p) f -> e p dc f", p=128)
        dn_r = down_p.rearrange("e (fc p) d -> e p fc d", p=128)
        WGv = WARENA.ap()[:, 0:6144].rearrange("p (sl dc f) -> p sl dc f", sl=3, dc=8)
        WUv = WARENA.ap()[:, 6144:12288].rearrange("p (sl dc f) -> p sl dc f", sl=3, dc=8)
        WDa = WARENA.ap()[:, 12288:23552]

        queue = [(e, pair) for e in range(NE) for pair in range(NPAIR)]
        slot_loaded = {}
        slot_consumed = {}
        slot_of = {}
        wd_loaded = {}
        wd_consumed = {}
        pf_ptr = [0]

        def issue_next_slot():
            if pf_ptr[0] >= len(queue):
                return
            idx = pf_ptr[0]
            e, pair = queue[idx]
            if idx >= 3:
                wait("gpsimd", "t", slot_consumed[queue[idx - 3]])
            sl = idx % 3
            slot_of[(e, pair)] = sl
            f0 = pair * 256
            inc("gpsimd", lambda ee, e=e, sl=sl, f0=f0: ee.dma_start(
                WGv[:, sl], gp_r[e][:, :, f0:f0 + 256]), "dw", 16)
            slot_loaded[(e, pair)] = inc("gpsimd", lambda ee, e=e, sl=sl, f0=f0: ee.dma_start(
                WUv[:, sl], up_r[e][:, :, f0:f0 + 256]), "dw", 16)
            pf_ptr[0] = idx + 1

        def issue_wd(e, sp):
            key = (e, sp - 1) if sp >= 1 else (e - 1, 1)
            if key in wd_consumed:
                wait("gpsimd", "t", wd_consumed[key])
            wd_loaded[(e, sp)] = inc("gpsimd", lambda ee, e=e, sp=sp: ee.dma_start(
                WDa.rearrange("p (fc d) -> p fc d", fc=FCS), dn_r[e][:, sp * FCS:(sp + 1) * FCS, :]), "dw", 16)

        wait("gpsimd", "v", ctx.routing_done)
        for _ in range(3):
            issue_next_slot()
        issue_wd(0, 0)

        psum_units = [(pA, pB), (pC, pD)]
        unit_free = {0: [], 1: []}
        ctx.sg_guard = {}
        delay_q = []
        acts_ready = {}
        mac_last = None

        for e in range(moe_ne):
            wait("tensor", "v", ctx.routing_done)
            for g in unit_free[0]:
                wait("tensor", g[0], g[1])
            if e == 0:
                wait("tensor", "dx", ctx.sel8_ready)
            wr_mm = None
            for q in range(4):
                pb = (pA, pB)[q // 2]
                wr_mm = inc("tensor", lambda ee, e=e, q=q, pb=pb: ee.matmul(
                    pb[:, (q % 2) * 512:(q % 2 + 1) * 512], sel8_sb[0:NE, e * 128:(e + 1) * 128],
                    ewT[0:NE, q * 512:(q + 1) * 512], start=True, stop=True), "t", 1)
            wait("vector", "t", wr_mm)
            cwr = None
            for q in range(2):
                pb = (pA, pB)[q]
                cwr = inc("vector", lambda ee, q=q, pb=pb: ee.tensor_copy(
                    w_rep[:, q * 1024:(q + 1) * 1024], pb[:, :]), "v", 1)
            unit_free = {0: [("v", cwr)], 1: unit_free[1]}

            for sp in range(NSP):
                for fcl in range(FCS):
                    fc = sp * FCS + fcl
                    pair, j = fc // 2, fc % 2
                    for th in range(2):
                        u = (fc * 2 + th) % 2
                        gate_ps, up_ps = psum_units[u]
                        wait("tensor", "dw", slot_loaded[(e, pair)])
                        for gg in unit_free[u]:
                            wait("tensor", gg[0], gg[1])
                        gmm = umm = None
                        sl = slot_of[(e, pair)]
                        for dc in range(ND):
                            for tw in range(2):
                                gmm = inc("tensor", lambda ee, dc=dc, tw=tw, th=th, sl=sl, j=j, gate_ps=gate_ps: ee.matmul(
                                    gate_ps[:, tw * 512:(tw + 1) * 512],
                                    WGv[:, sl, dc, j * 128:(j + 1) * 128],
                                    xnT[dc][:, th * 1024 + tw * 512: th * 1024 + (tw + 1) * 512],
                                    start=(dc == 0), stop=(dc == ND - 1)), "t", 1)
                        for dc in range(ND):
                            for tw in range(2):
                                umm = inc("tensor", lambda ee, dc=dc, tw=tw, th=th, sl=sl, j=j, up_ps=up_ps: ee.matmul(
                                    up_ps[:, tw * 512:(tw + 1) * 512],
                                    WUv[:, sl, dc, j * 128:(j + 1) * 128],
                                    xnT[dc][:, th * 1024 + tw * 512: th * 1024 + (tw + 1) * 512],
                                    start=(dc == 0), stop=(dc == ND - 1)), "t", 1)
                        slot_consumed[(e, pair)] = sems["t"].n
                        b2 = (fc * 2 + th) % 2
                        wait("scalar", "t", gmm)
                        prev = ctx.sg_guard.get(b2)
                        if prev is not None:
                            wait("scalar", "v", prev)
                        csg = inc("scalar", lambda ee, b2=b2, gate_ps=gate_ps: ee.activation(
                            sg_t[:, b2 * 1024:(b2 + 1) * 1024], gate_ps[:, :], AF.Silu), "s", 1)
                        wait("vector", "t", umm)
                        wait("vector", "s", csg)
                        ct1 = inc("vector", lambda ee, b2=b2, up_ps=up_ps: ee.tensor_tensor(
                            up_t[:, b2 * 1088: b2 * 1088 + 1024], up_ps[:, :],
                            sg_t[:, b2 * 1024:(b2 + 1) * 1024], ALU.mult), "v", 1)
                        ctx.sg_guard[b2] = ct1
                        unit_free[u] = [("s", csg), ("v", ct1)]
                        def acts_op(ee, b2=b2, fcl=fcl, th=th):
                            return ee.tensor_tensor(
                                A1[:, fcl * S + th * 1024: fcl * S + (th + 1) * 1024],
                                up_t[:, b2 * 1088: b2 * 1088 + 1024],
                                w_rep[:, th * 1024:(th + 1) * 1024], ALU.mult)
                        delay_q.append((ct1, acts_op, fcl))
                        if len(delay_q) >= 2:
                            cdep, op, fcl_d = delay_q.pop(0)
                            wait("vector", "v", cdep)
                            acts_ready[fcl_d] = inc("vector", op, "v", 1)
                    if j == 1:
                        issue_next_slot()
                while delay_q:
                    cdep, op, fcl_d = delay_q.pop(0)
                    wait("vector", "v", cdep)
                    acts_ready[fcl_d] = inc("vector", op, "v", 1)

                wait("tensor", "dw", wd_loaded[(e, sp)])
                down_free = {}
                for t in range(NT):
                    dp = (pA, pB)[t % 2]
                    if t < 2:
                        for uu in range(2):
                            for gg in unit_free[uu]:
                                wait("tensor", gg[0], gg[1])
                    else:
                        wait("tensor", "v", down_free[t % 2])
                    dmm = None
                    for fcl in range(FCS):
                        if t == 0:
                            wait("tensor", "v", acts_ready[fcl])
                        for h in range(2):
                            dmm = inc("tensor", lambda ee, fcl=fcl, h=h, t=t, dp=dp: ee.matmul(
                                dp[:, h * 512:(h + 1) * 512],
                                A1[:, fcl * S + t * 128: fcl * S + (t + 1) * 128],
                                WDa[:, fcl * 1024 + h * 512: fcl * 1024 + (h + 1) * 512],
                                start=(fcl == 0), stop=(fcl == FCS - 1)), "t", 1)
                    wait("vector", "t", dmm)
                    if e == 0 and sp == 0:
                        cmac = inc("vector", lambda ee, t=t, dp=dp: ee.tensor_copy(
                            accg[:, t * 1024:(t + 1) * 1024], dp[:, :]), "v", 1)
                    else:
                        cmac = inc("vector", lambda ee, t=t, dp=dp: ee.tensor_tensor(
                            accg[:, t * 1024:(t + 1) * 1024], dp[:, :],
                            accg[:, t * 1024:(t + 1) * 1024], ALU.add), "v", 1)
                    down_free[t % 2] = cmac
                    mac_last = cmac
                wd_consumed[(e, sp)] = sems["t"].n
                unit_free = {0: [("v", mac_last)], 1: []}
                if sp == 0:
                    issue_wd(e, 1)
                elif e + 1 < NE:
                    issue_wd(e + 1, 0)
        ctx.moe_mac_done = mac_last

        cg = None
        wait("scalar", "v", ctx.moe_mac_done)
        for t in range(NT):
            cg = inc("scalar", lambda ee, t=t: ee.activation(
                accg[:, t * 1024:(t + 1) * 1024], accg[:, t * 1024:(t + 1) * 1024],
                AF.Silu if moe_silu else AF.Copy), "s", 1)
        ctx.g_ready = cg

        if debug_taps:
            wait("gpsimd", "s", cg)
            cdx = None
            for t in range(NT):
                cdx = inc("gpsimd", lambda ee, t=t: ee.dma_start(
                    taps["g"][t * 128:(t + 1) * 128, :], accg[:, t * 1024:(t + 1) * 1024]), "dw", 16)
            cdx = inc("gpsimd", lambda ee: ee.dma_start(taps["wrep"], w_rep[:, :]), "dw", 16)
            for fcl in range(FCS):
                cdx = inc("gpsimd", lambda ee, fcl=fcl: ee.dma_start(
                    taps["acts"][fcl], A1[:, fcl * S:(fcl + 1) * S]), "dw", 16)
            for eng in ENG:
                wait(eng, "dw", cdx)

    def phase3():
        H = 16
        DH = 64
        wk_ap = WARENA.ap()[:, 0:8192].rearrange("p (dc f) -> p dc f", dc=8)
        wv_ap = WARENA.ap()[:, 8192:16384].rearrange("p (dc f) -> p dc f", dc=8)
        wq_ap = WARENA.ap()[:, 0:8192].rearrange("p (dc f) -> p dc f", dc=8)
        wout_ap = WARENA.ap()[:, 8192:16384].rearrange("p (dc f) -> p dc f", dc=8)
        wqkv_r = wqkv.rearrange("(dc p) f -> p dc f", p=128)
        wout_r = wout0.rearrange("(dc p) f -> p dc f", p=128)

        moe_t_done = sems["t"].n
        moe_v_done = sems["v"].n
        STG = 16384
        KV2 = 17408
        KS2 = KV2 + 1024
        ONES1 = KS2 + 16
        wait("gpsimd", "t", moe_t_done)
        cmz = inc("gpsimd", lambda e: e.memset(A1[:, KV2:ONES1], 0.0), "g", 1)
        cmz = inc("gpsimd", lambda e: e.memset(A1[:, ONES1:ONES1 + 1], 1.0), "g", 1)
        ctx.kv2_zeroed = cmz
        ck = inc("gpsimd", lambda e: e.dma_start(wk_ap, wqkv_r[:, :, D:2 * D]), "dw", 16)
        cv_ = inc("gpsimd", lambda e: e.dma_start(wv_ap, wqkv_r[:, :, 2 * D:3 * D]), "dw", 16)
        ctx.wk_ready, ctx.wv_ready = ck, cv_

        wait("tensor", "dw", cv_)
        wait("tensor", "g", ctx.kv2_zeroed)
        wait("tensor", "v", moe_v_done)
        wait("tensor", "s", ctx.g_ready)
        kel = {}
        vcp = {}
        ksmac = {}
        for t in range(NT):
            kp = pA if t % 2 == 0 else pB
            if t >= 2:
                wait("tensor", "v", kel[t - 2][1])
            kmm = None
            for dc in range(ND):
                for tw in range(2):
                    kmm = inc("tensor", lambda e, dc=dc, tw=tw, t=t, kp=kp: e.matmul(
                        kp[:, tw * 512:(tw + 1) * 512], xnT[dc][:, t * 128:(t + 1) * 128],
                        wk_ap[:, dc, tw * 512:(tw + 1) * 512],
                        start=(dc == 0), stop=(dc == ND - 1)), "t", 1)
            vmm = {}
            for vg in range(2):
                prev = vcp.get((t * 2 + vg) - 2)
                if prev is not None:
                    wait("tensor", "s", prev)
                if vg == 1 and t >= 1:
                    wait("tensor", "v", ksmac[t - 1])
                for dc in range(ND):
                    vmm[vg] = inc("tensor", lambda e, dc=dc, vg=vg, t=t: e.matmul(
                        pC[:, vg * 512:(vg + 1) * 512], xnT[dc][:, t * 128:(t + 1) * 128],
                        wv_ap[:, dc, vg * 512:(vg + 1) * 512],
                        start=(dc == 0), stop=(dc == ND - 1)), "t", 1)
            b = t % 2
            scr = xn_t if b == 0 else xt
            wait("vector", "t", kmm)
            c1 = inc("vector", lambda e, kp=kp, scr=scr: e.tensor_scalar(
                scr[:, 0:1024], kp[:, :], 0.0, None, ALU.min), "v", 1)
            wait("scalar", "v", c1)
            c2 = inc("scalar", lambda e, scr=scr: e.activation(
                scr[:, 1024:2048], scr[:, 0:1024], AF.Exp), "s", 1)
            wait("vector", "s", c2)
            c3 = inc("vector", lambda e, kp=kp, scr=scr, b=b: e.scalar_tensor_tensor(
                sg_t[:, b * 1024:(b + 1) * 1024], kp[:, :], 0.0, scr[:, 1024:2048],
                ALU.max, ALU.add), "v", 1)
            kel[t] = (c2, c3)
            for vg in range(2):
                wait("scalar", "t", vmm[vg])
                vcp[t * 2 + vg] = inc("scalar", lambda e, vg=vg, b=b: e.copy(
                    up_t[:, b * 1088 + vg * 512: b * 1088 + (vg + 1) * 512],
                    pC[:, vg * 512:(vg + 1) * 512]), "s", 1)
            wait("tensor", "v", c3)
            wait("tensor", "s", vcp[t * 2 + 1])
            for h in range(H):
                inc("tensor", lambda e, h=h, t=t, b=b: e.matmul(
                    pD[0:64, h * 64:(h + 1) * 64],
                    sg_t[:, b * 1024 + h * 64: b * 1024 + (h + 1) * 64],
                    up_t[:, b * 1088 + h * 64: b * 1088 + h * 64 + 64],
                    start=(t == 0 and h % 8 == 0), stop=(t == NT - 1 and h % 8 == 7)), "t", 1)
            ksm = None
            for c in range(ND):
                ksm = inc("tensor", lambda e, c=c, t=t, b=b: e.matmul(
                    pC[:, 512 + c:513 + c],
                    sg_t[:, b * 1024 + c * 128: b * 1024 + (c + 1) * 128],
                    A1[:, ONES1:ONES1 + 1],
                    start=(c == 0), stop=(c == ND - 1)), "t", 1)
            wait("vector", "t", ksm)
            if t == 0:
                ksmac[t] = inc("vector", lambda e: e.tensor_copy(
                    smallA[:, 48:56], pC[:, 512:520]), "v", 1)
            else:
                ksmac[t] = inc("vector", lambda e: e.tensor_tensor(
                    smallA[:, 48:56], pC[:, 512:520], smallA[:, 48:56], ALU.add), "v", 1)
        wait("vector", "t", sems["t"].n)
        ckv = inc("vector", lambda e: e.tensor_copy(A1[0:64, STG:STG + 512], pD[0:64, 0:512]), "v", 1)
        ckv = inc("vector", lambda e: e.tensor_copy(A1[0:64, STG + 512:STG + 1024], pD[0:64, 512:1024]), "v", 1)
        for h in range(H):
            ckv = inc("vector", lambda e, h=h: e.tensor_copy(
                A1[64 * (h % 2):64 * (h % 2) + 64, KS2 + h:KS2 + h + 1],
                smallA[64 * (h % 2):64 * (h % 2) + 64, 48 + h // 2:49 + h // 2]), "v", 1)
        wait("gpsimd", "v", ckv)
        cdma = None
        for h in range(H):
            cdma = inc("gpsimd", lambda e, h=h: e.dma_start(
                A1[64 * (h % 2):64 * (h % 2) + 64,
                   KV2 + (h // 2) * 128 + 64 * (h % 2): KV2 + (h // 2) * 128 + 64 * (h % 2) + 64],
                A1[0:64, STG + h * 64: STG + (h + 1) * 64]), "dw", 16)
        ctx.kv_ready_dw = cdma
        ctx.kv_ready = ckv

        cq = inc("gpsimd", lambda e: e.dma_start(wq_ap, wqkv_r[:, :, 0:D]), "dw", 16)
        co = inc("gpsimd", lambda e: e.dma_start(wout_ap, wout_r), "dw", 16)
        ctx.wq_ready, ctx.wout_ready = cq, co
        wait("tensor", "dw", cq)
        wait("tensor", "v", ctx.kv_ready)
        qel = {}
        for qc in range(ND):
            for th2 in range(2):
                step = qc * 2 + th2
                qp = pA if step % 2 == 0 else pB
                if step >= 2:
                    wait("tensor", "v", qel[step - 2])
                qmm = None
                for dc in range(ND):
                    for tw in range(2):
                        qmm = inc("tensor", lambda e, dc=dc, tw=tw, th2=th2, qc=qc, qp=qp: e.matmul(
                            qp[:, tw * 512:(tw + 1) * 512],
                            wq_ap[:, dc, qc * 128:(qc + 1) * 128],
                            xnT[dc][:, th2 * 1024 + tw * 512: th2 * 1024 + (tw + 1) * 512],
                            start=(dc == 0), stop=(dc == ND - 1)), "t", 1)
                b = step % 2
                scr = xn_t if b == 0 else xt
                wait("vector", "t", qmm)
                c1 = inc("vector", lambda e, qp=qp, scr=scr: e.tensor_scalar(
                    scr[:, 0:1024], qp[:, :], 0.0, None, ALU.min), "v", 1)
                wait("scalar", "v", c1)
                c2 = inc("scalar", lambda e, scr=scr: e.activation(
                    scr[:, 1024:2048], scr[:, 0:1024], AF.Exp), "s", 1)
                wait("vector", "s", c2)
                c3 = inc("vector", lambda e, qp=qp, scr=scr, qc=qc, th2=th2: e.scalar_tensor_tensor(
                    A1[:, qc * S + th2 * 1024: qc * S + (th2 + 1) * 1024], qp[:, :], 0.0,
                    scr[:, 1024:2048], ALU.max, ALU.add), "v", 1)
                qel[step] = c3
        ctx.q_ready = qel[15]

        wait("tensor", "dw", co)
        wait("tensor", "dw", ctx.kv_ready_dw)
        wait("tensor", "v", ctx.q_ready)
        den_ev = {}
        in_ev = {}
        comb = {}
        for t in range(l0_tiles):
            npp = pA if t % 2 == 0 else pB
            dnp = pB if t % 2 == 0 else pA
            if t >= 2:
                wait("tensor", "s", in_ev[t - 2])
            if t >= 1:
                wait("tensor", "v", den_ev[t - 1])
            for dc in range(ND):
                inc("tensor", lambda e, dc=dc, t=t, npp=npp: e.matmul(
                    npp[:, dc * 128:(dc + 1) * 128],
                    A1[:, dc * S + t * 128: dc * S + (t + 1) * 128],
                    A1[:, KV2 + dc * 128: KV2 + (dc + 1) * 128],
                    start=True, stop=True), "t", 1)
            if l0_stage < 2:
                continue
            nmm = None
            if t >= 1:
                wait("tensor", "s", in_ev[t - 1])
            for dc in range(ND):
                nmm = inc("tensor", lambda e, dc=dc, t=t, dnp=dnp: e.matmul(
                    dnp[:, 2 * dc:2 * dc + 2],
                    A1[:, dc * S + t * 128: dc * S + (t + 1) * 128],
                    A1[:, KS2 + 2 * dc: KS2 + 2 * dc + 2],
                    start=True, stop=True), "t", 1)
            if l0_stage < 3:
                continue
            wait("vector", "t", nmm)
            if t >= 1:
                wait("vector", "s", in_ev[t - 1])
            c1 = inc("vector", lambda e, dnp=dnp: e.tensor_scalar(
                stat[:, 16:32], dnp[:, 0:16], 1e-6, None, ALU.add), "v", 1)
            wait("vector", "v", c1)
            c2 = inc("vector", lambda e: e.reciprocal(smallA[:, 32:48], stat[:, 16:32]), "v", 1)
            den_ev[t] = c2
            wait("scalar", "v", c2)
            if l0_stage < 4:
                continue
            if t >= 1:
                wait("scalar", "t", comb[t - 1][1])
            sev = None
            for h in range(H):
                sev = inc("scalar", lambda e, h=h, npp=npp: e.activation(
                    inner_t[:, 0 * D + h * 64: 0 * D + (h + 1) * 64], npp[:, h * 64:(h + 1) * 64],
                    AF.Copy, scale=smallA[:, 32 + h:33 + h]), "s", 1)
            in_ev[t] = sev
            if l0_stage < 5:
                continue
            wait("tensor", "s", sev)
            if t >= 1:
                wait("tensor", "s", comb[t - 1][2])
            trl = None
            for dc in range(ND):
                trl = inc("tensor", lambda e, dc=dc: e.transpose(
                    pC[:, dc * 128 - (dc // 4) * 512 + (dc // 4) * 512: dc * 128 + 128] if False else pC[:, dc * 128: dc * 128 + 128] if dc < 4 else pC[:, 512 + (dc - 4) * 128: 512 + (dc - 4) * 128 + 128],
                    inner_t[:, dc * 128:(dc + 1) * 128], ident_sb[:, :]), "t", 1)
            tr_done = trl
            wait("scalar", "t", trl)
            iev = None
            for dc in range(ND):
                iev = inc("scalar", lambda e, dc=dc: e.copy(
                    sg_t[:, dc * 128:(dc + 1) * 128],
                    pC[:, dc * 128: dc * 128 + 128] if dc < 4 else pC[:, 512 + (dc - 4) * 128: 512 + (dc - 4) * 128 + 128]), "s", 1)
            if l0_stage < 6:
                continue
            wait("tensor", "s", iev)
            if t >= 1:
                wait("tensor", "v", comb[t - 1][0])
            pmm = None
            for dcol in range(2):
                for ic in range(ND):
                    pmm = inc("tensor", lambda e, dcol=dcol, ic=ic: e.matmul(
                        pD[:, dcol * 512:(dcol + 1) * 512], sg_t[:, ic * 128:(ic + 1) * 128],
                        wout_ap[:, ic, dcol * 512:(dcol + 1) * 512],
                        start=(ic == 0), stop=(ic == ND - 1)), "t", 1)
            if l0_stage < 7:
                continue
            if t == 0:
                wait("sync", "v", ctx.q_ready)
            if t >= 2:
                wait("sync", "v", comb[t - 2][3])
            cx = inc("sync", lambda e, t=t: e.dma_start(
                xt[:, (t % 2) * D:(t % 2 + 1) * D], x_in[t * 128:(t + 1) * 128, :]), "dx", 16)
            wait("vector", "t", pmm)
            if t == 0:
                wait("vector", "s", ctx.g_ready)
            cc1 = inc("vector", lambda e: e.tensor_tensor(
                xn_t[:, 1024:2048], pD[:, :], bias_rep[:, 0:D], ALU.add), "v", 1)
            wait("vector", "v", cc1)
            cc2 = inc("vector", lambda e, t=t: e.tensor_tensor(
                xn_t[:, 1024:2048], xn_t[:, 1024:2048], accg[:, t * 1024:(t + 1) * 1024], ALU.mult), "v", 1)
            wait("vector", "dx", cx)
            wait("vector", "v", cc2)
            cc3 = inc("vector", lambda e, t=t: e.tensor_tensor(
                xn_t[:, 1024:2048], xn_t[:, 1024:2048], xt[:, (t % 2) * D:(t % 2 + 1) * D], ALU.add), "v", 1)
            wait("sync", "v", cc3)
            cst = inc("sync", lambda e, t=t: e.dma_start(
                x1_dram[t * 128:(t + 1) * 128, :], xn_t[:, 1024:2048]), "dx", 16)
            wait("vector", "dx", cst)
            comb[t] = (cc1, tr_done, iev, cc3)
        ctx.x1_done = sems["dx"].n

        if debug_taps:
            wait("gpsimd", "dx", ctx.x1_done)
            cdx = inc("gpsimd", lambda e: e.dma_start(taps["x1"], x1_dram), "dw", 16)
            for eng in ENG:
                wait(eng, "dw", cdx)

    def phase4():
        p3_t = sems["t"].n
        p3_v = sems["v"].n
        p3_s = sems["s"].n

        def vtick(fn):
            return inc("vector", fn, "v", 1)

        def vwait(c):
            wait("vector", "v", c)

        wait("tensor", "v", p3_v)
        wait("tensor", "s", p3_s)
        ctx.p4_xq_pend = None
        ctx.p4_xq_done = {}
        ctx.p4_evac = {}
        tile_done = {}
        for t in range(NT):
            buf = t % 2
            if t == 0:
                wait("sync", "v", p3_v)
                wait("sync", "s", p3_s)
            if t >= 2:
                wait("sync", "v", tile_done[t - 2][0])
                wait("sync", "s", tile_done[t - 2][1])
                wait("sync", "t", ctx.p4_xq_done[t - 2])
            cx = inc("sync", lambda e, t=t, buf=buf: e.dma_start(
                xt[:, buf * D:(buf + 1) * D], x1_dram[t * 128:(t + 1) * 128, :]), "dx", 16)
            wait("vector", "dx", cx)
            c0 = vtick(lambda e, buf=buf: e.reduce_sum(stat[:, 0:1], xt[:, buf * D:(buf + 1) * D], axis=AX))
            wait("scalar", "dx", cx)
            csq = inc("scalar", lambda e, buf=buf: e.activation(
                xn_t[:, buf * D:(buf + 1) * D], xt[:, buf * D:(buf + 1) * D], AF.Square), "s", 1)
            wait("vector", "s", csq)
            c1 = vtick(lambda e, buf=buf: e.reduce_sum(stat[:, 1:2], xn_t[:, buf * D:(buf + 1) * D], axis=AX))
            vwait(c0)
            c2 = vtick(lambda e: e.tensor_scalar(stat[:, 2:3], stat[:, 0:1], 1.0 / D, None, ALU.mult))
            vwait(c2)
            c3 = vtick(lambda e: e.tensor_tensor(stat[:, 3:4], stat[:, 2:3], stat[:, 2:3], ALU.mult))
            vwait(c3)
            vwait(c1)
            c4 = vtick(lambda e: e.scalar_tensor_tensor(
                stat[:, 4:5], stat[:, 1:2], 1.0 / D, stat[:, 3:4], ALU.mult, ALU.subtract))
            wait("scalar", "v", c4)
            cs1 = inc("scalar", lambda e: e.activation(stat[:, 5:6], stat[:, 4:5], AF.Sqrt, bias=cns[:, 0:1]), "s", 1)
            wait("vector", "s", cs1)
            c5 = vtick(lambda e: e.reciprocal(stat[:, 6:7], stat[:, 5:6]))
            vwait(c5)
            c6 = vtick(lambda e: e.scalar_tensor_tensor(
                stat[:, 7:8], stat[:, 2:3], -1.0, stat[:, 6:7], ALU.mult, ALU.mult))
            vwait(c6)
            cs2 = vtick(lambda e, buf=buf: e.tensor_scalar(
                xn_t[:, buf * D:(buf + 1) * D], xt[:, buf * D:(buf + 1) * D],
                stat[:, 6:7], stat[:, 7:8], ALU.mult, ALU.add))
            c7 = vtick(lambda e: e.tensor_tensor(stat[:, 8:9], stat[:, 4:5], cns[:, 0:1], ALU.add))
            vwait(c7)
            c8 = vtick(lambda e: e.reciprocal(stat[:, 9:10], stat[:, 8:9]))
            vwait(c8)
            c9 = vtick(lambda e: e.tensor_tensor(stat[:, 10:11], stat[:, 4:5], stat[:, 9:10], ALU.mult))
            vwait(c9)
            c10 = vtick(lambda e: e.tensor_tensor(stat[:, 11:12], stat[:, 10:11], cns[:, 1:2], ALU.add))
            wait("scalar", "v", c10)
            cs3 = inc("scalar", lambda e: e.activation(stat[:, 12:13], stat[:, 11:12], AF.Sqrt), "s", 1)
            wait("vector", "s", cs3)
            c11 = vtick(lambda e: e.reciprocal(stat[:, 13:14], stat[:, 12:13]))
            vwait(cs2)
            c12 = vtick(lambda e, buf=buf: e.reduce_max(
                stat[:, 14:15], xn_t[:, buf * D:(buf + 1) * D], axis=AX, apply_absolute_value=True))
            vwait(c11)
            vwait(c12)
            c13 = vtick(lambda e: e.tensor_tensor(stat[:, 15:16], stat[:, 13:14], stat[:, 14:15], ALU.mult))
            vwait(c13)
            c14 = vtick(lambda e: e.tensor_scalar(stat[:, 16:17], stat[:, 15:16], 1e-5, None, ALU.max))
            vwait(c14)
            c15 = vtick(lambda e: e.reciprocal(stat[:, 17:18], stat[:, 16:17]))
            vwait(c15)
            c16 = vtick(lambda e: e.tensor_scalar(stat[:, 18:19], stat[:, 17:18], 127.0, None, ALU.mult))
            c17 = vtick(lambda e: e.tensor_scalar(stat[:, 19:20], stat[:, 16:17], 1.0 / 127.0, None, ALU.mult))
            vwait(c16)
            c18 = vtick(lambda e: e.tensor_tensor(stat[:, 20:21], stat[:, 13:14], stat[:, 18:19], ALU.mult))
            vwait(c18)
            if t >= 2:
                wait("vector", "t", ctx.p4_xq_done[t - 2])
            q1 = vtick(lambda e, buf=buf: e.tensor_scalar(
                xt[:, buf * D:(buf + 1) * D], xn_t[:, buf * D:(buf + 1) * D],
                stat[:, 20:21], None, ALU.mult))
            vwait(q1)
            q2 = vtick(lambda e, buf=buf: e.tensor_scalar(
                xt[:, buf * D:(buf + 1) * D], xt[:, buf * D:(buf + 1) * D],
                RND_C, RND_C, ALU.add, ALU.subtract))
            vwait(q2)
            q3 = vtick(lambda e, buf=buf: e.tensor_scalar(
                xt[:, buf * D:(buf + 1) * D], xt[:, buf * D:(buf + 1) * D],
                127.0, -128.0, ALU.min, ALU.max))
            vwait(q3)
            vwait(c17)
            q4 = vtick(lambda e, buf=buf: e.tensor_scalar(
                xt[:, buf * D:(buf + 1) * D], xt[:, buf * D:(buf + 1) * D],
                stat[:, 19:20], None, ALU.mult))
            wait("tensor", "v", cs2)
            if t >= 1:
                wait("tensor", "s", ctx.p4_evac[t - 1])
            mm1 = None
            for dc in range(ND):
                pbank = pA if dc % 2 == 0 else pB
                off = (dc // 2) * 128
                mm1 = inc("tensor", lambda e, dc=dc, buf=buf, pbank=pbank, off=off: e.transpose(
                    pbank[:, off:off + 128], xn_t[:, buf * D + dc * 128: buf * D + (dc + 1) * 128],
                    ident_sb[:, :]), "t", 1)
            mm2 = None
            if ctx.p4_xq_pend is not None:
                pt, pbuf, pq4 = ctx.p4_xq_pend
                wait("tensor", "v", pq4)
                for dc in range(ND):
                    pbank = pC if dc % 2 == 0 else pD
                    off = (dc // 2) * 128
                    mm2 = inc("tensor", lambda e, dc=dc, pbuf=pbuf, pbank=pbank, off=off: e.transpose(
                        pbank[:, off:off + 128], xt[:, pbuf * D + dc * 128: pbuf * D + (dc + 1) * 128],
                        ident_sb[:, :]), "t", 1)
                ctx.p4_xq_done[pt] = mm2
            pend2 = ctx.p4_xq_pend
            ctx.p4_xq_pend = (t, buf, q4)
            wait("scalar", "t", mm2 if mm2 is not None else mm1)
            sev = None
            for dc in range(ND):
                pbank = pA if dc % 2 == 0 else pB
                off = (dc // 2) * 128
                sev = inc("scalar", lambda e, dc=dc, t=t, pbank=pbank, off=off: e.copy(
                    xnT[dc][:, t * 128:(t + 1) * 128], pbank[:, off:off + 128]), "s", 1)
            if pend2 is not None:
                pt, pbuf, pq4 = pend2
                for dc in range(ND):
                    pbank = pC if dc % 2 == 0 else pD
                    off = (dc // 2) * 128
                    sev = inc("scalar", lambda e, dc=dc, pt=pt, pbank=pbank, off=off: e.copy(
                        A1[:, dc * S + pt * 128: dc * S + (pt + 1) * 128], pbank[:, off:off + 128]), "s", 1)
            ctx.p4_evac[t] = sev
            tile_done[t] = (q4, csq, mm2 if mm2 is not None else mm1)
        pt, pbuf, pq4 = ctx.p4_xq_pend
        wait("tensor", "v", pq4)
        wait("tensor", "s", ctx.p4_evac[NT - 1])
        mm2 = None
        for dc in range(ND):
            pbank = pC if dc % 2 == 0 else pD
            off = (dc // 2) * 128
            mm2 = inc("tensor", lambda e, dc=dc, pbuf=pbuf, pbank=pbank, off=off: e.transpose(
                pbank[:, off:off + 128], xt[:, pbuf * D + dc * 128: pbuf * D + (dc + 1) * 128],
                ident_sb[:, :]), "t", 1)
        ctx.p4_xq_done[pt] = mm2
        wait("scalar", "t", mm2)
        sev = None
        for dc in range(ND):
            pbank = pC if dc % 2 == 0 else pD
            off = (dc // 2) * 128
            sev = inc("scalar", lambda e, dc=dc, pt=pt, pbank=pbank, off=off: e.copy(
                A1[:, dc * S + pt * 128: dc * S + (pt + 1) * 128], pbank[:, off:off + 128]), "s", 1)
        ctx.p4_done_s = sems["s"].n
        ctx.p4_done_v = sems["v"].n
        ctx.p4_done_t = sems["t"].n

    def phase5():
        wi_r = w_int8.rearrange("(wc p) i -> wc p i", p=128)
        wait("tensor", "s", ctx.p4_done_s)
        wait("tensor", "dx", ctx.cols_ready)
        cwm = inc("tensor", lambda e: e.matmul(
            pA[:, 0:1], ones_sb[0:1, :], cols1[0:1, 5:6], start=True, stop=True), "t", 1)
        wait("vector", "t", cwm)
        cws = inc("vector", lambda e: e.tensor_copy(smallA[:, 60:61], pA[:, 0:1]), "v", 1)
        wait("gpsimd", "v", ctx.p4_done_v)
        wait("gpsimd", "t", ctx.p4_done_t)
        wt_ev = {}
        for wc in range(ND):
            buf = wc % 2
            scr = xt if buf == 0 else xn_t
            if wc >= 2:
                wait("gpsimd", "t", wt_ev[wc - 2][0])
            cwl = inc("gpsimd", lambda e, wc=wc, scr=scr: e.dma_start(
                scr[:, 0:1024], wi_r[wc]), "dw", 16)
            wait("tensor", "dw", cwl)
            if wc == 0:
                wait("tensor", "v", cws)
            if wc >= 2:
                wait("tensor", "s", wt_ev[wc - 2][1])
            mmw = None
            for ic in range(ND):
                pbank = (pA if ic % 2 == 0 else pB) if wc % 2 == 0 else (pC if ic % 2 == 0 else pD)
                off = (ic // 2) * 128
                mmw = inc("tensor", lambda e, ic=ic, scr=scr, pbank=pbank, off=off: e.transpose(
                    pbank[:, off:off + 128], scr[:, ic * 128:(ic + 1) * 128], ident_sb[:, :]), "t", 1)
            wait("scalar", "t", mmw)
            sev = None
            for ic in range(ND):
                pbank = (pA if ic % 2 == 0 else pB) if wc % 2 == 0 else (pC if ic % 2 == 0 else pD)
                off = (ic // 2) * 128
                sev = inc("scalar", lambda e, ic=ic, wc=wc, pbank=pbank, off=off: e.copy(
                    WARENA.ap()[:, 0:8192].rearrange("p (ic o) -> p ic o", ic=8)[:, ic, wc * 128:(wc + 1) * 128],
                    pbank[:, off:off + 128]), "s", 1)
            wt_ev[wc] = (mmw, sev)
        wt_v = WARENA.ap()[:, 0:8192].rearrange("p (ic o) -> p ic o", ic=8)
        wait("tensor", "s", wt_ev[ND - 1][1])
        wait("tensor", "s", wt_ev[ND - 2][1])
        g1ev = {}
        for t in range(NT):
            gp = (pA, pB, pC, pD)[t % 4]
            if t >= 4:
                wait("tensor", "s", g1ev[t - 4])
            gmm = None
            for dc in range(ND):
                for oc in range(2):
                    gmm = inc("tensor", lambda e, dc=dc, oc=oc, t=t, gp=gp: e.matmul(
                        gp[:, oc * 512:(oc + 1) * 512],
                        A1[:, dc * S + t * 128: dc * S + (t + 1) * 128],
                        wt_v[:, dc, oc * 512:(oc + 1) * 512],
                        start=(dc == 0), stop=(dc == ND - 1)), "t", 1)
            wait("vector", "t", gmm)
            if t == 0:
                wait("vector", "v", cws)
            if t >= 2:
                wait("vector", "s", g1ev[t - 2])
            buf = t % 2
            scr = xt if buf == 0 else xn_t
            cv1 = inc("vector", lambda e, gp=gp, scr=scr: e.scalar_tensor_tensor(
                scr[:, 1024:2048], gp[:, :], smallA[:, 60:61], bias_rep[:, 2 * D:3 * D],
                ALU.mult, ALU.add), "v", 1)
            wait("scalar", "v", cv1)
            g1ev[t] = inc("scalar", lambda e, t=t, scr=scr: e.activation(
                accg[:, t * 1024:(t + 1) * 1024], scr[:, 1024:2048], AF.Silu), "s", 1)
        ctx.g1_ready = g1ev[NT - 1]

        if debug_taps:
            wait("gpsimd", "s", ctx.g1_ready)
            cdx = None
            for t in range(NT):
                cdx = inc("gpsimd", lambda ee, t=t: ee.dma_start(
                    taps["g1"][t * 128:(t + 1) * 128, :], accg[:, t * 1024:(t + 1) * 1024]), "dw", 16)
            for eng in ENG:
                wait(eng, "dw", cdx)

    def phase6():
        H2 = 2048
        NIB = 8
        v_tm = WARENA.ap()[:, 0:16384].rearrange("p (t d) -> p t d", t=NT)
        wqk_v = WARENA.ap()[:, 16384:17408].rearrange("p (dc f) -> p dc f", dc=8)
        at_sb = WARENA.ap()[:, 17408:17920]
        wv_v = A1.ap()[:, 0:8192].rearrange("p (dc f) -> p dc f", dc=8)
        QF, KF = 16384, 18432
        qf_ap = A1.ap()[:, QF:QF + 2048]
        kf_ap = A1.ap()[:, KF:KF + 2048]

        p5_t = sems["t"].n
        wait("vector", "dx", ctx.cols_ready)
        ct1_ = inc("vector", lambda e: e.tensor_scalar(
            tblp[0:NB, 0:1], tblp[0:NB, 0:1], REL_SCALE / SIGMA, -MU / SIGMA, ALU.mult, ALU.add), "v", 1)
        cms = inc("gpsimd", lambda e: e.memset(tblp[NB:NB + 1, 0:1], MASK_NEG), "g", 1)
        wait("tensor", "s", ctx.p4_done_s)
        bv_ev = {}
        for hf in range(2):
            wait("sync", "s", sems["s"].n)
            wait("sync", "v", sems["v"].n)
            chl = inc("sync", lambda e, hf=hf: e.dma_start(
                xt[0:NB + 1, 0:1280], oht[:, hf * 1280:(hf + 1) * 1280]), "dx", 16)
            wait("tensor", "dx", chl)
            wait("tensor", "v", ct1_)
            wait("tensor", "g", cms)
            wait("tensor", "s", sems["s"].n)
            wait("tensor", "v", sems["v"].n)
            for mc in range(10):
                c = hf * 10 + mc
                bank = c % 2
                prev = bv_ev.get(bank)
                if prev is not None:
                    wait("tensor", "v", prev)
                bmm = inc("tensor", lambda e, mc=mc, bank=bank: e.matmul(
                    pC[0:1, bank * 512: bank * 512 + 128],
                    tblp[0:NB + 1, 0:1], xt[0:NB + 1, mc * 128:(mc + 1) * 128],
                    start=True, stop=True), "t", 1)
                wait("vector", "t", bmm)
                if c < 16:
                    bv_ev[bank] = inc("vector", lambda e, c=c, bank=bank: e.tensor_copy(
                        xn_t[0:1, c * 128:(c + 1) * 128],
                        pC[0:1, bank * 512: bank * 512 + 128]), "v", 1)
                else:
                    bv_ev[bank] = inc("vector", lambda e, c=c, bank=bank: e.tensor_copy(
                        strip[0:1, (c - 16) * 128:(c - 15) * 128],
                        pC[0:1, bank * 512: bank * 512 + 128]), "v", 1)
        wait("sync", "v", sems["v"].n)
        cbd = inc("sync", lambda e: e.dma_start(bvec_dram[0:2048], xn_t[0:1, 0:2048]), "dx", 16)
        cbd = inc("sync", lambda e: e.dma_start(bvec_dram[2048:2560], strip[0:1, 0:512]), "dx", 16)
        wait("sync", "dx", cbd)
        cst = None
        for p in range(128):
            cst = inc("sync", lambda e, p=p: e.dma_start(
                strip[p:p + 1, :], bvec_dram[128 - p:128 - p + STRIP_W]), "dx", 16)
        ctx.strip_ready = cst
        if debug_taps:
            wait("gpsimd", "dx", cst)
            inc("gpsimd", lambda e: e.dma_start(taps["strip"], strip[:, :]), "dw", 16)

        wqk_r = wqk1.rearrange("(dc p) f -> p dc f", p=128)
        wv_r = wv1.rearrange("(dc p) f -> p dc f", p=128)
        wait("gpsimd", "t", p5_t)
        cqk = inc("gpsimd", lambda e: e.dma_start(wqk_v, wqk_r), "dw", 16)
        cwv = inc("gpsimd", lambda e: e.dma_start(wv_v, wv_r), "dw", 16)

        wait("tensor", "dw", cqk)
        wait("tensor", "v", sems["v"].n)
        wait("tensor", "s", sems["s"].n)
        qkev = {}
        for tb in range(4):
            qp = (pA, pB, pC, pD)[tb]
            qmm = None
            for dc in range(ND):
                qmm = inc("tensor", lambda e, dc=dc, tb=tb, qp=qp: e.matmul(
                    qp[:, 0:512], wqk_v[:, dc, :], xnT[dc][:, tb * 512:(tb + 1) * 512],
                    start=(dc == 0), stop=(dc == ND - 1)), "t", 1)
            wait("scalar", "t", qmm)
            wait("scalar", "dx", ctx.cols_ready)
            if tb >= 1:
                wait("scalar", "v", qkev[tb - 1][1])
            csl = inc("scalar", lambda e, tb=tb, qp=qp: e.activation(
                sg_t[:, 1024:1536], qp[:, 0:512], AF.Silu, bias=cols1[:, 0:1]), "s", 1)
            wait("vector", "s", csl)
            cq1 = inc("vector", lambda e, tb=tb: e.tensor_scalar(
                qf_ap[:, tb * 512:(tb + 1) * 512], sg_t[:, 1024:1536],
                cols1[:, 1:2], cols1[:, 3:4], ALU.mult, ALU.add), "v", 1)
            cq2 = inc("vector", lambda e, tb=tb: e.tensor_scalar(
                kf_ap[:, tb * 512:(tb + 1) * 512], sg_t[:, 1024:1536],
                cols1[:, 2:3], cols1[:, 4:5], ALU.mult, ALU.add), "v", 1)
            qkev[tb] = (csl, cq2)
        ctx.qk_ready = qkev[3][1]

        wait("tensor", "dw", cwv)
        wait("tensor", "v", ctx.qk_ready)
        vev = {}
        for t in range(NT):
            vp = pA if t % 2 == 0 else pB
            if t >= 2:
                wait("tensor", "s", vev[t - 2])
            vmm = None
            for dc in range(ND):
                for oc in range(2):
                    vmm = inc("tensor", lambda e, dc=dc, oc=oc, t=t, vp=vp: e.matmul(
                        vp[:, oc * 512:(oc + 1) * 512],
                        A1[:, dc * S + t * 128: dc * S + (t + 1) * 128] if False else
                        xnT[dc][:, t * 128:(t + 1) * 128],
                        wv_v[:, dc, oc * 512:(oc + 1) * 512],
                        start=(dc == 0), stop=(dc == ND - 1)), "t", 1)
            buf = t % 2
            scr = xt if buf == 0 else xn_t
            wait("vector", "t", vmm)
            if t >= 2:
                wait("vector", "s", vev[t - 2])
            cv1 = inc("vector", lambda e, vp=vp, scr=scr: e.tensor_tensor(
                scr[:, 0:1024], vp[:, :], bias_rep[:, D:2 * D], ALU.add), "v", 1)
            wait("scalar", "v", cv1)
            vev[t] = inc("scalar", lambda e, t=t, scr=scr: e.activation(
                v_tm[:, t, :], scr[:, 0:1024], AF.Silu), "s", 1)
        ctx.v_ready = vev[NT - 1]

        C_MUL = 1.0 / (S * SIGMA)
        wait("tensor", "s", ctx.v_ready)
        wait("tensor", "dx", ctx.strip_ready)
        wait("vector", "dx", ctx.strip_ready)
        at_free = {}
        comb2 = {}
        ctx.p6_prev_gelu = None
        ctx.p6_prev_store = None
        ctx.p6_yv = {}
        for ib in range(p6_ibs):
            i0 = ib * 256
            njc = 2 * ib + 2
            pending = None
            for jc in range(njc):
                abuf = jc % 2
                ap_ = pA if abuf == 0 else pB
                prevy = ctx.p6_yv.get(abuf)
                if prevy is not None:
                    wait("tensor", "v", prevy)
                amm = inc("tensor", lambda e, jc=jc, i0=i0, ap_=ap_: e.matmul(
                    ap_[:, 0:256], kf_ap[:, jc * 128:(jc + 1) * 128], qf_ap[:, i0:i0 + 256],
                    start=True, stop=True), "t", 1)
                Dg = i0 - jc * 128
                so = Dg + 384
                wait("vector", "t", amm)
                prevg = ctx.p6_prev_gelu
                if prevg is not None:
                    wait("vector", "s", prevg)
                y1 = inc("vector", lambda e, ap_=ap_, so=so: e.scalar_tensor_tensor(
                    xt[:, 0:256], ap_[:, 0:256], C_MUL, strip[:, so:so + 256],
                    ALU.mult, ALU.add), "v", 1)
                ctx.p6_yv[abuf] = y1
                wait("scalar", "v", y1)
                g1_ = inc("scalar", lambda e: e.activation(
                    xn_t[:, 0:256], xt[:, 0:256], AF.Gelu), "s", 1)
                ctx.p6_prev_gelu = g1_
                m1_ = inc("vector", lambda e: e.tensor_scalar(
                    xt[:, 256:512], xt[:, 0:256], 0.0, None, ALU.is_equal), "v", 1)
                wait("vector", "s", g1_)
                wait("vector", "v", m1_)
                n1 = inc("vector", lambda e: e.scalar_tensor_tensor(
                    xn_t[:, 256:512], xt[:, 256:512], 0.5, xn_t[:, 0:256], ALU.mult, ALU.add), "v", 1)
                d1 = inc("vector", lambda e: e.tensor_tensor(
                    xt[:, 512:768], xt[:, 0:256], xt[:, 256:512], ALU.add), "v", 1)
                wait("vector", "v", d1)
                r1 = inc("vector", lambda e: e.reciprocal(xt[:, 768:1024], xt[:, 512:768]), "v", 1)
                wait("vector", "v", r1)
                wait("vector", "v", n1)
                if at_free.get(abuf) is not None:
                    wait("vector", "t", at_free[abuf])
                ca = inc("vector", lambda e, abuf=abuf: e.tensor_tensor(
                    at_sb[:, abuf * 256:(abuf + 1) * 256], xn_t[:, 256:512], xt[:, 768:1024],
                    ALU.mult), "v", 1)
                if pending is not None:
                    pjc, pca, pabuf = pending
                    wait("tensor", "v", pca)
                    if pjc < 2 and ib > 0:
                        wait("tensor", "v", comb2[ib - 1])
                    last_amm = None
                    for isub in range(2):
                        op_ = pC if isub == 0 else pD
                        for dh in range(2):
                            last_amm = inc("tensor", lambda e, isub=isub, dh=dh, pjc=pjc, pabuf=pabuf, op_=op_, njc=njc: e.matmul(
                                op_[:, dh * 512:(dh + 1) * 512],
                                at_sb[:, pabuf * 256 + isub * 128: pabuf * 256 + isub * 128 + 128],
                                v_tm[:, pjc, dh * 512:(dh + 1) * 512],
                                start=(pjc == 0), stop=(pjc == njc - 1)), "t", 1)
                    at_free[pabuf] = last_amm
                pending = (jc, ca, abuf)
            pjc, pca, pabuf = pending
            wait("tensor", "v", pca)
            if pjc < 2 and ib > 0:
                wait("tensor", "v", comb2[ib - 1])
            last_amm = None
            for isub in range(2):
                op_ = pC if isub == 0 else pD
                for dh in range(2):
                    last_amm = inc("tensor", lambda e, isub=isub, dh=dh, pjc=pjc, pabuf=pabuf, op_=op_, njc=njc: e.matmul(
                        op_[:, dh * 512:(dh + 1) * 512],
                        at_sb[:, pabuf * 256 + isub * 128: pabuf * 256 + isub * 128 + 128],
                        v_tm[:, pjc, dh * 512:(dh + 1) * 512],
                        start=(pjc == 0), stop=(pjc == njc - 1)), "t", 1)
            at_free[pabuf] = last_amm
            cc_l = None
            for isub in range(2):
                t = 2 * ib + isub
                op_ = pC if isub == 0 else pD
                if ctx.p6_prev_store is not None:
                    wait("sync", "dx", ctx.p6_prev_store)
                cx = inc("sync", lambda e, t=t: e.dma_start(
                    xt[:, 1024:2048], x1_dram[t * 128:(t + 1) * 128, :]), "dx", 16)
                wait("vector", "t", at_free[0])
                wait("vector", "t", at_free[1])
                if ib == 0 and isub == 0:
                    wait("vector", "s", ctx.g1_ready)
                c1_ = inc("vector", lambda e, t=t, op_=op_: e.tensor_tensor(
                    sg_t[:, 0:1024], op_[:, :], accg[:, t * 1024:(t + 1) * 1024], ALU.mult), "v", 1)
                wait("vector", "dx", cx)
                wait("vector", "v", c1_)
                c2_ = inc("vector", lambda e: e.tensor_tensor(
                    xt[:, 1024:2048], xt[:, 1024:2048], sg_t[:, 0:1024], ALU.add), "v", 1)
                wait("sync", "v", c2_)
                cso = inc("sync", lambda e, t=t: e.dma_start(
                    out[t * 128:(t + 1) * 128, :], xt[:, 1024:2048]), "dx", 16)
                ctx.p6_prev_store = cso
                cc_l = c2_
            comb2[ib] = cc_l
        ctx.out_done = sems["dx"].n
        wait("sync", "dx", ctx.out_done)

    phase_list = [phase0, phase1, phase2, phase3, phase4, phase5, phase6]

    with (
        nc.semaphore("sem_dw") as h_dw,
        nc.semaphore("sem_dx") as h_dx,
        nc.semaphore("sem_t") as h_t,
        nc.semaphore("sem_v") as h_v,
        nc.semaphore("sem_s") as h_s,
        nc.semaphore("sem_g") as h_g,
    ):
        sem_handles.update({"dw": h_dw, "dx": h_dx, "t": h_t, "v": h_v, "s": h_s, "g": h_g})
        for ph in phase_list:
            ph()



        with nc.Block() as block:
            @block.sync
            def _(e):
                for f in prog["sync"]:
                    f(e)

            @block.gpsimd
            def _(e):
                for f in prog["gpsimd"]:
                    f(e)

            @block.vector
            def _(e):
                for f in prog["vector"]:
                    f(e)

            @block.scalar
            def _(e):
                for f in prog["scalar"]:
                    f(e)

            @block.tensor
            def _(e):
                for f in prog["tensor"]:
                    f(e)

    return nc


def _aux_inputs():
    ident = np.eye(128, dtype=np.float32)
    buckets = _rel_bucket_ext()
    oht = np.zeros((NB + 1, BVEC_N), dtype=np.float32)
    oht[buckets, np.arange(BVEC_N)] = 1.0
    ones = np.ones((1, 128), dtype=np.float32)
    sel = np.zeros((NE, NE * 128), dtype=ml_dtypes.bfloat16)
    for e in range(NE):
        sel[e, e * 128:(e + 1) * 128] = 1.0
    return {"aux_ident": ident, "aux_oht": oht, "aux_ones": ones, "aux_sel8": sel}


def kernel(**inputs):
    nc = build_program(debug_taps=False)
    aux = _aux_inputs()
    x_full = np.ascontiguousarray(inputs["L0_attn_x"], dtype=np.float32)
    base = {k: np.ascontiguousarray(v) for k, v in inputs.items() if k != "L0_attn_x"}
    base["L1_ffn_weight_scale"] = np.asarray(inputs["L1_ffn_weight_scale"], dtype=np.float32).reshape(1, 1)
    in_maps = []
    for c in range(8):
        m = dict(base)
        m.update(aux)
        m["L0_attn_x"] = x_full[c]
        in_maps.append(m)
    res = run_bass_kernel_spmd(nc, in_maps, core_ids=list(range(8)))
    return np.stack([res.results[c]["out"] for c in range(8)], axis=0)


# version 05
# speedup vs baseline: 1.1540x; 1.0317x over previous
import math
import numpy as np
import ml_dtypes

import concourse.bass as bass
import concourse.mybir as mybir
from concourse.bass_utils import run_bass_kernel_spmd

F32 = mybir.dt.float32
BF16 = mybir.dt.bfloat16
I32 = mybir.dt.int32
AF = mybir.ActivationFunctionType
ALU = mybir.AluOpType
AX = mybir.AxisListType.X

S = 2048
D = 1024
NT = S // 128
ND = D // 128
F = 2816
NFC = F // 128
NE = 8
DQK = 128
NB = 32

MU = math.sqrt(0.5)
SIGMA = math.sqrt(0.25 * math.pi)
REL_SCALE = DQK ** 0.5
MASK_NEG = -100000.0
RND_C23 = float(3 * 2.0 ** 22)
RND_C52 = float(3 * 2.0 ** 51)
RND_C32 = float(3 * 2.0 ** 31)

STRIP_W = 2176
STRIP_PAD = 512
BVEC_N = 2560


def _rel_bucket_ext():
    out = np.zeros(BVEC_N, dtype=np.int64)
    for i in range(BVEC_N):
        m = i - STRIP_PAD
        if m < 0:
            out[i] = NB
        elif m < 16:
            out[i] = m
        else:
            t = np.float32(np.log(np.float32(m) / np.float32(16.0)))
            t = np.float32(t / np.float32(math.log(128 / 16)))
            t = np.float32(t * np.float32(16))
            out[i] = min(16 + int(np.int32(t)), NB - 1)
    return out


def build_program(debug_taps=False, moe_ne=NE, moe_silu=True, l0_tiles=NT, l0_stage=9, HPAR=2, p6_ibs=8):
    nc = bass.Bass("TRN2", target_bir_lowering=False, detect_race_conditions=False)

    def din(name, shape, dtype=F32):
        return nc.dram_tensor(name, shape, dtype, kind="ExternalInput").ap()

    x_in = din("L0_attn_x", [S, D])
    wqkv = din("L0_attn_Wqkv", [D, 3 * D])
    wout0 = din("L0_attn_Wout", [D, D])
    bout0 = din("L0_attn_bout", [D])
    router_w = din("L0_ffn_router_w", [D, NE])
    gate_p = din("L0_ffn_gate_proj", [NE, D, F])
    up_p = din("L0_ffn_up_proj", [NE, D, F])
    down_p = din("L0_ffn_down_proj", [NE, F, D])
    wqk1 = din("L1_attn_wqk", [D, DQK])
    bqk1 = din("L1_attn_bqk", [DQK])
    gamma1 = din("L1_attn_gamma", [2, DQK])
    beta1 = din("L1_attn_beta", [2, DQK])
    wv1 = din("L1_attn_wv", [D, D])
    bv1 = din("L1_attn_bv", [D])
    relb = din("L1_attn_rel_bias", [NB])
    w_int8 = din("L1_ffn_w_int8", [D, D], I32)
    wscale = din("L1_ffn_weight_scale", [1, 1])
    bias1 = din("L1_ffn_bias", [D])
    ident = din("aux_ident", [128, 128])
    oht = din("aux_oht", [NB + 1, BVEC_N])
    ones_row = din("aux_ones", [1, 128])
    sel8 = din("aux_sel8", [NE, NE * 128], BF16)

    out = nc.dram_tensor("out", [S, D], F32, kind="ExternalOutput").ap()

    x1_dram = nc.dram_tensor("x1_scratch", [S, D], F32, kind="Internal").ap()
    bvec_dram = nc.dram_tensor("bvec_scratch", [BVEC_N], F32, kind="Internal").ap()

    taps = {}
    if debug_taps:
        taps["xnT"] = nc.dram_tensor("tap_xnT", [ND, 128, S], F32, kind="ExternalOutput").ap()
        taps["ewT"] = nc.dram_tensor("tap_ewT", [NE, S], F32, kind="ExternalOutput").ap()
        taps["g"] = nc.dram_tensor("tap_g", [S, D], F32, kind="ExternalOutput").ap()
        taps["x1"] = nc.dram_tensor("tap_x1", [S, D], F32, kind="ExternalOutput").ap()
        taps["g1"] = nc.dram_tensor("tap_g1", [S, D], F32, kind="ExternalOutput").ap()
        taps["strip"] = nc.dram_tensor("tap_strip", [128, STRIP_W], F32, kind="ExternalOutput").ap()
        taps["wrep"] = nc.dram_tensor("tap_wrep", [128, S], F32, kind="ExternalOutput").ap()
        taps["acts"] = nc.dram_tensor("tap_acts", [11, 128, S], F32, kind="ExternalOutput").ap()

    sb = nc.alloc_sbuf_tensor
    xnT = [sb(f"xnT{c}", [128, S], BF16) for c in range(ND)]
    accg = sb("accg", [128, NT * D], BF16)
    A1 = sb("A1", [128, 11 * S], BF16)
    WARENA = sb("WARENA", [128, 23552], BF16)
    w_rep = sb("w_rep", [128, S], BF16)
    ewT = sb("ewT", [NE, S], BF16)
    xt = sb("xt", [128, 2 * D], F32)
    xn_t = sb("xn_t", [128, 2 * D], F32)
    sg_t = sb("sg_t", [128, 2 * 1024], BF16)
    up_t = sb("up_t", [128, 2 * 1088], BF16)
    stat = sb("stat", [128, 32], F32)
    scores = sb("scores", [128, NT * 8], F32)
    strip = sb("strip", [128, STRIP_W], F32)
    ident_sb = sb("ident_sb", [128, 128], F32)
    ones_sb = sb("ones_sb", [1, 128], F32)
    tblp = sb("tblp", [NB + 1, 1], F32)
    smallA = sb("smallA", [128, 64], F32)
    rw_hl = sb("rw_hl", [128, 8 * 16], BF16)
    bias_rep = sb("bias_rep", [128, 3 * D], BF16)
    rw_stage = strip
    cns = sb("cns", [128, 4], F32)

    sel8_sb = sb("sel8_sb", [NE, NE * 128], BF16)
    inner_t = xn_t
    innerT_t = sg_t
    cols1 = sb("cols1", [128, 8], F32)
    g1b = None

    pp = nc.alloc_psum_tensor
    pA = pp("pA", [128, 1024], F32)
    pB = pp("pB", [128, 1024], F32)
    pC = pp("pC", [128, 1024], F32)
    pD = pp("pD", [128, 1024], F32)

    ENG = ["sync", "gpsimd", "vector", "scalar", "tensor"]
    prog = {e: [] for e in ENG}
    sem_handles = {}
    counts = {}

    class Ctx:
        pass

    ctx = Ctx()

    def emit(eng, fn):
        prog[eng].append(fn)

    class SemTracker:
        def __init__(self, name):
            self.name = name
            self.n = 0

        def tick(self, amt):
            self.n += amt
            return self.n

    sems = {k: SemTracker(k) for k in ["dw", "dx", "t", "v", "s", "g"]}

    def inc(eng, ins_fn, sem, amt):
        c = sems[sem].tick(amt)

        def f(e, ins_fn=ins_fn, sem=sem, amt=amt):
            ins_fn(e).then_inc(sem_handles[sem], amt)
        emit(eng, f)
        return c

    def raw(eng, ins_fn):
        emit(eng, ins_fn)

    def wait(eng, sem, val):
        if val <= 0:
            return

        def f(e, sem=sem, val=val):
            e.wait_ge(sem_handles[sem], val)
        emit(eng, f)

    def phase0():
        cg = inc("gpsimd", lambda e: e.memset(cns[:, 0:1], 1e-5), "g", 1)
        cg = inc("gpsimd", lambda e: e.memset(cns[:, 1:2], 1e-6), "g", 1)

        ctx.cns_ready = cg
        c = inc("sync", lambda e: e.dma_start(ident_sb[:, :], ident), "dx", 16)
        ctx.sel8_ready = inc("sync", lambda e: e.dma_start(sel8_sb[:, :], sel8), "dx", 16)
        inc("sync", lambda e: e.dma_start(ones_sb[:, :], ones_row), "dx", 16)
        rw_r = router_w.rearrange("(dc p) e -> p dc e", p=128)
        c2 = inc("sync", lambda e: e.dma_start(rw_stage[:, 0:64].rearrange("p (dc e) -> p dc e", dc=8), rw_r), "dx", 16)
        wait("scalar", "dx", c2)
        for dc in range(ND):
            raw("scalar", lambda e, dc=dc: e.copy(rw_hl[:, dc * 16: dc * 16 + 8], rw_stage[:, dc * 8: dc * 8 + 8]))
        cs = inc("scalar", lambda e: e.copy(smallA[:, 63:64], ident_sb[:, 0:1]), "s", 1)
        wait("vector", "s", cs)
        for dc in range(ND):
            fn = lambda e, dc=dc: e.tensor_tensor(
                rw_hl[:, dc * 16 + 8: dc * 16 + 16], rw_stage[:, dc * 8: dc * 8 + 8],
                rw_hl[:, dc * 16: dc * 16 + 8], ALU.subtract)
            if dc < ND - 1:
                raw("vector", fn)
            else:
                ctx.rw_ready = inc("vector", fn, "v", 1)

        row_buf = smallA
        c3 = inc("sync", lambda e: e.dma_start(xt[0:1, 0:D], bout0[None, :]), "dx", 16)
        c3 = inc("sync", lambda e: e.dma_start(xt[0:1, D:2 * D], bv1[None, :]), "dx", 16)
        c3 = inc("sync", lambda e: e.dma_start(xn_t[0:1, 0:D], bias1[None, :]), "dx", 16)
        wait("tensor", "dx", c3)
        tmm = None
        for k in range(3):
            pb = (pA, pB, pC)[k]
            src_row = (xt, xt, xn_t)[k]
            off = (0, D, 0)[k]
            for half in range(2):
                tmm = inc("tensor", lambda e, half=half, pb=pb, src_row=src_row, off=off: e.matmul(
                    pb[:, half * 512:(half + 1) * 512],
                    ones_sb[0:1, :], src_row[0:1, off + half * 512: off + (half + 1) * 512],
                    start=True, stop=True), "t", 1)
        ctx.bias_mm_done = tmm
        wait("vector", "t", tmm)
        cv = None
        for k in range(3):
            cv = inc("vector", lambda e, k=k: e.tensor_copy(
                bias_rep[:, k * D:(k + 1) * D], (pA if k == 0 else (pB if k == 1 else pC))[:, 0:1024]), "v", 1)
        ctx.bias_rep_ready = cv
        ctx.p0_psum_free = cv

        c4 = inc("sync", lambda e: e.dma_start(cols1[:, 0:1], bqk1[:, None]), "dx", 16)
        c4 = inc("sync", lambda e: e.dma_start(cols1[:, 1:2], gamma1[0, :][:, None]), "dx", 16)
        c4 = inc("sync", lambda e: e.dma_start(cols1[:, 2:3], gamma1[1, :][:, None]), "dx", 16)
        c4 = inc("sync", lambda e: e.dma_start(cols1[:, 3:4], beta1[0, :][:, None]), "dx", 16)
        c4 = inc("sync", lambda e: e.dma_start(cols1[:, 4:5], beta1[1, :][:, None]), "dx", 16)
        c4 = inc("sync", lambda e: e.dma_start(cols1[0:1, 5:6], wscale), "dx", 16)
        c4 = inc("sync", lambda e: e.dma_start(tblp[0:NB, 0:1], relb[:, None]), "dx", 16)
        ctx.cols_ready = c4

    def phase1():
        def vtick(fn):
            return inc("vector", fn, "v", 1)

        def vwait(c):
            wait("vector", "v", c)

        lo_arena = accg
        wait("scalar", "g", ctx.cns_ready)
        ctx.p1_evac = {}
        ctx.p1_tile_done = {}
        wait("sync", "t", ctx.bias_mm_done)
        mm_last = 0
        for t in range(NT):
            buf = t % 2
            if t >= 2:
                wait("sync", "v", ctx.p1_tile_done[t - 2][0])
                wait("sync", "s", ctx.p1_tile_done[t - 2][1])
            cx = inc("sync", lambda e, t=t, buf=buf: e.dma_start(
                xt[:, buf * D:(buf + 1) * D], x_in[t * 128:(t + 1) * 128, :]), "dx", 16)
            wait("vector", "dx", cx)
            c0 = vtick(lambda e, buf=buf: e.reduce_sum(stat[:, 0:1], xt[:, buf * D:(buf + 1) * D], axis=AX))
            wait("scalar", "dx", cx)
            csq = inc("scalar", lambda e, buf=buf: e.activation(
                xn_t[:, buf * D:(buf + 1) * D], xt[:, buf * D:(buf + 1) * D], AF.Square), "s", 1)
            wait("vector", "s", csq)
            c1 = vtick(lambda e, buf=buf: e.reduce_sum(stat[:, 1:2], xn_t[:, buf * D:(buf + 1) * D], axis=AX))
            vwait(c0)
            c2 = vtick(lambda e: e.tensor_scalar(stat[:, 2:3], stat[:, 0:1], 1.0 / D, None, ALU.mult))
            vwait(c2)
            c3 = vtick(lambda e: e.tensor_tensor(stat[:, 3:4], stat[:, 2:3], stat[:, 2:3], ALU.mult))
            vwait(c3)
            vwait(c1)
            c4 = vtick(lambda e: e.scalar_tensor_tensor(
                stat[:, 4:5], stat[:, 1:2], 1.0 / D, stat[:, 3:4], ALU.mult, ALU.subtract))
            wait("scalar", "v", c4)
            cs1 = inc("scalar", lambda e: e.activation(stat[:, 5:6], stat[:, 4:5], AF.Sqrt, bias=cns[:, 0:1]), "s", 1)
            wait("vector", "s", cs1)
            c5 = vtick(lambda e: e.reciprocal(stat[:, 6:7], stat[:, 5:6]))
            vwait(c5)
            c6 = vtick(lambda e: e.scalar_tensor_tensor(
                stat[:, 7:8], stat[:, 2:3], -1.0, stat[:, 6:7], ALU.mult, ALU.mult))
            vwait(c6)
            cs2 = vtick(lambda e, buf=buf: e.tensor_scalar(
                xn_t[:, buf * D:(buf + 1) * D], xt[:, buf * D:(buf + 1) * D],
                stat[:, 6:7], stat[:, 7:8], ALU.mult, ALU.add))
            wait("tensor", "v", cs2)
            if t == 0:
                wait("tensor", "v", ctx.bias_rep_ready)
            if t >= 2:
                wait("tensor", "v", ctx.p1_evac[t - 2])
            for dc in range(ND):
                pbank = (pA if (dc % 2 == 0) else pB) if t % 2 == 0 else (pC if (dc % 2 == 0) else pD)
                off = (dc // 2) * 128
                mm_last = inc("tensor", lambda e, dc=dc, buf=buf, pbank=pbank, off=off: e.transpose(
                    pbank[:, off:off + 128], xn_t[:, buf * D + dc * 128: buf * D + (dc + 1) * 128], ident_sb[:, :]),
                    "t", 1)
            wait("scalar", "t", mm_last)
            sev = None
            for dc in range(ND):
                pbank = (pA if (dc % 2 == 0) else pB) if t % 2 == 0 else (pC if (dc % 2 == 0) else pD)
                off = (dc // 2) * 128
                sev = inc("scalar", lambda e, dc=dc, t=t, pbank=pbank, off=off: e.copy(
                    xnT[dc][:, t * 128:(t + 1) * 128], pbank[:, off:off + 128]), "s", 1)
            wait("vector", "s", sev)
            vev = None
            for dc in range(ND):
                pbank = (pA if (dc % 2 == 0) else pB) if t % 2 == 0 else (pC if (dc % 2 == 0) else pD)
                off = (dc // 2) * 128
                vev = vtick(lambda e, dc=dc, t=t, pbank=pbank, off=off: e.tensor_tensor(
                    lo_arena[:, dc * S + t * 128: dc * S + (t + 1) * 128],
                    pbank[:, off:off + 128], xnT[dc][:, t * 128:(t + 1) * 128], ALU.subtract))
            ctx.p1_evac[t] = vev
            ctx.p1_tile_done[t] = (cs2, csq)

        ctx.xnT_done_v = sems["v"].n
        ctx.xnT_done_s = sems["s"].n

        wait("tensor", "v", ctx.p1_evac[NT - 1])
        wait("tensor", "v", ctx.rw_ready)
        ctx.scores_evac = {}
        for t in range(NT):
            pbank = pA if t % 2 == 0 else pB
            if t >= 2:
                wait("tensor", "v", ctx.scores_evac[t - 2])
            smm = None
            for dc in range(ND):
                smm = inc("tensor", lambda e, t=t, dc=dc, pbank=pbank: e.matmul(
                    pbank[:, 0:8], xnT[dc][:, t * 128:(t + 1) * 128], rw_hl[:, dc * 16: dc * 16 + 8],
                    start=(dc == 0), stop=False), "t", 1)
            for dc in range(ND):
                smm = inc("tensor", lambda e, t=t, dc=dc, pbank=pbank: e.matmul(
                    pbank[:, 0:8], xnT[dc][:, t * 128:(t + 1) * 128], rw_hl[:, dc * 16 + 8: dc * 16 + 16],
                    start=False, stop=False), "t", 1)
            for dc in range(ND):
                smm = inc("tensor", lambda e, t=t, dc=dc, pbank=pbank: e.matmul(
                    pbank[:, 0:8], lo_arena[:, dc * S + t * 128: dc * S + (t + 1) * 128],
                    rw_hl[:, dc * 16: dc * 16 + 8],
                    start=False, stop=(dc == ND - 1)), "t", 1)
            wait("vector", "t", smm)
            ctx.scores_evac[t] = vtick(lambda e, t=t, pbank=pbank: e.tensor_copy(
                scores[:, t * 8:(t + 1) * 8], pbank[:, 0:8]))

        ctx.ew_evac = {}
        for t in range(NT):
            sl = scores[:, t * 8:(t + 1) * 8]
            vwait(ctx.scores_evac[t])
            c0 = vtick(lambda e, sl=sl: e.reduce_max(stat[:, 8:9], sl, axis=AX))
            vwait(c0)
            c1 = vtick(lambda e, sl=sl: e.tensor_scalar(smallA[:, 0:8], sl, stat[:, 8:9], None, ALU.is_ge))
            vwait(c1)
            c2 = vtick(lambda e, sl=sl: e.scalar_tensor_tensor(
                smallA[:, 8:16], smallA[:, 0:8], -1e30, sl, ALU.mult, ALU.add))
            vwait(c2)
            c3 = vtick(lambda e: e.reduce_max(stat[:, 9:10], smallA[:, 8:16], axis=AX))
            vwait(c3)
            c4 = vtick(lambda e: e.tensor_scalar(smallA[:, 8:16], smallA[:, 8:16], stat[:, 9:10], None, ALU.is_ge))
            c5 = vtick(lambda e: e.tensor_tensor(stat[:, 10:11], stat[:, 8:9], stat[:, 9:10], ALU.subtract))
            wait("scalar", "v", c5)
            cs = inc("scalar", lambda e: e.activation(stat[:, 11:12], stat[:, 10:11], AF.Sigmoid), "s", 1)
            wait("vector", "s", cs)
            c6 = vtick(lambda e: e.tensor_scalar(stat[:, 12:13], stat[:, 11:12], -1.0, 1.0, ALU.mult, ALU.add))
            c7 = vtick(lambda e: e.tensor_scalar(smallA[:, 16:24], smallA[:, 0:8], stat[:, 11:12], None, ALU.mult))
            vwait(c6)
            vwait(c7)
            vwait(c4)
            c8 = vtick(lambda e: e.scalar_tensor_tensor(
                smallA[:, 16:24], smallA[:, 8:16], stat[:, 12:13], smallA[:, 16:24], ALU.mult, ALU.add))
            wait("tensor", "v", c8)
            if t >= 2:
                wait("tensor", "v", ctx.ew_evac[t - 2])
            pbank = pC if t % 2 == 0 else pD
            temm = inc("tensor", lambda e, pbank=pbank: e.transpose(
                pbank[0:8, 0:128], smallA[:, 16:24], ident_sb[:, :]), "t", 1)
            wait("vector", "t", temm)
            ctx.ew_evac[t] = vtick(lambda e, t=t, pbank=pbank: e.tensor_copy(
                ewT[0:8, t * 128:(t + 1) * 128], pbank[0:8, 0:128]))
        ctx.routing_done = sems["v"].n

        if debug_taps:
            wait("gpsimd", "v", ctx.routing_done)
            cdx = None
            for dc in range(ND):
                cdx = inc("gpsimd", lambda e, dc=dc: e.dma_start(taps["xnT"][dc], xnT[dc][:, :]), "dw", 16)
            cdx = inc("gpsimd", lambda e: e.dma_start(taps["ewT"], ewT[:, :]), "dw", 16)
            ctx.tap1_done = cdx
            for eng in ENG:
                wait(eng, "dw", cdx)

    def phase2():
        NSP = 2
        FCS = 11
        NPAIR = 11
        gp_r = gate_p.rearrange("e (dc p) f -> e p dc f", p=128)
        up_r = up_p.rearrange("e (dc p) f -> e p dc f", p=128)
        dn_r = down_p.rearrange("e (fc p) d -> e p fc d", p=128)
        WGv = WARENA.ap()[:, 0:6144].rearrange("p (sl dc f) -> p sl dc f", sl=3, dc=8)
        WUv = WARENA.ap()[:, 6144:12288].rearrange("p (sl dc f) -> p sl dc f", sl=3, dc=8)
        WDa = WARENA.ap()[:, 12288:23552]

        queue = [(e, pair) for e in range(NE) for pair in range(NPAIR)]
        slot_loaded = {}
        slot_consumed = {}
        slot_of = {}
        wd_loaded = {}
        wd_consumed = {}
        pf_ptr = [0]

        def issue_next_slot():
            if pf_ptr[0] >= len(queue):
                return
            idx = pf_ptr[0]
            e, pair = queue[idx]
            if idx >= 3:
                wait("gpsimd", "t", slot_consumed[queue[idx - 3]])
            sl = idx % 3
            slot_of[(e, pair)] = sl
            f0 = pair * 256
            inc("gpsimd", lambda ee, e=e, sl=sl, f0=f0: ee.dma_start(
                WGv[:, sl], gp_r[e][:, :, f0:f0 + 256]), "dw", 16)
            slot_loaded[(e, pair)] = inc("gpsimd", lambda ee, e=e, sl=sl, f0=f0: ee.dma_start(
                WUv[:, sl], up_r[e][:, :, f0:f0 + 256]), "dw", 16)
            pf_ptr[0] = idx + 1

        def issue_wd(e, sp):
            key = (e, sp - 1) if sp >= 1 else (e - 1, 1)
            if key in wd_consumed:
                wait("gpsimd", "t", wd_consumed[key])
            wd_loaded[(e, sp)] = inc("gpsimd", lambda ee, e=e, sp=sp: ee.dma_start(
                WDa.rearrange("p (fc d) -> p fc d", fc=FCS), dn_r[e][:, sp * FCS:(sp + 1) * FCS, :]), "dw", 16)

        wait("gpsimd", "v", ctx.routing_done)
        for _ in range(3):
            issue_next_slot()
        issue_wd(0, 0)

        psum_units = [(pA, pB), (pC, pD)]
        unit_free = {0: [], 1: []}
        ctx.sg_guard = {}
        delay_q = []
        acts_ready = {}
        mac_last = None

        for e in range(moe_ne):
            wait("tensor", "v", ctx.routing_done)
            for g in unit_free[0]:
                wait("tensor", g[0], g[1])
            if e == 0:
                wait("tensor", "dx", ctx.sel8_ready)
            wr_mm = None
            for q in range(4):
                pb = (pA, pB)[q // 2]
                wr_mm = inc("tensor", lambda ee, e=e, q=q, pb=pb: ee.matmul(
                    pb[:, (q % 2) * 512:(q % 2 + 1) * 512], sel8_sb[0:NE, e * 128:(e + 1) * 128],
                    ewT[0:NE, q * 512:(q + 1) * 512], start=True, stop=True), "t", 1)
            wait("vector", "t", wr_mm)
            cwr = None
            for q in range(2):
                pb = (pA, pB)[q]
                cwr = inc("vector", lambda ee, q=q, pb=pb: ee.tensor_copy(
                    w_rep[:, q * 1024:(q + 1) * 1024], pb[:, :]), "v", 1)
            unit_free = {0: [("v", cwr)], 1: unit_free[1]}

            for sp in range(NSP):
                for fcl in range(FCS):
                    fc = sp * FCS + fcl
                    pair, j = fc // 2, fc % 2
                    for th in range(2):
                        u = (fc * 2 + th) % 2
                        gate_ps, up_ps = psum_units[u]
                        wait("tensor", "dw", slot_loaded[(e, pair)])
                        for gg in unit_free[u]:
                            wait("tensor", gg[0], gg[1])
                        gmm = umm = None
                        sl = slot_of[(e, pair)]
                        for dc in range(ND):
                            for tw in range(2):
                                gmm = inc("tensor", lambda ee, dc=dc, tw=tw, th=th, sl=sl, j=j, gate_ps=gate_ps: ee.matmul(
                                    gate_ps[:, tw * 512:(tw + 1) * 512],
                                    WGv[:, sl, dc, j * 128:(j + 1) * 128],
                                    xnT[dc][:, th * 1024 + tw * 512: th * 1024 + (tw + 1) * 512],
                                    start=(dc == 0), stop=(dc == ND - 1)), "t", 1)
                        for dc in range(ND):
                            for tw in range(2):
                                umm = inc("tensor", lambda ee, dc=dc, tw=tw, th=th, sl=sl, j=j, up_ps=up_ps: ee.matmul(
                                    up_ps[:, tw * 512:(tw + 1) * 512],
                                    WUv[:, sl, dc, j * 128:(j + 1) * 128],
                                    xnT[dc][:, th * 1024 + tw * 512: th * 1024 + (tw + 1) * 512],
                                    start=(dc == 0), stop=(dc == ND - 1)), "t", 1)
                        slot_consumed[(e, pair)] = sems["t"].n
                        b2 = (fc * 2 + th) % 2
                        wait("scalar", "t", gmm)
                        prev = ctx.sg_guard.get(b2)
                        if prev is not None:
                            wait("scalar", "v", prev)
                        csg = inc("scalar", lambda ee, b2=b2, gate_ps=gate_ps: ee.activation(
                            sg_t[:, b2 * 1024:(b2 + 1) * 1024], gate_ps[:, :], AF.Silu), "s", 1)
                        wait("vector", "t", umm)
                        wait("vector", "s", csg)
                        ct1 = inc("vector", lambda ee, b2=b2, up_ps=up_ps: ee.tensor_tensor(
                            up_t[:, b2 * 1088: b2 * 1088 + 1024], up_ps[:, :],
                            sg_t[:, b2 * 1024:(b2 + 1) * 1024], ALU.mult), "v", 1)
                        ctx.sg_guard[b2] = ct1
                        unit_free[u] = [("s", csg), ("v", ct1)]
                        def acts_op(ee, b2=b2, fcl=fcl, th=th):
                            return ee.tensor_tensor(
                                A1[:, fcl * S + th * 1024: fcl * S + (th + 1) * 1024],
                                up_t[:, b2 * 1088: b2 * 1088 + 1024],
                                w_rep[:, th * 1024:(th + 1) * 1024], ALU.mult)
                        delay_q.append((ct1, acts_op, fcl))
                        if len(delay_q) >= 2:
                            cdep, op, fcl_d = delay_q.pop(0)
                            wait("vector", "v", cdep)
                            acts_ready[fcl_d] = inc("vector", op, "v", 1)
                    if j == 1:
                        issue_next_slot()
                while delay_q:
                    cdep, op, fcl_d = delay_q.pop(0)
                    wait("vector", "v", cdep)
                    acts_ready[fcl_d] = inc("vector", op, "v", 1)

                wait("tensor", "dw", wd_loaded[(e, sp)])
                down_free = {}
                for t in range(NT):
                    dp = (pA, pB)[t % 2]
                    if t < 2:
                        for uu in range(2):
                            for gg in unit_free[uu]:
                                wait("tensor", gg[0], gg[1])
                    else:
                        wait("tensor", "v", down_free[t % 2])
                    dmm = None
                    for fcl in range(FCS):
                        if t == 0:
                            wait("tensor", "v", acts_ready[fcl])
                        for h in range(2):
                            dmm = inc("tensor", lambda ee, fcl=fcl, h=h, t=t, dp=dp: ee.matmul(
                                dp[:, h * 512:(h + 1) * 512],
                                A1[:, fcl * S + t * 128: fcl * S + (t + 1) * 128],
                                WDa[:, fcl * 1024 + h * 512: fcl * 1024 + (h + 1) * 512],
                                start=(fcl == 0), stop=(fcl == FCS - 1)), "t", 1)
                    wait("vector", "t", dmm)
                    if e == 0 and sp == 0:
                        cmac = inc("vector", lambda ee, t=t, dp=dp: ee.tensor_copy(
                            accg[:, t * 1024:(t + 1) * 1024], dp[:, :]), "v", 1)
                    else:
                        cmac = inc("vector", lambda ee, t=t, dp=dp: ee.tensor_tensor(
                            accg[:, t * 1024:(t + 1) * 1024], dp[:, :],
                            accg[:, t * 1024:(t + 1) * 1024], ALU.add), "v", 1)
                    down_free[t % 2] = cmac
                    mac_last = cmac
                wd_consumed[(e, sp)] = sems["t"].n
                unit_free = {0: [("v", mac_last)], 1: []}
                if sp == 0:
                    issue_wd(e, 1)
                elif e + 1 < NE:
                    issue_wd(e + 1, 0)
        ctx.moe_mac_done = mac_last

        cg = None
        wait("scalar", "v", ctx.moe_mac_done)
        for t in range(NT):
            cg = inc("scalar", lambda ee, t=t: ee.activation(
                accg[:, t * 1024:(t + 1) * 1024], accg[:, t * 1024:(t + 1) * 1024],
                AF.Silu if moe_silu else AF.Copy), "s", 1)
        ctx.g_ready = cg

        if debug_taps:
            wait("gpsimd", "s", cg)
            cdx = None
            for t in range(NT):
                cdx = inc("gpsimd", lambda ee, t=t: ee.dma_start(
                    taps["g"][t * 128:(t + 1) * 128, :], accg[:, t * 1024:(t + 1) * 1024]), "dw", 16)
            cdx = inc("gpsimd", lambda ee: ee.dma_start(taps["wrep"], w_rep[:, :]), "dw", 16)
            for fcl in range(FCS):
                cdx = inc("gpsimd", lambda ee, fcl=fcl: ee.dma_start(
                    taps["acts"][fcl], A1[:, fcl * S:(fcl + 1) * S]), "dw", 16)
            for eng in ENG:
                wait(eng, "dw", cdx)

    def phase3():
        H = 16
        DH = 64
        wk_ap = WARENA.ap()[:, 0:8192].rearrange("p (dc f) -> p dc f", dc=8)
        wv_ap = WARENA.ap()[:, 8192:16384].rearrange("p (dc f) -> p dc f", dc=8)
        wq_ap = WARENA.ap()[:, 0:8192].rearrange("p (dc f) -> p dc f", dc=8)
        wout_ap = WARENA.ap()[:, 8192:16384].rearrange("p (dc f) -> p dc f", dc=8)
        wqkv_r = wqkv.rearrange("(dc p) f -> p dc f", p=128)
        wout_r = wout0.rearrange("(dc p) f -> p dc f", p=128)

        moe_t_done = sems["t"].n
        moe_v_done = sems["v"].n
        STG = 16384
        KV2 = 17408
        KS2 = KV2 + 1024
        ONES1 = KS2 + 16
        wait("gpsimd", "t", moe_t_done)
        cmz = inc("gpsimd", lambda e: e.memset(A1[:, KV2:ONES1], 0.0), "g", 1)
        cmz = inc("gpsimd", lambda e: e.memset(A1[:, ONES1:ONES1 + 1], 1.0), "g", 1)
        ctx.kv2_zeroed = cmz
        ck = inc("gpsimd", lambda e: e.dma_start(wk_ap, wqkv_r[:, :, D:2 * D]), "dw", 16)
        cv_ = inc("gpsimd", lambda e: e.dma_start(wv_ap, wqkv_r[:, :, 2 * D:3 * D]), "dw", 16)
        ctx.wk_ready, ctx.wv_ready = ck, cv_

        wait("tensor", "dw", cv_)
        wait("tensor", "g", ctx.kv2_zeroed)
        wait("tensor", "v", moe_v_done)
        wait("tensor", "s", ctx.g_ready)
        kel = {}
        vcp = {}
        ksmac = {}
        for t in range(NT):
            kp = pA if t % 2 == 0 else pB
            if t >= 2:
                wait("tensor", "v", kel[t - 2][1])
            kmm = None
            for dc in range(ND):
                for tw in range(2):
                    kmm = inc("tensor", lambda e, dc=dc, tw=tw, t=t, kp=kp: e.matmul(
                        kp[:, tw * 512:(tw + 1) * 512], xnT[dc][:, t * 128:(t + 1) * 128],
                        wk_ap[:, dc, tw * 512:(tw + 1) * 512],
                        start=(dc == 0), stop=(dc == ND - 1)), "t", 1)
            vmm = {}
            for vg in range(2):
                prev = vcp.get((t * 2 + vg) - 2)
                if prev is not None:
                    wait("tensor", "s", prev)
                if vg == 1 and t >= 1:
                    wait("tensor", "v", ksmac[t - 1])
                for dc in range(ND):
                    vmm[vg] = inc("tensor", lambda e, dc=dc, vg=vg, t=t: e.matmul(
                        pC[:, vg * 512:(vg + 1) * 512], xnT[dc][:, t * 128:(t + 1) * 128],
                        wv_ap[:, dc, vg * 512:(vg + 1) * 512],
                        start=(dc == 0), stop=(dc == ND - 1)), "t", 1)
            b = t % 2
            scr = xn_t if b == 0 else xt
            wait("vector", "t", kmm)
            c1 = inc("vector", lambda e, kp=kp, scr=scr: e.tensor_scalar(
                scr[:, 0:1024], kp[:, :], 0.0, None, ALU.min), "v", 1)
            wait("scalar", "v", c1)
            c2 = inc("scalar", lambda e, scr=scr: e.activation(
                scr[:, 1024:2048], scr[:, 0:1024], AF.Exp), "s", 1)
            wait("vector", "s", c2)
            c3 = inc("vector", lambda e, kp=kp, scr=scr, b=b: e.scalar_tensor_tensor(
                sg_t[:, b * 1024:(b + 1) * 1024], kp[:, :], 0.0, scr[:, 1024:2048],
                ALU.max, ALU.add), "v", 1)
            kel[t] = (c2, c3)
            for vg in range(2):
                wait("scalar", "t", vmm[vg])
                vcp[t * 2 + vg] = inc("scalar", lambda e, vg=vg, b=b: e.copy(
                    up_t[:, b * 1088 + vg * 512: b * 1088 + (vg + 1) * 512],
                    pC[:, vg * 512:(vg + 1) * 512]), "s", 1)
            wait("tensor", "v", c3)
            wait("tensor", "s", vcp[t * 2 + 1])
            for h in range(H):
                inc("tensor", lambda e, h=h, t=t, b=b: e.matmul(
                    pD[0:64, h * 64:(h + 1) * 64],
                    sg_t[:, b * 1024 + h * 64: b * 1024 + (h + 1) * 64],
                    up_t[:, b * 1088 + h * 64: b * 1088 + h * 64 + 64],
                    start=(t == 0 and h % 8 == 0), stop=(t == NT - 1 and h % 8 == 7)), "t", 1)
            ksm = None
            for c in range(ND):
                ksm = inc("tensor", lambda e, c=c, t=t, b=b: e.matmul(
                    pC[:, 512 + c:513 + c],
                    sg_t[:, b * 1024 + c * 128: b * 1024 + (c + 1) * 128],
                    A1[:, ONES1:ONES1 + 1],
                    start=(c == 0), stop=(c == ND - 1)), "t", 1)
            wait("vector", "t", ksm)
            if t == 0:
                ksmac[t] = inc("vector", lambda e: e.tensor_copy(
                    smallA[:, 48:56], pC[:, 512:520]), "v", 1)
            else:
                ksmac[t] = inc("vector", lambda e: e.tensor_tensor(
                    smallA[:, 48:56], pC[:, 512:520], smallA[:, 48:56], ALU.add), "v", 1)
        wait("vector", "t", sems["t"].n)
        ckv = inc("vector", lambda e: e.tensor_copy(A1[0:64, STG:STG + 512], pD[0:64, 0:512]), "v", 1)
        ckv = inc("vector", lambda e: e.tensor_copy(A1[0:64, STG + 512:STG + 1024], pD[0:64, 512:1024]), "v", 1)
        for h in range(H):
            ckv = inc("vector", lambda e, h=h: e.tensor_copy(
                A1[64 * (h % 2):64 * (h % 2) + 64, KS2 + h:KS2 + h + 1],
                smallA[64 * (h % 2):64 * (h % 2) + 64, 48 + h // 2:49 + h // 2]), "v", 1)
        wait("gpsimd", "v", ckv)
        cdma = None
        for h in range(H):
            cdma = inc("gpsimd", lambda e, h=h: e.dma_start(
                A1[64 * (h % 2):64 * (h % 2) + 64,
                   KV2 + (h // 2) * 128 + 64 * (h % 2): KV2 + (h // 2) * 128 + 64 * (h % 2) + 64],
                A1[0:64, STG + h * 64: STG + (h + 1) * 64]), "dw", 16)
        ctx.kv_ready_dw = cdma
        ctx.kv_ready = ckv

        cq = inc("gpsimd", lambda e: e.dma_start(wq_ap, wqkv_r[:, :, 0:D]), "dw", 16)
        co = inc("gpsimd", lambda e: e.dma_start(wout_ap, wout_r), "dw", 16)
        ctx.wq_ready, ctx.wout_ready = cq, co
        wait("tensor", "dw", cq)
        wait("tensor", "v", ctx.kv_ready)
        qel = {}
        for qc in range(ND):
            for th2 in range(2):
                step = qc * 2 + th2
                qp = pA if step % 2 == 0 else pB
                if step >= 2:
                    wait("tensor", "v", qel[step - 2])
                qmm = None
                for dc in range(ND):
                    for tw in range(2):
                        qmm = inc("tensor", lambda e, dc=dc, tw=tw, th2=th2, qc=qc, qp=qp: e.matmul(
                            qp[:, tw * 512:(tw + 1) * 512],
                            wq_ap[:, dc, qc * 128:(qc + 1) * 128],
                            xnT[dc][:, th2 * 1024 + tw * 512: th2 * 1024 + (tw + 1) * 512],
                            start=(dc == 0), stop=(dc == ND - 1)), "t", 1)
                b = step % 2
                scr = xn_t if b == 0 else xt
                wait("vector", "t", qmm)
                c1 = inc("vector", lambda e, qp=qp, scr=scr: e.tensor_scalar(
                    scr[:, 0:1024], qp[:, :], 0.0, None, ALU.min), "v", 1)
                wait("scalar", "v", c1)
                c2 = inc("scalar", lambda e, scr=scr: e.activation(
                    scr[:, 1024:2048], scr[:, 0:1024], AF.Exp), "s", 1)
                wait("vector", "s", c2)
                c3 = inc("vector", lambda e, qp=qp, scr=scr, qc=qc, th2=th2: e.scalar_tensor_tensor(
                    A1[:, qc * S + th2 * 1024: qc * S + (th2 + 1) * 1024], qp[:, :], 0.0,
                    scr[:, 1024:2048], ALU.max, ALU.add), "v", 1)
                qel[step] = c3
        ctx.q_ready = qel[15]

        wait("tensor", "dw", co)
        wait("tensor", "dw", ctx.kv_ready_dw)
        wait("tensor", "v", ctx.q_ready)
        den_ev = {}
        in_ev = {}
        comb = {}
        for t in range(l0_tiles):
            npp = pA if t % 2 == 0 else pB
            dnp = pB if t % 2 == 0 else pA
            if t >= 2:
                wait("tensor", "s", in_ev[t - 2])
            if t >= 1:
                wait("tensor", "v", den_ev[t - 1])
            for dc in range(ND):
                inc("tensor", lambda e, dc=dc, t=t, npp=npp: e.matmul(
                    npp[:, dc * 128:(dc + 1) * 128],
                    A1[:, dc * S + t * 128: dc * S + (t + 1) * 128],
                    A1[:, KV2 + dc * 128: KV2 + (dc + 1) * 128],
                    start=True, stop=True), "t", 1)
            if l0_stage < 2:
                continue
            nmm = None
            if t >= 1:
                wait("tensor", "s", in_ev[t - 1])
            for dc in range(ND):
                nmm = inc("tensor", lambda e, dc=dc, t=t, dnp=dnp: e.matmul(
                    dnp[:, 2 * dc:2 * dc + 2],
                    A1[:, dc * S + t * 128: dc * S + (t + 1) * 128],
                    A1[:, KS2 + 2 * dc: KS2 + 2 * dc + 2],
                    start=True, stop=True), "t", 1)
            if l0_stage < 3:
                continue
            wait("vector", "t", nmm)
            if t >= 1:
                wait("vector", "s", in_ev[t - 1])
            c1 = inc("vector", lambda e, dnp=dnp: e.tensor_scalar(
                stat[:, 16:32], dnp[:, 0:16], 1e-6, None, ALU.add), "v", 1)
            wait("vector", "v", c1)
            c2 = inc("vector", lambda e: e.reciprocal(smallA[:, 32:48], stat[:, 16:32]), "v", 1)
            den_ev[t] = c2
            wait("scalar", "v", c2)
            if l0_stage < 4:
                continue
            if t >= 1:
                wait("scalar", "t", comb[t - 1][1])
            sev = None
            for h in range(H):
                sev = inc("scalar", lambda e, h=h, npp=npp: e.activation(
                    inner_t[:, 0 * D + h * 64: 0 * D + (h + 1) * 64], npp[:, h * 64:(h + 1) * 64],
                    AF.Copy, scale=smallA[:, 32 + h:33 + h]), "s", 1)
            in_ev[t] = sev
            if l0_stage < 5:
                continue
            wait("tensor", "s", sev)
            if t >= 1:
                wait("tensor", "s", comb[t - 1][2])
            trl = None
            for dc in range(ND):
                trl = inc("tensor", lambda e, dc=dc: e.transpose(
                    pC[:, dc * 128 - (dc // 4) * 512 + (dc // 4) * 512: dc * 128 + 128] if False else pC[:, dc * 128: dc * 128 + 128] if dc < 4 else pC[:, 512 + (dc - 4) * 128: 512 + (dc - 4) * 128 + 128],
                    inner_t[:, dc * 128:(dc + 1) * 128], ident_sb[:, :]), "t", 1)
            tr_done = trl
            wait("scalar", "t", trl)
            iev = None
            for dc in range(ND):
                iev = inc("scalar", lambda e, dc=dc: e.copy(
                    sg_t[:, dc * 128:(dc + 1) * 128],
                    pC[:, dc * 128: dc * 128 + 128] if dc < 4 else pC[:, 512 + (dc - 4) * 128: 512 + (dc - 4) * 128 + 128]), "s", 1)
            if l0_stage < 6:
                continue
            wait("tensor", "s", iev)
            if t >= 1:
                wait("tensor", "v", comb[t - 1][0])
            pmm = None
            for dcol in range(2):
                for ic in range(ND):
                    pmm = inc("tensor", lambda e, dcol=dcol, ic=ic: e.matmul(
                        pD[:, dcol * 512:(dcol + 1) * 512], sg_t[:, ic * 128:(ic + 1) * 128],
                        wout_ap[:, ic, dcol * 512:(dcol + 1) * 512],
                        start=(ic == 0), stop=(ic == ND - 1)), "t", 1)
            if l0_stage < 7:
                continue
            if t == 0:
                wait("sync", "v", ctx.q_ready)
            if t >= 2:
                wait("sync", "v", comb[t - 2][3])
            cx = inc("sync", lambda e, t=t: e.dma_start(
                xt[:, (t % 2) * D:(t % 2 + 1) * D], x_in[t * 128:(t + 1) * 128, :]), "dx", 16)
            wait("vector", "t", pmm)
            if t == 0:
                wait("vector", "s", ctx.g_ready)
            cc1 = inc("vector", lambda e: e.tensor_tensor(
                xn_t[:, 1024:2048], pD[:, :], bias_rep[:, 0:D], ALU.add), "v", 1)
            wait("vector", "v", cc1)
            cc2 = inc("vector", lambda e, t=t: e.tensor_tensor(
                xn_t[:, 1024:2048], xn_t[:, 1024:2048], accg[:, t * 1024:(t + 1) * 1024], ALU.mult), "v", 1)
            wait("vector", "dx", cx)
            wait("vector", "v", cc2)
            cc3 = inc("vector", lambda e, t=t: e.tensor_tensor(
                xn_t[:, 1024:2048], xn_t[:, 1024:2048], xt[:, (t % 2) * D:(t % 2 + 1) * D], ALU.add), "v", 1)
            wait("sync", "v", cc3)
            cst = inc("sync", lambda e, t=t: e.dma_start(
                x1_dram[t * 128:(t + 1) * 128, :], xn_t[:, 1024:2048]), "dx", 16)
            wait("vector", "dx", cst)
            comb[t] = (cc1, tr_done, iev, cc3)
        ctx.x1_done = sems["dx"].n

        if debug_taps:
            wait("gpsimd", "dx", ctx.x1_done)
            cdx = inc("gpsimd", lambda e: e.dma_start(taps["x1"], x1_dram), "dw", 16)
            for eng in ENG:
                wait(eng, "dw", cdx)

    def phase4():
        p3_t = sems["t"].n
        p3_v = sems["v"].n
        p3_s = sems["s"].n

        def vtick(fn):
            return inc("vector", fn, "v", 1)

        def vwait(c):
            wait("vector", "v", c)

        wait("tensor", "v", p3_v)
        wait("tensor", "s", p3_s)
        ctx.p4_xq_pend = None
        ctx.p4_xq_done = {}
        ctx.p4_evac = {}
        tile_done = {}
        for t in range(NT):
            buf = t % 2
            if t == 0:
                wait("sync", "v", p3_v)
                wait("sync", "s", p3_s)
            if t >= 2:
                wait("sync", "v", tile_done[t - 2][0])
                wait("sync", "s", tile_done[t - 2][1])
                wait("sync", "t", ctx.p4_xq_done[t - 2])
            cx = inc("sync", lambda e, t=t, buf=buf: e.dma_start(
                xt[:, buf * D:(buf + 1) * D], x1_dram[t * 128:(t + 1) * 128, :]), "dx", 16)
            wait("vector", "dx", cx)
            c0 = vtick(lambda e, buf=buf: e.reduce_sum(stat[:, 0:1], xt[:, buf * D:(buf + 1) * D], axis=AX))
            wait("scalar", "dx", cx)
            csq = inc("scalar", lambda e, buf=buf: e.activation(
                xn_t[:, buf * D:(buf + 1) * D], xt[:, buf * D:(buf + 1) * D], AF.Square), "s", 1)
            wait("vector", "s", csq)
            c1 = vtick(lambda e, buf=buf: e.reduce_sum(stat[:, 1:2], xn_t[:, buf * D:(buf + 1) * D], axis=AX))
            vwait(c0)
            c2 = vtick(lambda e: e.tensor_scalar(stat[:, 2:3], stat[:, 0:1], 1.0 / D, None, ALU.mult))
            vwait(c2)
            c3 = vtick(lambda e: e.tensor_tensor(stat[:, 3:4], stat[:, 2:3], stat[:, 2:3], ALU.mult))
            vwait(c3)
            vwait(c1)
            c4 = vtick(lambda e: e.scalar_tensor_tensor(
                stat[:, 4:5], stat[:, 1:2], 1.0 / D, stat[:, 3:4], ALU.mult, ALU.subtract))
            wait("scalar", "v", c4)
            cs1 = inc("scalar", lambda e: e.activation(stat[:, 5:6], stat[:, 4:5], AF.Sqrt, bias=cns[:, 0:1]), "s", 1)
            wait("vector", "s", cs1)
            c5 = vtick(lambda e: e.reciprocal(stat[:, 6:7], stat[:, 5:6]))
            vwait(c5)
            c6 = vtick(lambda e: e.scalar_tensor_tensor(
                stat[:, 7:8], stat[:, 2:3], -1.0, stat[:, 6:7], ALU.mult, ALU.mult))
            vwait(c6)
            cs2 = vtick(lambda e, buf=buf: e.tensor_scalar(
                xn_t[:, buf * D:(buf + 1) * D], xt[:, buf * D:(buf + 1) * D],
                stat[:, 6:7], stat[:, 7:8], ALU.mult, ALU.add))
            c7 = vtick(lambda e: e.tensor_tensor(stat[:, 8:9], stat[:, 4:5], cns[:, 0:1], ALU.add))
            vwait(c7)
            c8 = vtick(lambda e: e.reciprocal(stat[:, 9:10], stat[:, 8:9]))
            vwait(c8)
            c9 = vtick(lambda e: e.tensor_tensor(stat[:, 10:11], stat[:, 4:5], stat[:, 9:10], ALU.mult))
            vwait(c9)
            c10 = vtick(lambda e: e.tensor_tensor(stat[:, 11:12], stat[:, 10:11], cns[:, 1:2], ALU.add))
            wait("scalar", "v", c10)
            cs3 = inc("scalar", lambda e: e.activation(stat[:, 12:13], stat[:, 11:12], AF.Sqrt), "s", 1)
            wait("vector", "s", cs3)
            c11 = vtick(lambda e: e.reciprocal(stat[:, 13:14], stat[:, 12:13]))
            vwait(cs2)
            c12 = vtick(lambda e, buf=buf: e.reduce_max(
                stat[:, 14:15], xn_t[:, buf * D:(buf + 1) * D], axis=AX, apply_absolute_value=True))
            vwait(c11)
            vwait(c12)
            c13 = vtick(lambda e: e.tensor_tensor(stat[:, 15:16], stat[:, 13:14], stat[:, 14:15], ALU.mult))
            vwait(c13)
            c14 = vtick(lambda e: e.tensor_scalar(stat[:, 16:17], stat[:, 15:16], 1e-5, None, ALU.max))
            vwait(c14)
            c15 = vtick(lambda e: e.reciprocal(stat[:, 17:18], stat[:, 16:17]))
            vwait(c15)
            c16 = vtick(lambda e: e.tensor_scalar(stat[:, 18:19], stat[:, 17:18], 127.0, None, ALU.mult))
            c17 = vtick(lambda e: e.tensor_scalar(stat[:, 19:20], stat[:, 16:17], 1.0 / 127.0, None, ALU.mult))
            vwait(c16)
            c18 = vtick(lambda e: e.tensor_tensor(stat[:, 20:21], stat[:, 13:14], stat[:, 18:19], ALU.mult))
            vwait(c18)
            if t >= 2:
                wait("vector", "t", ctx.p4_xq_done[t - 2])
            q1 = vtick(lambda e, buf=buf: e.tensor_scalar(
                xt[:, buf * D:(buf + 1) * D], xn_t[:, buf * D:(buf + 1) * D],
                stat[:, 20:21], None, ALU.mult))
            vwait(q1)
            q2a = vtick(lambda e, buf=buf: e.tensor_scalar(
                strip[:, 64:1088], xt[:, buf * D:(buf + 1) * D], RND_C52, RND_C52, ALU.add, ALU.subtract))
            q2b = vtick(lambda e, buf=buf: e.tensor_scalar(
                xt[:, buf * D:(buf + 1) * D], xt[:, buf * D:(buf + 1) * D], RND_C23, RND_C23, ALU.add, ALU.subtract))
            vwait(q2a)
            vwait(q2b)
            q2c = vtick(lambda e, buf=buf: e.tensor_tensor(
                strip[:, 64:1088], xt[:, buf * D:(buf + 1) * D], strip[:, 64:1088], ALU.subtract))
            vwait(q2c)
            q2d = vtick(lambda e: e.tensor_scalar(
                strip[:, 64:1088], strip[:, 64:1088], RND_C32, RND_C32, ALU.add, ALU.subtract))
            vwait(q2d)
            q2 = vtick(lambda e, buf=buf: e.tensor_tensor(
                xt[:, buf * D:(buf + 1) * D], xt[:, buf * D:(buf + 1) * D], strip[:, 64:1088], ALU.subtract))
            vwait(q2)
            q3 = vtick(lambda e, buf=buf: e.tensor_scalar(
                xt[:, buf * D:(buf + 1) * D], xt[:, buf * D:(buf + 1) * D],
                127.0, -128.0, ALU.min, ALU.max))
            vwait(q3)
            vwait(c17)
            q4 = vtick(lambda e, buf=buf: e.tensor_scalar(
                xt[:, buf * D:(buf + 1) * D], xt[:, buf * D:(buf + 1) * D],
                stat[:, 19:20], None, ALU.mult))
            wait("tensor", "v", cs2)
            if t >= 1:
                wait("tensor", "s", ctx.p4_evac[t - 1])
            mm1 = None
            for dc in range(ND):
                pbank = pA if dc % 2 == 0 else pB
                off = (dc // 2) * 128
                mm1 = inc("tensor", lambda e, dc=dc, buf=buf, pbank=pbank, off=off: e.transpose(
                    pbank[:, off:off + 128], xn_t[:, buf * D + dc * 128: buf * D + (dc + 1) * 128],
                    ident_sb[:, :]), "t", 1)
            mm2 = None
            if ctx.p4_xq_pend is not None:
                pt, pbuf, pq4 = ctx.p4_xq_pend
                wait("tensor", "v", pq4)
                for dc in range(ND):
                    pbank = pC if dc % 2 == 0 else pD
                    off = (dc // 2) * 128
                    mm2 = inc("tensor", lambda e, dc=dc, pbuf=pbuf, pbank=pbank, off=off: e.transpose(
                        pbank[:, off:off + 128], xt[:, pbuf * D + dc * 128: pbuf * D + (dc + 1) * 128],
                        ident_sb[:, :]), "t", 1)
                ctx.p4_xq_done[pt] = mm2
            pend2 = ctx.p4_xq_pend
            ctx.p4_xq_pend = (t, buf, q4)
            wait("scalar", "t", mm2 if mm2 is not None else mm1)
            sev = None
            for dc in range(ND):
                pbank = pA if dc % 2 == 0 else pB
                off = (dc // 2) * 128
                sev = inc("scalar", lambda e, dc=dc, t=t, pbank=pbank, off=off: e.copy(
                    xnT[dc][:, t * 128:(t + 1) * 128], pbank[:, off:off + 128]), "s", 1)
            if pend2 is not None:
                pt, pbuf, pq4 = pend2
                for dc in range(ND):
                    pbank = pC if dc % 2 == 0 else pD
                    off = (dc // 2) * 128
                    sev = inc("scalar", lambda e, dc=dc, pt=pt, pbank=pbank, off=off: e.copy(
                        A1[:, dc * S + pt * 128: dc * S + (pt + 1) * 128], pbank[:, off:off + 128]), "s", 1)
            ctx.p4_evac[t] = sev
            tile_done[t] = (q4, csq, mm2 if mm2 is not None else mm1)
        pt, pbuf, pq4 = ctx.p4_xq_pend
        wait("tensor", "v", pq4)
        wait("tensor", "s", ctx.p4_evac[NT - 1])
        mm2 = None
        for dc in range(ND):
            pbank = pC if dc % 2 == 0 else pD
            off = (dc // 2) * 128
            mm2 = inc("tensor", lambda e, dc=dc, pbuf=pbuf, pbank=pbank, off=off: e.transpose(
                pbank[:, off:off + 128], xt[:, pbuf * D + dc * 128: pbuf * D + (dc + 1) * 128],
                ident_sb[:, :]), "t", 1)
        ctx.p4_xq_done[pt] = mm2
        wait("scalar", "t", mm2)
        sev = None
        for dc in range(ND):
            pbank = pC if dc % 2 == 0 else pD
            off = (dc // 2) * 128
            sev = inc("scalar", lambda e, dc=dc, pt=pt, pbank=pbank, off=off: e.copy(
                A1[:, dc * S + pt * 128: dc * S + (pt + 1) * 128], pbank[:, off:off + 128]), "s", 1)
        ctx.p4_done_s = sems["s"].n
        ctx.p4_done_v = sems["v"].n
        ctx.p4_done_t = sems["t"].n

    def phase5():
        wi_r = w_int8.rearrange("(wc p) i -> wc p i", p=128)
        wait("tensor", "s", ctx.p4_done_s)
        wait("tensor", "dx", ctx.cols_ready)
        cwm = inc("tensor", lambda e: e.matmul(
            pA[:, 0:1], ones_sb[0:1, :], cols1[0:1, 5:6], start=True, stop=True), "t", 1)
        wait("vector", "t", cwm)
        cws = inc("vector", lambda e: e.tensor_copy(smallA[:, 60:61], pA[:, 0:1]), "v", 1)
        wait("gpsimd", "v", ctx.p4_done_v)
        wait("gpsimd", "t", ctx.p4_done_t)
        wt_ev = {}
        for wc in range(ND):
            buf = wc % 2
            scr = xt if buf == 0 else xn_t
            if wc >= 2:
                wait("gpsimd", "t", wt_ev[wc - 2][0])
            cwl = inc("gpsimd", lambda e, wc=wc, scr=scr: e.dma_start(
                scr[:, 0:1024], wi_r[wc]), "dw", 16)
            wait("tensor", "dw", cwl)
            if wc == 0:
                wait("tensor", "v", cws)
            if wc >= 2:
                wait("tensor", "s", wt_ev[wc - 2][1])
            mmw = None
            for ic in range(ND):
                pbank = (pA if ic % 2 == 0 else pB) if wc % 2 == 0 else (pC if ic % 2 == 0 else pD)
                off = (ic // 2) * 128
                mmw = inc("tensor", lambda e, ic=ic, scr=scr, pbank=pbank, off=off: e.transpose(
                    pbank[:, off:off + 128], scr[:, ic * 128:(ic + 1) * 128], ident_sb[:, :]), "t", 1)
            wait("scalar", "t", mmw)
            sev = None
            for ic in range(ND):
                pbank = (pA if ic % 2 == 0 else pB) if wc % 2 == 0 else (pC if ic % 2 == 0 else pD)
                off = (ic // 2) * 128
                sev = inc("scalar", lambda e, ic=ic, wc=wc, pbank=pbank, off=off: e.copy(
                    WARENA.ap()[:, 0:8192].rearrange("p (ic o) -> p ic o", ic=8)[:, ic, wc * 128:(wc + 1) * 128],
                    pbank[:, off:off + 128]), "s", 1)
            wt_ev[wc] = (mmw, sev)
        wt_v = WARENA.ap()[:, 0:8192].rearrange("p (ic o) -> p ic o", ic=8)
        wait("tensor", "s", wt_ev[ND - 1][1])
        wait("tensor", "s", wt_ev[ND - 2][1])
        g1ev = {}
        for t in range(NT):
            gp = (pA, pB, pC, pD)[t % 4]
            if t >= 4:
                wait("tensor", "s", g1ev[t - 4])
            gmm = None
            for dc in range(ND):
                for oc in range(2):
                    gmm = inc("tensor", lambda e, dc=dc, oc=oc, t=t, gp=gp: e.matmul(
                        gp[:, oc * 512:(oc + 1) * 512],
                        A1[:, dc * S + t * 128: dc * S + (t + 1) * 128],
                        wt_v[:, dc, oc * 512:(oc + 1) * 512],
                        start=(dc == 0), stop=(dc == ND - 1)), "t", 1)
            wait("vector", "t", gmm)
            if t == 0:
                wait("vector", "v", cws)
            if t >= 2:
                wait("vector", "s", g1ev[t - 2])
            buf = t % 2
            scr = xt if buf == 0 else xn_t
            cv1 = inc("vector", lambda e, gp=gp, scr=scr: e.scalar_tensor_tensor(
                scr[:, 1024:2048], gp[:, :], smallA[:, 60:61], bias_rep[:, 2 * D:3 * D],
                ALU.mult, ALU.add), "v", 1)
            wait("scalar", "v", cv1)
            g1ev[t] = inc("scalar", lambda e, t=t, scr=scr: e.activation(
                accg[:, t * 1024:(t + 1) * 1024], scr[:, 1024:2048], AF.Silu), "s", 1)
        ctx.g1_ready = g1ev[NT - 1]

        if debug_taps:
            wait("gpsimd", "s", ctx.g1_ready)
            cdx = None
            for t in range(NT):
                cdx = inc("gpsimd", lambda ee, t=t: ee.dma_start(
                    taps["g1"][t * 128:(t + 1) * 128, :], accg[:, t * 1024:(t + 1) * 1024]), "dw", 16)
            for eng in ENG:
                wait(eng, "dw", cdx)

    def phase6():
        H2 = 2048
        NIB = 8
        v_tm = WARENA.ap()[:, 0:16384].rearrange("p (t d) -> p t d", t=NT)
        wqk_v = WARENA.ap()[:, 16384:17408].rearrange("p (dc f) -> p dc f", dc=8)
        at_sb = WARENA.ap()[:, 17408:17920]
        wv_v = A1.ap()[:, 0:8192].rearrange("p (dc f) -> p dc f", dc=8)
        QF, KF = 16384, 18432
        qf_ap = A1.ap()[:, QF:QF + 2048]
        kf_ap = A1.ap()[:, KF:KF + 2048]

        p5_t = sems["t"].n
        wait("vector", "dx", ctx.cols_ready)
        ct1_ = inc("vector", lambda e: e.tensor_scalar(
            tblp[0:NB, 0:1], tblp[0:NB, 0:1], REL_SCALE / SIGMA, -MU / SIGMA, ALU.mult, ALU.add), "v", 1)
        cms = inc("gpsimd", lambda e: e.memset(tblp[NB:NB + 1, 0:1], MASK_NEG), "g", 1)
        wait("tensor", "s", ctx.p4_done_s)
        bv_ev = {}
        for hf in range(2):
            wait("sync", "s", sems["s"].n)
            wait("sync", "v", sems["v"].n)
            chl = inc("sync", lambda e, hf=hf: e.dma_start(
                xt[0:NB + 1, 0:1280], oht[:, hf * 1280:(hf + 1) * 1280]), "dx", 16)
            wait("tensor", "dx", chl)
            wait("tensor", "v", ct1_)
            wait("tensor", "g", cms)
            wait("tensor", "s", sems["s"].n)
            wait("tensor", "v", sems["v"].n)
            for mc in range(10):
                c = hf * 10 + mc
                bank = c % 2
                prev = bv_ev.get(bank)
                if prev is not None:
                    wait("tensor", "v", prev)
                bmm = inc("tensor", lambda e, mc=mc, bank=bank: e.matmul(
                    pC[0:1, bank * 512: bank * 512 + 128],
                    tblp[0:NB + 1, 0:1], xt[0:NB + 1, mc * 128:(mc + 1) * 128],
                    start=True, stop=True), "t", 1)
                wait("vector", "t", bmm)
                if c < 16:
                    bv_ev[bank] = inc("vector", lambda e, c=c, bank=bank: e.tensor_copy(
                        xn_t[0:1, c * 128:(c + 1) * 128],
                        pC[0:1, bank * 512: bank * 512 + 128]), "v", 1)
                else:
                    bv_ev[bank] = inc("vector", lambda e, c=c, bank=bank: e.tensor_copy(
                        strip[0:1, (c - 16) * 128:(c - 15) * 128],
                        pC[0:1, bank * 512: bank * 512 + 128]), "v", 1)
        wait("sync", "v", sems["v"].n)
        cbd = inc("sync", lambda e: e.dma_start(bvec_dram[0:2048], xn_t[0:1, 0:2048]), "dx", 16)
        cbd = inc("sync", lambda e: e.dma_start(bvec_dram[2048:2560], strip[0:1, 0:512]), "dx", 16)
        wait("sync", "dx", cbd)
        cst = None
        for p in range(128):
            cst = inc("sync", lambda e, p=p: e.dma_start(
                strip[p:p + 1, :], bvec_dram[384 - p:384 - p + STRIP_W]), "dx", 16)
        ctx.strip_ready = cst
        if debug_taps:
            wait("gpsimd", "dx", cst)
            inc("gpsimd", lambda e: e.dma_start(taps["strip"], strip[:, :]), "dw", 16)

        wqk_r = wqk1.rearrange("(dc p) f -> p dc f", p=128)
        wv_r = wv1.rearrange("(dc p) f -> p dc f", p=128)
        wait("gpsimd", "t", p5_t)
        cqk = inc("gpsimd", lambda e: e.dma_start(wqk_v, wqk_r), "dw", 16)
        cwv = inc("gpsimd", lambda e: e.dma_start(wv_v, wv_r), "dw", 16)

        wait("tensor", "dw", cqk)
        wait("tensor", "v", sems["v"].n)
        wait("tensor", "s", sems["s"].n)
        qkev = {}
        for tb in range(4):
            qp = (pA, pB, pC, pD)[tb]
            qmm = None
            for dc in range(ND):
                qmm = inc("tensor", lambda e, dc=dc, tb=tb, qp=qp: e.matmul(
                    qp[:, 0:512], wqk_v[:, dc, :], xnT[dc][:, tb * 512:(tb + 1) * 512],
                    start=(dc == 0), stop=(dc == ND - 1)), "t", 1)
            wait("scalar", "t", qmm)
            wait("scalar", "dx", ctx.cols_ready)
            if tb >= 1:
                wait("scalar", "v", qkev[tb - 1][1])
            csl = inc("scalar", lambda e, tb=tb, qp=qp: e.activation(
                sg_t[:, 1024:1536], qp[:, 0:512], AF.Silu, bias=cols1[:, 0:1]), "s", 1)
            wait("vector", "s", csl)
            cq1 = inc("vector", lambda e, tb=tb: e.tensor_scalar(
                qf_ap[:, tb * 512:(tb + 1) * 512], sg_t[:, 1024:1536],
                cols1[:, 1:2], cols1[:, 3:4], ALU.mult, ALU.add), "v", 1)
            cq2 = inc("vector", lambda e, tb=tb: e.tensor_scalar(
                kf_ap[:, tb * 512:(tb + 1) * 512], sg_t[:, 1024:1536],
                cols1[:, 2:3], cols1[:, 4:5], ALU.mult, ALU.add), "v", 1)
            qkev[tb] = (csl, cq2)
        ctx.qk_ready = qkev[3][1]

        wait("tensor", "dw", cwv)
        wait("tensor", "v", ctx.qk_ready)
        vev = {}
        for t in range(NT):
            vp = pA if t % 2 == 0 else pB
            if t >= 2:
                wait("tensor", "s", vev[t - 2])
            vmm = None
            for dc in range(ND):
                for oc in range(2):
                    vmm = inc("tensor", lambda e, dc=dc, oc=oc, t=t, vp=vp: e.matmul(
                        vp[:, oc * 512:(oc + 1) * 512],
                        A1[:, dc * S + t * 128: dc * S + (t + 1) * 128] if False else
                        xnT[dc][:, t * 128:(t + 1) * 128],
                        wv_v[:, dc, oc * 512:(oc + 1) * 512],
                        start=(dc == 0), stop=(dc == ND - 1)), "t", 1)
            buf = t % 2
            scr = xt if buf == 0 else xn_t
            wait("vector", "t", vmm)
            if t >= 2:
                wait("vector", "s", vev[t - 2])
            cv1 = inc("vector", lambda e, vp=vp, scr=scr: e.tensor_tensor(
                scr[:, 0:1024], vp[:, :], bias_rep[:, D:2 * D], ALU.add), "v", 1)
            wait("scalar", "v", cv1)
            vev[t] = inc("scalar", lambda e, t=t, scr=scr: e.activation(
                v_tm[:, t, :], scr[:, 0:1024], AF.Silu), "s", 1)
        ctx.v_ready = vev[NT - 1]

        C_MUL = 1.0 / (S * SIGMA)
        wait("tensor", "s", ctx.v_ready)
        wait("tensor", "dx", ctx.strip_ready)
        wait("vector", "dx", ctx.strip_ready)
        at_free = {}
        comb2 = {}
        ctx.p6_prev_gelu = None
        ctx.p6_prev_store = None
        ctx.p6_yv = {}
        for ib in range(p6_ibs):
            i0 = ib * 256
            njc = 2 * ib + 2
            pending = None
            for jc in range(njc):
                abuf = jc % 2
                ap_ = pA if abuf == 0 else pB
                prevy = ctx.p6_yv.get(abuf)
                if prevy is not None:
                    wait("tensor", "v", prevy)
                amm = inc("tensor", lambda e, jc=jc, i0=i0, ap_=ap_: e.matmul(
                    ap_[:, 0:256], kf_ap[:, jc * 128:(jc + 1) * 128], qf_ap[:, i0:i0 + 256],
                    start=True, stop=True), "t", 1)
                Dg = i0 - jc * 128
                so = Dg + 128
                wait("vector", "t", amm)
                prevg = ctx.p6_prev_gelu
                if prevg is not None:
                    wait("vector", "s", prevg)
                y1 = inc("vector", lambda e, ap_=ap_, so=so: e.scalar_tensor_tensor(
                    xt[:, 0:256], ap_[:, 0:256], C_MUL, strip[:, so:so + 256],
                    ALU.mult, ALU.add), "v", 1)
                ctx.p6_yv[abuf] = y1
                wait("scalar", "v", y1)
                g1_ = inc("scalar", lambda e: e.activation(
                    xn_t[:, 0:256], xt[:, 0:256], AF.Gelu), "s", 1)
                ctx.p6_prev_gelu = g1_
                m1_ = inc("vector", lambda e: e.tensor_scalar(
                    xt[:, 256:512], xt[:, 0:256], 0.0, None, ALU.is_equal), "v", 1)
                wait("vector", "s", g1_)
                wait("vector", "v", m1_)
                n1 = inc("vector", lambda e: e.scalar_tensor_tensor(
                    xn_t[:, 256:512], xt[:, 256:512], 0.5, xn_t[:, 0:256], ALU.mult, ALU.add), "v", 1)
                d1 = inc("vector", lambda e: e.tensor_tensor(
                    xt[:, 512:768], xt[:, 0:256], xt[:, 256:512], ALU.add), "v", 1)
                wait("vector", "v", d1)
                r1 = inc("vector", lambda e: e.reciprocal(xt[:, 768:1024], xt[:, 512:768]), "v", 1)
                wait("vector", "v", r1)
                wait("vector", "v", n1)
                if at_free.get(abuf) is not None:
                    wait("vector", "t", at_free[abuf])
                ca = inc("vector", lambda e, abuf=abuf: e.tensor_tensor(
                    at_sb[:, abuf * 256:(abuf + 1) * 256], xn_t[:, 256:512], xt[:, 768:1024],
                    ALU.mult), "v", 1)
                if pending is not None:
                    pjc, pca, pabuf = pending
                    wait("tensor", "v", pca)
                    if pjc < 2 and ib > 0:
                        wait("tensor", "v", comb2[ib - 1])
                    last_amm = None
                    for isub in range(2):
                        op_ = pC if isub == 0 else pD
                        for dh in range(2):
                            last_amm = inc("tensor", lambda e, isub=isub, dh=dh, pjc=pjc, pabuf=pabuf, op_=op_, njc=njc: e.matmul(
                                op_[:, dh * 512:(dh + 1) * 512],
                                at_sb[:, pabuf * 256 + isub * 128: pabuf * 256 + isub * 128 + 128],
                                v_tm[:, pjc, dh * 512:(dh + 1) * 512],
                                start=(pjc == 0), stop=(pjc == njc - 1)), "t", 1)
                    at_free[pabuf] = last_amm
                pending = (jc, ca, abuf)
            pjc, pca, pabuf = pending
            wait("tensor", "v", pca)
            if pjc < 2 and ib > 0:
                wait("tensor", "v", comb2[ib - 1])
            last_amm = None
            for isub in range(2):
                op_ = pC if isub == 0 else pD
                for dh in range(2):
                    last_amm = inc("tensor", lambda e, isub=isub, dh=dh, pjc=pjc, pabuf=pabuf, op_=op_, njc=njc: e.matmul(
                        op_[:, dh * 512:(dh + 1) * 512],
                        at_sb[:, pabuf * 256 + isub * 128: pabuf * 256 + isub * 128 + 128],
                        v_tm[:, pjc, dh * 512:(dh + 1) * 512],
                        start=(pjc == 0), stop=(pjc == njc - 1)), "t", 1)
            at_free[pabuf] = last_amm
            cc_l = None
            for isub in range(2):
                t = 2 * ib + isub
                op_ = pC if isub == 0 else pD
                if ctx.p6_prev_store is not None:
                    wait("sync", "dx", ctx.p6_prev_store)
                cx = inc("sync", lambda e, t=t: e.dma_start(
                    xt[:, 1024:2048], x1_dram[t * 128:(t + 1) * 128, :]), "dx", 16)
                wait("vector", "t", at_free[0])
                wait("vector", "t", at_free[1])
                if ib == 0 and isub == 0:
                    wait("vector", "s", ctx.g1_ready)
                c1_ = inc("vector", lambda e, t=t, op_=op_: e.tensor_tensor(
                    sg_t[:, 0:1024], op_[:, :], accg[:, t * 1024:(t + 1) * 1024], ALU.mult), "v", 1)
                wait("vector", "dx", cx)
                wait("vector", "v", c1_)
                c2_ = inc("vector", lambda e: e.tensor_tensor(
                    xt[:, 1024:2048], xt[:, 1024:2048], sg_t[:, 0:1024], ALU.add), "v", 1)
                wait("sync", "v", c2_)
                cso = inc("sync", lambda e, t=t: e.dma_start(
                    out[t * 128:(t + 1) * 128, :], xt[:, 1024:2048]), "dx", 16)
                ctx.p6_prev_store = cso
                cc_l = c2_
            comb2[ib] = cc_l
        ctx.out_done = sems["dx"].n
        wait("sync", "dx", ctx.out_done)

    phase_list = [phase0, phase1, phase2, phase3, phase4, phase5, phase6]

    with (
        nc.semaphore("sem_dw") as h_dw,
        nc.semaphore("sem_dx") as h_dx,
        nc.semaphore("sem_t") as h_t,
        nc.semaphore("sem_v") as h_v,
        nc.semaphore("sem_s") as h_s,
        nc.semaphore("sem_g") as h_g,
    ):
        sem_handles.update({"dw": h_dw, "dx": h_dx, "t": h_t, "v": h_v, "s": h_s, "g": h_g})
        for ph in phase_list:
            ph()



        with nc.Block() as block:
            @block.sync
            def _(e):
                for f in prog["sync"]:
                    f(e)

            @block.gpsimd
            def _(e):
                for f in prog["gpsimd"]:
                    f(e)

            @block.vector
            def _(e):
                for f in prog["vector"]:
                    f(e)

            @block.scalar
            def _(e):
                for f in prog["scalar"]:
                    f(e)

            @block.tensor
            def _(e):
                for f in prog["tensor"]:
                    f(e)

    return nc


def _aux_inputs():
    ident = np.eye(128, dtype=np.float32)
    buckets = _rel_bucket_ext()
    oht = np.zeros((NB + 1, BVEC_N), dtype=np.float32)
    oht[buckets, np.arange(BVEC_N)] = 1.0
    ones = np.ones((1, 128), dtype=np.float32)
    sel = np.zeros((NE, NE * 128), dtype=ml_dtypes.bfloat16)
    for e in range(NE):
        sel[e, e * 128:(e + 1) * 128] = 1.0
    return {"aux_ident": ident, "aux_oht": oht, "aux_ones": ones, "aux_sel8": sel}


def kernel(**inputs):
    nc = build_program(debug_taps=False)
    aux = _aux_inputs()
    x_full = np.ascontiguousarray(inputs["L0_attn_x"], dtype=np.float32)
    base = {k: np.ascontiguousarray(v) for k, v in inputs.items() if k != "L0_attn_x"}
    base["L1_ffn_weight_scale"] = np.asarray(inputs["L1_ffn_weight_scale"], dtype=np.float32).reshape(1, 1)
    in_maps = []
    for c in range(8):
        m = dict(base)
        m.update(aux)
        m["L0_attn_x"] = x_full[c]
        in_maps.append(m)
    res = run_bass_kernel_spmd(nc, in_maps, core_ids=list(range(8)))
    return np.stack([res.results[c]["out"] for c in range(8)], axis=0)
